# version 09
# baseline (speedup 1.0000x reference)
import contextlib
import numpy as np
import ml_dtypes
import concourse.bass as bass
import concourse.mybir as mybir
from concourse.engine_type import EngineType
from concourse.bass_utils import run_bass_kernel_spmd

F32 = mybir.dt.float32
BF16 = mybir.dt.bfloat16
I32 = mybir.dt.int32
AF = mybir.ActivationFunctionType
OP = mybir.AluOpType
AX = mybir.AxisListType
BF = ml_dtypes.bfloat16
ts = bass.ts

D = 4096
IF = 11008
NI = IF // 128
T = 1024
NT = T // 128
NHC = D // 128
EPS = 1e-5
CHUNK = 2048
GSZ = 4
QK_SCALE = 1.0 / np.sqrt(128.0)
NEG = -1e30

ENGINES = ("sync", "scalar", "vector", "tensor", "gpsimd")


class KB:

    def __init__(self, nc, stack):
        self.nc = nc
        self.stack = stack
        self.ops = {e: [] for e in ENGINES}
        self.semh = {}
        self.semc = {}
        self.last_inc = {}

    def sem(self, name):
        if name not in self.semh:
            self.semh[name] = self.stack.enter_context(self.nc.semaphore(name))
            self.semc[name] = 0
        return self.semh[name]

    def inc(self, name, k=1):
        self.sem(name)
        self.semc[name] += k
        return self.semc[name]

    def val(self, name):
        return self.semc.get(name, 0)

    def emit(self, eng, method, *args, waits=(), inc=None, inck=1, **kw):
        rv = None
        ih = None
        waits = list(waits)
        is_dma = method == "dma_start"
        if eng in ("vector", "scalar") and not is_dma and eng in self.last_inc:
            waits.append(self.last_inc[eng])
        if inc is not None:
            rv = self.inc(inc, inck)
            ih = self.semh[inc]
            if eng in ("vector", "scalar") and not is_dma:
                self.last_inc[eng] = (inc, rv)
        seen = set()
        wl = []
        for (n, v) in waits:
            if v > 0 and (n, v) not in seen:
                seen.add((n, v))
                wl.append((self.sem(n), v))

        def f(E, method=method, args=args, kw=kw, wl=wl, ih=ih, inck=inck):
            for (h, v) in wl:
                E.wait_ge(h, v)
            r = getattr(E, method)(*args, **kw)
            if ih is not None:
                r.then_inc(ih, inck)

        self.ops[eng].append(f)
        return rv

    def wait(self, eng, *waits):
        wl = [(self.sem(n), v) for (n, v) in waits if v > 0]

        def f(E, wl=wl):
            for (h, v) in wl:
                E.wait_ge(h, v)

        self.ops[eng].append(f)

    def play(self, block):
        for e in ENGINES:
            dec = getattr(block, e)
            ops = self.ops[e]

            def body(E, ops=ops):
                for f in ops:
                    f(E)

            dec(body)
        self.ops = {e: [] for e in ENGINES}


def build_nc(stop_after=None):
    NI = IF // 128
    NT = T // 128
    NHC = D // 128
    TH = T // 2
    FCW = min(512, D, T)
    FC_N = D // FCW
    GCH = [(c0, min(CHUNK, D - c0)) for c0 in range(0, D, CHUNK)]
    DCH = [(c0, min(CHUNK, IF - c0)) for c0 in range(0, IF, CHUNK)]
    nc = bass.Bass("TRN2", target_bir_lowering=False)

    x0 = nc.dram_tensor("x0", [T, D], F32, kind="ExternalInput").ap()
    wqkv = nc.dram_tensor("wqkv", [3 * D, D], F32, kind="ExternalInput").ap()
    wout = nc.dram_tensor("wout", [D, D], F32, kind="ExternalInput").ap()
    qwg = nc.dram_tensor("qwg", [IF, D], I32, kind="ExternalInput").ap()
    qwu = nc.dram_tensor("qwu", [IF, D], I32, kind="ExternalInput").ap()
    qwd = nc.dram_tensor("qwd", [D, IF], I32, kind="ExternalInput").ap()
    szg = nc.dram_tensor("szg", [IF, 64], F32, kind="ExternalInput").ap()
    szu = nc.dram_tensor("szu", [IF, 64], F32, kind="ExternalInput").ap()
    szd = nc.dram_tensor("szd", [D, 172], F32, kind="ExternalInput").ap()
    qlnw = nc.dram_tensor("qlnw", [128, NHC], F32, kind="ExternalInput").ap()
    qlnb = nc.dram_tensor("qlnb", [128, NHC], F32, kind="ExternalInput").ap()
    klnw = nc.dram_tensor("klnw", [128, NHC], F32, kind="ExternalInput").ap()
    klnb = nc.dram_tensor("klnb", [128, NHC], F32, kind="ExternalInput").ap()
    ident = nc.dram_tensor("ident", [128, 128], BF16, kind="ExternalInput").ap()
    masksI = nc.dram_tensor("masks", [4, 128, 512], F32, kind="ExternalInput").ap()
    alibiI = nc.dram_tensor("alibi", [NHC // 4, 128, 4 * T], F32,
                            kind="ExternalInput").ap()

    xoutD = nc.dram_tensor("xout", [T, D], F32, kind="ExternalOutput").ap()

    wg_c = nc.dram_tensor("wg_c", [IF, D], BF16, kind="Internal").ap()
    wu_c = nc.dram_tensor("wu_c", [IF, D], BF16, kind="Internal").ap()
    wd_c = nc.dram_tensor("wd_c", [D, IF], BF16, kind="Internal").ap()
    wqkv_c = nc.dram_tensor("wqkv_c", [3 * D, D], BF16, kind="Internal").ap()
    wout_c = nc.dram_tensor("wout_c", [D, D], BF16, kind="Internal").ap()
    x1b = nc.dram_tensor("x1b", [T, D], F32, kind="Internal").ap()
    x2b = nc.dram_tensor("x2b", [T, D], F32, kind="Internal").ap()
    xlnb = nc.dram_tensor("xlnb", [T, D], BF16, kind="Internal").ap()
    hbuf = nc.dram_tensor("hbuf", [NI, 128, T], BF16, kind="Internal").ap()
    segb = nc.dram_tensor("segb", [T, D], BF16, kind="Internal").ap()
    qt_in = nc.dram_tensor("qt_in", [4, NHC // 4, 128, T], BF16, kind="Internal")
    kt_in = nc.dram_tensor("kt_in", [4, NHC // 4, 128, T], BF16, kind="Internal")
    vt_in = nc.dram_tensor("vt_in", [4, NHC // 4, T, 128], BF16, kind="Internal")
    qt_out = nc.dram_tensor("qt_out", [16, NHC // 4, 128, T], BF16, kind="Internal")
    kt_out = nc.dram_tensor("kt_out", [16, NHC // 4, 128, T], BF16, kind="Internal")
    vt_out = nc.dram_tensor("vt_out", [16, NHC // 4, T, 128], BF16, kind="Internal")
    ao_in = nc.dram_tensor("ao_in", [4, NHC // 4, 128, T], BF16, kind="Internal")
    ao_out = nc.dram_tensor("ao_out", [16, NHC // 4, 128, T], BF16, kind="Internal")
    qt_cmp = nc.dram_tensor("qt_cmp", [4, NHC // 4, 128, T], BF16, kind="Internal").ap()
    kt_cmp = nc.dram_tensor("kt_cmp", [4, NHC // 4, 128, T], BF16, kind="Internal").ap()
    vt_cmp = nc.dram_tensor("vt_cmp", [4, NHC // 4, T, 128], BF16, kind="Internal").ap()
    ao_cmp = nc.dram_tensor("ao_cmp", [4, NHC // 4, 128, T], BF16, kind="Internal").ap()

    RG = [[0, 1, 2, 3], [4, 5, 6, 7]]
    pid = nc.partition_id([EngineType.Pool, EngineType.SP])
    rkv = pid % 4
    RIDX = [rkv + 4 * r for r in range(4)]

    with contextlib.ExitStack() as stack:
        en = stack.enter_context
        xlnT = en(nc.sbuf_tensor("xlnT", [128, NHC, T], BF16))
        xtile = en(nc.sbuf_tensor("xtile", [128, D], F32))
        xnrm = [en(nc.sbuf_tensor(f"xnrm{i}", [128, D], BF16)) for i in range(2)]
        xres = [en(nc.sbuf_tensor(f"xres{i}", [128, 512], F32)) for i in range(2)]
        ytmp = [en(nc.sbuf_tensor(f"ytmp{i}", [128, 512], F32)) for i in range(2)]
        xo = [en(nc.sbuf_tensor(f"xo{i}", [128, 512], F32)) for i in range(2)]
        stats = en(nc.sbuf_tensor("stats", [128, 224], F32))
        PSB = [en(nc.psum_tensor(f"psb{i}", [128, 1024], F32)) for i in range(4)]
        kb = KB(nc, stack)

        def barrier(*finals):
            kb.wait("gpsimd", *finals)
            phv = kb.emit("gpsimd", "memset", stats[0:1, 220:221], 0.0, inc="ph")
            for e in ("sync", "scalar", "vector", "tensor"):
                kb.wait(e, ("ph", phv))
            return phv

        def ln_finalize(tt, s_col, q_col):
            base = 128 + 4 * tt
            mu = stats[:, base:base + 1]
            mu2 = stats[:, base + 1:base + 2]
            var = stats[:, base + 2:base + 3]
            sd = stats[:, base + 3:base + 4]
            sc = stats[:, 168 + tt:168 + tt + 1]
            bi = stats[:, 176 + tt:176 + tt + 1]
            kb.emit("vector", "tensor_scalar", out=mu, in0=s_col, scalar1=1.0 / D,
                    scalar2=None, op0=OP.mult, op1=OP.bypass, inc="ve")
            kb.emit("vector", "tensor_tensor", out=mu2, in0=mu, in1=mu, op=OP.mult,
                    inc="ve")
            kb.emit("vector", "tensor_scalar", out=var, in0=q_col, scalar1=1.0 / D,
                    scalar2=None, op0=OP.mult, op1=OP.bypass, inc="ve")
            kb.emit("vector", "tensor_tensor", out=var, in0=var, in1=mu2,
                    op=OP.subtract, inc="ve")
            ve_done = kb.emit("vector", "tensor_scalar", out=var, in0=var,
                              scalar1=EPS, scalar2=None, op0=OP.add,
                              op1=OP.bypass, inc="ve")
            sq_done = kb.emit("scalar", "activation", out=sd, in_=var, func=AF.Sqrt,
                              waits=(("ve", ve_done),), inc="se")
            kb.emit("vector", "reciprocal", sc, sd, waits=(("se", sq_done),),
                    inc="ve")
            kb.emit("vector", "tensor_tensor", out=bi, in0=mu, in1=sc, op=OP.mult,
                    inc="ve")
            bi_done = kb.emit("vector", "tensor_scalar", out=bi, in0=bi,
                              scalar1=-1.0, scalar2=None, op0=OP.mult,
                              op1=OP.bypass, inc="ve")
            return bi_done, sc, bi

        def emit_ln_to_xlnT(src, stats_mode, xlnT_guard):
            for tt in range(NT):
                b = tt % 2
                ld = kb.emit("gpsimd", "dma_start", out=xtile[:, :],
                             in_=src[ts(tt, 128), :],
                             waits=(("se_ln", kb.val("se_ln")),),
                             inc="xl", inck=16)
                xw_guard = (f"xw{b}", kb.val(f"xw{b}"))
                if stats_mode == "self":
                    kb.emit("vector", "reduce_sum", stats[:, tt:tt + 1],
                            xtile[:, :], axis=AX.X, waits=(("xl", ld),),
                            inc="ve")
                    qm = kb.emit("scalar", "activation", out=xnrm[b][:, :],
                                 in_=xtile[:, :], func=AF.Square,
                                 accum_out=stats[:, 64 + tt:64 + tt + 1],
                                 waits=(("xl", ld), xw_guard),
                                 inc="se")
                    kb.wait("vector", ("se", qm))
                    rdy, sc_col, bi_col = ln_finalize(tt, stats[:, tt:tt + 1],
                                                      stats[:, 64 + tt:64 + tt + 1])
                else:
                    sm = stats[:, 184:185]
                    qm_ = stats[:, 185:186]
                    kb.emit("vector", "reduce_sum", sm,
                            stats[:, tt * 8:tt * 8 + FC_N], axis=AX.X, inc="ve")
                    kb.emit("vector", "reduce_sum", qm_,
                            stats[:, 64 + tt * 8:64 + tt * 8 + FC_N], axis=AX.X,
                            inc="ve")
                    rdy, sc_col, bi_col = ln_finalize(tt, sm, qm_)
                nd = kb.emit("scalar", "activation", out=xnrm[b][:, :],
                             in_=xtile[:, :], func=AF.Identity,
                             bias=bi_col, scale=sc_col,
                             waits=(("ve", rdy), ("xl", ld), xw_guard),
                             inc="se_ln")
                kb.emit("gpsimd", "dma_start", out=xlnb[ts(tt, 128), :],
                        in_=xnrm[b][:, :], waits=(("se_ln", nd),),
                        inc=f"xw{b}", inck=16)
            xw_fin = (("xw0", kb.val("xw0")), ("xw1", kb.val("xw1")))
            for half in range(2):
                kb.emit("sync", "dma_start", out=xlnT[:, :, ts(half, T // 2)],
                        in_=xlnb[ts(half, T // 2), :], transpose=True,
                        waits=xw_fin + (xlnT_guard,),
                        inc="xt", inck=16)
            return kb.val("xt")

        def emit_ffn_phase(call, x_res_src, x_dst, collect_stats, src_for_ln,
                           ln_mode, final_store=False):
            with contextlib.ExitStack() as ps:
                pe = ps.enter_context
                NBUF = 5
                qnat = [pe(nc.sbuf_tensor(f"qnat{call}_{i}", [128, min(CHUNK, max(D, IF))], BF16))
                        for i in range(NBUF)]
                wnat = [pe(nc.sbuf_tensor(f"wnat{call}_{i}", [128, min(CHUNK, max(D, IF))], BF16))
                        for i in range(NBUF)]
                szb = [pe(nc.sbuf_tensor(f"szb{call}_{i}", [128, 64], F32))
                       for i in range(NBUF)]
                wgT = [pe(nc.sbuf_tensor(f"wgT{call}_{i}", [128, NHC, 128], BF16))
                       for i in range(2)]
                wuT = [pe(nc.sbuf_tensor(f"wuT{call}_{i}", [128, NHC, 128], BF16))
                       for i in range(2)]
                hsil = pe(nc.sbuf_tensor(f"hsil{call}", [128, T], F32))
                hbf = [pe(nc.sbuf_tensor(f"hbf{call}_{i}", [128, T], BF16))
                       for i in range(2)]
                wdT = [pe(nc.sbuf_tensor(f"wdT{call}_{i}", [128, GSZ, FCW], BF16))
                       for i in range(2)]
                hld = [pe(nc.sbuf_tensor(f"hld{call}_{i}", [128, T], BF16))
                       for i in range(3)]
                block = pe(nc.Block())

                xt_ready = emit_ln_to_xlnT(src_for_ln, ln_mode,
                                           ("pe", kb.val("pe")))

                hw_base = kb.val("hw")

                p_state = {"cnt": 0}
                wc_gu = {}
                wcd_done = {"v": 0}

                def produce_tile(qsrc, szsrc, cache, crow, ccol0, cw, wcfam):
                    p = p_state["cnt"]
                    p_state["cnt"] += 1
                    b = p % NBUF
                    dq_guard = ("dq", kb.val("dq") - (NBUF - 1))
                    qn = kb.emit("gpsimd", "dma_start", out=qnat[b][:, 0:cw],
                                 in_=qsrc[ts(crow, 128), ccol0:ccol0 + cw],
                                 waits=(dq_guard,), inc=f"qn{b}", inck=16)
                    ng = cw // 128
                    g0 = ccol0 // 128
                    sz = kb.emit("gpsimd", "dma_start", out=szb[b][:, 0:2 * ng],
                                 in_=szsrc[ts(crow, 128), 2 * g0:2 * (g0 + ng)],
                                 waits=(dq_guard,), inc=f"sz{b}", inck=16)
                    wc_guard = (f"wc{b}", kb.val(f"wc{b}"))
                    for g in range(ng):
                        w = ((f"qn{b}", qn), (f"sz{b}", sz), wc_guard) if g == 0 else ()
                        kb.emit("vector", "tensor_scalar",
                                out=wnat[b][:, ts(g, 128)],
                                in0=qnat[b][:, ts(g, 128)],
                                scalar1=szb[b][:, 2 * g + 1:2 * g + 2],
                                scalar2=szb[b][:, 2 * g:2 * g + 1],
                                op0=OP.subtract, op1=OP.mult,
                                waits=w, inc="dq" if g == ng - 1 else None)
                    v = kb.emit("gpsimd", "dma_start",
                                out=cache[ts(crow, 128), ccol0:ccol0 + cw],
                                in_=wnat[b][:, 0:cw],
                                waits=(("dq", kb.val("dq")),),
                                inc=f"wc{b}", inck=16)
                    return (f"wc{b}", v)

                def produce_gu(ic):
                    wg = [produce_tile(qwg, szg, wg_c, ic, c0, cw, "wcg")
                          for c0, cw in GCH]
                    wu = [produce_tile(qwu, szu, wu_c, ic, c0, cw, "wcu")
                          for c0, cw in GCH]
                    wc_gu[ic] = (wg, wu)

                def produce_down(j):
                    fc, cj = divmod(j, len(DCH))
                    c0, cw = DCH[cj]
                    produce_tile(qwd, szd, wd_c, fc, c0, cw, "wcd")
                    wcd_done["v"] = tuple((f"wc{i}", kb.val(f"wc{i}"))
                                          for i in range(NBUF))

                pe_mark = {}
                hw_val = {}
                down_j = {"j": 0}
                ND = NHC * len(DCH)

                def consume_ic(ic):
                    s = ic % 2
                    for m, bufs, cache, wci in (("g", wgT, wg_c, 0),
                                                ("u", wuT, wu_c, 1)):
                        buf = bufs[s]
                        w = [("pe", pe_mark.get(ic - 2, 0))]
                        if call == 1:
                            w.extend(wc_gu[ic][wci])
                        kb.emit("sync", "dma_start", out=buf[:, :, :],
                                in_=cache[ts(ic, 128), :], transpose=True,
                                waits=tuple(w), inc=f"wt{m}{s}", inck=16)
                    wt_w = ((f"wtg{s}", kb.val(f"wtg{s}")),
                            (f"wtu{s}", kb.val(f"wtu{s}")))
                    par = ic % 2
                    psg = PSB[2 * par]
                    psu = PSB[2 * par + 1]
                    sil_guard = ("se_sil", kb.val("se_sil") - 1)
                    mul_guard = ("ve_mul", kb.val("ve_mul") - 1)
                    gbuf = wgT[ic % 2]
                    ubuf = wuT[ic % 2]
                    halves = [(0, T)] if T <= 512 else [(0, TH), (TH, T)]
                    for hc in range(NHC):
                        w0 = wt_w + (("xt", xt_ready), sil_guard,
                              mul_guard) if hc == 0 else ()
                        for (h0, h1) in halves:
                            kb.emit("tensor", "matmul", psg[:, h0:h1],
                                    gbuf[:, hc, :], xlnT[:, hc, h0:h1],
                                    start=(hc == 0), stop=(hc == NHC - 1),
                                    waits=w0 if h0 == 0 else ())
                    for hc in range(NHC):
                        for (h0, h1) in halves:
                            kb.emit("tensor", "matmul", psu[:, h0:h1],
                                    ubuf[:, hc, :], xlnT[:, hc, h0:h1],
                                    start=(hc == 0), stop=(hc == NHC - 1),
                                    inc="pe" if (hc == NHC - 1 and h1 == T) else None)
                    pev = kb.val("pe")
                    hb = ic % 2
                    hw_guard = (f"hw{hb}", kb.val(f"hw{hb}"))
                    sv = kb.emit("scalar", "activation", out=hsil[:, :],
                                 in_=psg[:, 0:T], func=AF.Silu,
                                 waits=(("pe", pev), ("ve_mul", kb.val("ve_mul"))),
                                 inc="se_sil")
                    mv = kb.emit("vector", "tensor_tensor", out=hbf[hb][:, :],
                                 in0=hsil[:, :], in1=psu[:, 0:T], op=OP.mult,
                                 waits=(("se_sil", sv), hw_guard), inc="ve_mul")
                    hwv = kb.emit("scalar", "dma_start", out=hbuf[ic, :, :],
                                  in_=hbf[hb][:, :], waits=(("ve_mul", mv),),
                                  inc=f"hw{hb}", inck=16)
                    hw_val[ic] = (f"hw{hb}", hwv)
                    pe_mark[ic] = kb.val("pe")

                if call == 1:
                    cast_jobs = []
                    R3 = 3 * D
                    CR = max(128, R3 // 96)
                    for r0 in range(0, R3, CR):
                        rn = min(CR, R3 - r0)
                        cast_jobs.append((wqkv_c[r0:r0 + rn, :],
                                          wqkv[r0:r0 + rn, :], "wqc"))
                    CR2 = max(128, D // 16)
                    for r0 in range(0, D, CR2):
                        cast_jobs.append((wout_c[r0:r0 + CR2, :],
                                          wout[r0:r0 + CR2, :], "woc"))
                    cj = {"j": 0}

                    def emit_casts(frac):
                        njobs = min(len(cast_jobs),
                                    int(frac * len(cast_jobs)) + 1)
                        while cj["j"] < njobs:
                            o, i, sem = cast_jobs[cj["j"]]
                            kb.emit("gpsimd", "dma_start", out=o, in_=i,
                                    inc=sem, inck=16)
                            cj["j"] += 1

                    produce_gu(0)
                    produce_gu(1)
                    for ic in range(NI):
                        if ic + 2 < NI:
                            produce_gu(ic + 2)
                        nd = min(ND, (ic + 1) * ND // NI)
                        while down_j["j"] < nd:
                            produce_down(down_j["j"])
                            down_j["j"] += 1
                        consume_ic(ic)
                        emit_casts((ic + 1) / NI)
                else:
                    for ic in range(NI):
                        consume_ic(ic)

                hw_fin = (("hw0", kb.val("hw0")), ("hw1", kb.val("hw1")))
                pe_gu_done = kb.val("pe")

                groups = [(g0, min(GSZ, NI - g0)) for g0 in range(0, NI, GSZ)]
                gmark = {}
                gidx = 0
                hln = {"n": 0}
                for fc in range(FC_N):
                    ev_guard = ("ve_x", kb.val("ve_x"))
                    for gi, (g0, gn) in enumerate(groups):
                        b = gidx % 2
                        w = [("pe_d", gmark.get(gidx - 2, 0))]
                        if call == 1:
                            w.extend(wcd_done["v"])
                        kb.emit("sync", "dma_start", out=wdT[b][:, 0:gn, 0:FCW],
                                in_=wd_c[ts(fc, FCW), g0 * 128:(g0 + gn) * 128],
                                transpose=True, waits=tuple(w), inc=f"wtd{b}",
                                inck=16)
                        tl = (f"wtd{b}", kb.val(f"wtd{b}"))
                        for j in range(gn):
                            ic = g0 + j
                            hb = hln["n"] % 3
                            hln["n"] += 1
                            hl_guard = ("pe_d", kb.val("pe_d") - 2)
                            hneed = (hw_val[ic],) if (call == 1 and fc == 0) else hw_fin
                            hl = kb.emit("gpsimd", "dma_start",
                                         out=hld[hb][:, :], in_=hbuf[ic, :, :],
                                         waits=hneed + (hl_guard,),
                                         inc=f"hr{hb}", inck=16)
                            for tt in range(NT):
                                w0 = []
                                if tt == 0:
                                    w0 = [(f"hr{hb}", hl), tl]
                                    if ic == 0:
                                        w0.append(ev_guard)
                                kb.emit("tensor", "matmul",
                                        PSB[(tt * FCW) // 1024][:, (tt * FCW) % 1024:(tt * FCW) % 1024 + FCW],
                                        hld[hb][:, ts(tt, 128)],
                                        wdT[b][:, j, 0:FCW],
                                        start=(ic == 0), stop=(ic == NI - 1),
                                        waits=tuple(w0),
                                        inc="pe_d" if tt == NT - 1 else None)
                        gmark[gidx] = kb.val("pe_d")
                        gidx += 1
                    pedv = kb.val("pe_d")
                    for tt in range(NT):
                        b = tt % 2
                        xr = kb.emit("scalar", "dma_start", out=xres[b][:, 0:FCW],
                                     in_=x_res_src[ts(tt, 128), ts(fc, FCW)],
                                     waits=(("ve_x", kb.val("ve_x") - 1),),
                                     inc=f"xr{b}", inck=16)
                        yv = kb.emit("scalar", "activation", out=ytmp[b][:, 0:FCW],
                                     in_=PSB[(tt * FCW) // 1024][:, (tt * FCW) % 1024:(tt * FCW) % 1024 + FCW],
                                     func=AF.Copy, scale=0.5,
                                     waits=(("pe_d", pedv),
                                            ("ve_x", kb.val("ve_x") - 1)),
                                     inc="se")
                        xv = kb.emit("vector", "tensor_tensor", out=xo[b][:, 0:FCW],
                                     in0=ytmp[b][:, 0:FCW], in1=xres[b][:, 0:FCW],
                                     op=OP.add,
                                     waits=(("se", yv), (f"xr{b}", xr),
                                            (f"xs{b}", kb.val(f"xs{b}"))),
                                     inc="ve_x")
                        if collect_stats:
                            kb.emit("vector", "reduce_sum",
                                    stats[:, tt * 8 + fc:tt * 8 + fc + 1],
                                    xo[b][:, 0:FCW], axis=AX.X, inc="ve")
                            sq = kb.emit("scalar", "activation",
                                         out=xnrm[0][:, 0:FCW],
                                         in_=xo[b][:, 0:FCW], func=AF.Square,
                                         accum_out=stats[:, 64 + tt * 8 + fc:64 + tt * 8 + fc + 1],
                                         waits=(("ve_x", xv),), inc="se_st")
                            wv = (("se_st", sq),)
                        else:
                            wv = (("ve_x", xv),)
                        kb.emit("scalar", "dma_start",
                                out=x_dst[ts(tt, 128), ts(fc, FCW)],
                                in_=xo[b][:, 0:FCW], waits=wv, inc=f"xs{b}",
                                inck=16)

                xs_fin = (("xs0", kb.val("xs0")), ("xs1", kb.val("xs1")))
                if final_store:
                    kb.wait("gpsimd", *xs_fin)
                else:
                    barrier(*(xs_fin + (("pe_d", kb.val("pe_d")),
                            ("ve_x", kb.val("ve_x")),
                            ("hr0", kb.val("hr0")), ("hr1", kb.val("hr1")),
                            ("se_st" if collect_stats else "se",
                             kb.val("se_st") if collect_stats else kb.val("se")))))
                kb.play(block)
                return xs_fin


        NH_LOC = NHC // 4
        S = 4 * T
        NQB = S // 512
        QBR = max(1, T // 512)

        def emit_qkv_phase():
            with contextlib.ExitStack() as ps:
                pen = ps.enter_context
                wqT2 = [pen(nc.sbuf_tensor(f"wqT_{i}", [128, NHC, 512], BF16))
                        for i in range(2)]
                evq = [pen(nc.sbuf_tensor(f"evq_{i}", [128, 512], BF16))
                       for i in range(2)]
                ntmp = [pen(nc.sbuf_tensor(f"ntmp_{i}", [128, D], BF16))
                        for i in range(2)]
                hT = [pen(nc.sbuf_tensor(f"hT_{i}", [128, T], BF16))
                      for i in range(2)]
                hTa = [pen(nc.sbuf_tensor(f"hTa_{i}", [128, T], BF16))
                       for i in range(2)]
                lnp = pen(nc.sbuf_tensor("lnp", [128, 4 * NHC], F32))
                lnq = pen(nc.sbuf_tensor("lnq", [128, 2 * NHC], F32))
                block = pen(nc.Block())

                xt2 = emit_ln_to_xlnT(x1b, "cols", ("pe", kb.val("pe")))

                lnl = 0
                for i, srcp in enumerate((qlnw, qlnb, klnw, klnb)):
                    lnl = kb.emit("gpsimd", "dma_start",
                                  out=lnp[:, i * NHC:(i + 1) * NHC],
                                  in_=srcp, inc="lnl", inck=16)
                lq = kb.emit("vector", "tensor_scalar", out=lnq[:, :],
                             in0=lnp[:, 0:2 * NHC], scalar1=QK_SCALE,
                             scalar2=None, op0=OP.mult, op1=OP.bypass,
                             waits=(("lnl", lnl),), inc="ve")

                wq_total = ("wqc", kb.val("wqc"))
                segs = [("k", D, kt_in, "ccK"), ("v", 2 * D, vt_in, "ccV"),
                        ("q", 0, qt_in, "ccQ")]
                FCQ = D // 512
                qm_mark = {}
                gfc = {"i": 0}
                for sname, row0, dest, ccsem in segs:
                    cnt = 0
                    seg0 = (("nw0", kb.val("nw0")), ("nw1", kb.val("nw1")),
                            ("tp0", kb.val("tp0")), ("tp1", kb.val("tp1")),
                            ("xt", xt2))
                    for fc in range(FCQ):
                        gi_ = gfc["i"]
                        b = gi_ % 2
                        w = [wq_total, ("pe_q", qm_mark.get(gi_ - 2, 0))]
                        kb.emit("sync", "dma_start", out=wqT2[b][:, :, :],
                                in_=wqkv_c[row0 + fc * 512:row0 + fc * 512 + 512, :],
                                transpose=True, waits=tuple(w),
                                inc=f"wq{b}", inck=16)
                        wqv = (f"wq{b}", kb.val(f"wq{b}"))
                        for tt in range(NT):
                            par = cnt % 2
                            pss = PSB[par][:, 0:512]
                            evg = ("ve_q", kb.val("ve_q") - 1)
                            for hc in range(NHC):
                                w0 = (wqv, ("xt", xt2), evg) if hc == 0 else ()
                                kb.emit("tensor", "matmul", pss,
                                        xlnT[:, hc, ts(tt, 128)],
                                        wqT2[b][:, hc, :],
                                        start=(hc == 0), stop=(hc == NHC - 1),
                                        waits=w0,
                                        inc="pe_q" if hc == NHC - 1 else None)
                            pqv = kb.val("pe_q")
                            sg_g = (f"sg{par}", kb.val(f"sg{par}"))
                            ev = kb.emit("vector", "tensor_scalar",
                                         out=evq[par][:, :], in0=pss,
                                         scalar1=-8.0, scalar2=8.0,
                                         op0=OP.max, op1=OP.min,
                                         waits=(("pe_q", pqv), sg_g),
                                         inc="ve_q")
                            wv = (("ve_q", ev),)
                            if sname != "v":
                                kb.emit("vector", "reduce_sum",
                                        stats[:, tt * 8 + fc:tt * 8 + fc + 1],
                                        evq[par][:, :], axis=AX.X, inc="ve")
                                sq = kb.emit("scalar", "activation",
                                             out=ntmp[0][:, 0:512],
                                             in_=evq[par][:, :], func=AF.Square,
                                             accum_out=stats[:, 64 + tt * 8 + fc:64 + tt * 8 + fc + 1],
                                             waits=(("ve_q", ev),) +
                                             (seg0 if cnt == 0 else ()),
                                             inc="se_q")
                                wv = (("se_q", sq),)
                            kb.emit("gpsimd", "dma_start",
                                    out=segb[ts(tt, 128), ts(fc, 512)],
                                    in_=evq[par][:, :],
                                    waits=wv + (seg0 if cnt == 0 else ()),
                                    inc=f"sg{par}", inck=16)
                            cnt += 1
                        qm_mark[gi_] = kb.val("pe_q")
                        gfc["i"] += 1
                    sg_fin = ((f"sg0", kb.val("sg0")), (f"sg1", kb.val("sg1")))
                    if sname != "v":
                        for tt in range(NT):
                            bb = tt % 2
                            ld = kb.emit("gpsimd", "dma_start", out=xtile[:, :],
                                         in_=segb[ts(tt, 128), :],
                                         waits=sg_fin + (("se_ln", kb.val("se_ln")),),
                                         inc="xl", inck=16)
                            sm = stats[:, 184:185]
                            qm_ = stats[:, 185:186]
                            kb.emit("vector", "reduce_sum", sm,
                                    stats[:, tt * 8:tt * 8 + FCQ], axis=AX.X,
                                    inc="ve")
                            kb.emit("vector", "reduce_sum", qm_,
                                    stats[:, 64 + tt * 8:64 + tt * 8 + FCQ],
                                    axis=AX.X, inc="ve")
                            rdy, sc_col, bi_col = ln_finalize(tt, sm, qm_)
                            nw_g = (f"nw{bb}", kb.val(f"nw{bb}"))
                            nd = kb.emit("scalar", "activation",
                                         out=ntmp[bb][:, :], in_=xtile[:, :],
                                         func=AF.Identity, bias=bi_col,
                                         scale=sc_col,
                                         waits=(("ve", rdy), ("xl", ld), nw_g),
                                         inc="se_ln")
                            kb.emit("gpsimd", "dma_start",
                                    out=xlnb[ts(tt, 128), :], in_=ntmp[bb][:, :],
                                    waits=(("se_ln", nd),), inc=f"nw{bb}",
                                    inck=16)
                        nw_fin = (("nw0", kb.val("nw0")), ("nw1", kb.val("nw1")))
                        hsrc = xlnb
                    else:
                        nw_fin = sg_fin
                        hsrc = segb
                    for h in range(NHC):
                        hb = h % 2
                        if sname == "v":
                            kb.emit("gpsimd", "dma_start",
                                    out=dest[h // NH_LOC, h % NH_LOC, :, :],
                                    in_=segb[0:T, ts(h, 128)],
                                    waits=sg_fin if h == 0 else (),
                                    inc=f"sd{hb}", inck=16)
                            continue
                        tp_g = ("se_a", kb.val("se_a") - 1)
                        tp = kb.emit("sync", "dma_start", out=hT[hb][:, :],
                                     in_=hsrc[0:T, ts(h, 128)], transpose=True,
                                     waits=nw_fin + (tp_g,),
                                     inc=f"tp{hb}", inck=16)
                        tpv = (f"tp{hb}", kb.val(f"tp{hb}"))
                        if sname == "q":
                            sc_col = lnq[:, h:h + 1]
                            bi_col = lnq[:, NHC + h:NHC + h + 1]
                        else:
                            sc_col = lnp[:, 2 * NHC + h:2 * NHC + h + 1]
                            bi_col = lnp[:, 3 * NHC + h:3 * NHC + h + 1]
                        sd_g = (f"sd{hb}", kb.val(f"sd{hb}"))
                        av = kb.emit("scalar", "activation",
                                     out=hTa[hb][:, :], in_=hT[hb][:, :],
                                     func=AF.Identity, scale=sc_col,
                                     bias=bi_col,
                                     waits=(tpv, ("ve", lq), sd_g),
                                     inc="se_a")
                        kb.emit("gpsimd", "dma_start",
                                out=dest[h // NH_LOC, h % NH_LOC, :, :],
                                in_=hTa[hb][:, :], waits=(("se_a", av),),
                                inc=f"sd{hb}", inck=16)
                    sd_fin = (("sd0", kb.val("sd0")), ("sd1", kb.val("sd1")))
                    gathered = {"ccK": kt_out, "ccV": vt_out,
                                "ccQ": qt_out}[ccsem]
                    cmp_t = {"ccK": kt_cmp, "ccV": vt_cmp,
                             "ccQ": qt_cmp}[ccsem]
                    kb.emit("gpsimd", "collective_compute",
                            "AllGather", OP.bypass, replica_groups=RG,
                            ins=[dest.ap().opt()],
                            outs=[gathered.ap().opt()],
                            waits=sd_fin, inc=ccsem)
                    kb.emit("gpsimd", "dma_start", out=cmp_t,
                            in_=gathered.ap()
                            .rearrange("(s d) a b c -> s d a b c", d=4)[:, rkv],
                            waits=((ccsem, kb.val(ccsem)),),
                            inc=ccsem + "c", inck=16)
                barrier(("sd0", kb.val("sd0")), ("sd1", kb.val("sd1")),
                        ("tp0", kb.val("tp0")), ("tp1", kb.val("tp1")),
                        ("se_a", kb.val("se_a")),
                        ("pe_q", kb.val("pe_q")), ("ve_q", kb.val("ve_q")),
                        ("se_q", kb.val("se_q")))
                kb.play(block)

        def emit_attn_phase():
            with contextlib.ExitStack() as ps:
                pen = ps.enter_context
                Kh = pen(nc.sbuf_tensor("Kh", [128, S], BF16))
                Qh = pen(nc.sbuf_tensor("Qh", [128, S], BF16))
                Vh = pen(nc.sbuf_tensor("Vh", [128, S // 128, 128], BF16))
                Pf = pen(nc.sbuf_tensor("Pf", [128, S], F32))
                pfd = pen(nc.sbuf_tensor("pfd", [128, 512], F32))
                maskC = pen(nc.sbuf_tensor("maskC", [128, 4, 512], F32))
                identC = pen(nc.sbuf_tensor("identC", [128, 128], BF16))
                scor = pen(nc.sbuf_tensor("scor", [128, S], F32))
                prb = pen(nc.sbuf_tensor("prb", [128, 2, S], BF16))
                prT = [pen(nc.sbuf_tensor(f"prT_{i}", [128, 256], BF16))
                       for i in range(2)]
                dg = pen(nc.sbuf_tensor("dg", [128, 2, 128], BF16))
                aosb = [pen(nc.sbuf_tensor(f"aosb_{i}", [128, 256], BF16))
                        for i in range(2)]
                block = pen(nc.Block())

                mk = kb.emit("sync", "dma_start", out=maskC[:, :, :],
                             in_=masksI.rearrange("q p f -> p q f"),
                             inc="mk", inck=16)
                mk = kb.emit("sync", "dma_start", out=identC[:, :], in_=ident,
                             inc="mk", inck=16)
                mkv = ("mk", kb.val("mk"))

                sc_cnt = 0
                pv_cnt = 0
                ao_cnt = 0
                head_mark = {"pe_s": 0, "pe_pv": 0, "ve_s": 0}
                for hl in range(NH_LOC):
                    kg = ("pe_s", head_mark["pe_s"])
                    vg = ("pe_pv", head_mark["pe_pv"])
                    pg = ("ve_s", head_mark["ve_s"])
                    for r in range(4):
                        kb.emit("gpsimd", "dma_start", out=Kh[:, ts(r, T)],
                                in_=kt_cmp[r, hl, :, :],
                                waits=(("ccKc", kb.val("ccKc")), kg),
                                inc="kh", inck=16)
                        kb.emit("gpsimd", "dma_start", out=Qh[:, ts(r, T)],
                                in_=qt_cmp[r, hl, :, :],
                                waits=(("ccQc", kb.val("ccQc")), kg),
                                inc="kh", inck=16)
                        kb.emit("gpsimd", "dma_start",
                                out=Vh[:, r * (T // 128):(r + 1) * (T // 128), :],
                                in_=vt_cmp[r, hl, :, :]
                                .rearrange("(c p) d -> p c d", p=128),
                                waits=(("ccVc", kb.val("ccVc")), vg),
                                inc="vh", inck=16)
                    pfl = kb.emit("gpsimd", "dma_start", out=Pf[:, :],
                                  in_=alibiI[hl, :, :],
                                  waits=(pg,), inc="pfl", inck=16)
                    khv = ("kh", kb.val("kh"))
                    vhv = ("vh", kb.val("vh"))
                    pflv = ("pfl", kb.val("pfl"))
                    for qb in range(NQB):
                        span = qb + 1
                        for qt in range(4):
                            qcol = qb * 512 + qt * 128
                            dv = kb.emit("vector", "tensor_tensor", out=pfd[:, :],
                                         in0=Pf[:, ts(qb, 512)],
                                         in1=maskC[:, qt, :], op=OP.add,
                                         waits=(pflv, mkv,
                                                ("ve_s", kb.val("ve_s"))),
                                         inc="ve")
                            exp_guard = ("se_e", kb.val("se_e"))
                            for kt in range(span):
                                par = sc_cnt % 2
                                pss = PSB[2][:, par * 512:par * 512 + 512]
                                evg = ("ve_s", kb.val("ve_s") - 1)
                                kb.emit("tensor", "matmul", pss,
                                        Qh[:, qcol:qcol + 128],
                                        Kh[:, ts(kt, 512)],
                                        start=True, stop=True,
                                        waits=(khv, evg), inc="pe_s")
                                psv = kb.val("pe_s")
                                bias_t = pfd[:, :] if kt == qb \
                                    else Pf[:, ts(kt, 512)]
                                bw = [("pe_s", psv)]
                                if kt == qb:
                                    bw.append(("ve", dv))
                                if kt == 0:
                                    bw.append(exp_guard)
                                kb.emit("vector", "tensor_tensor",
                                        out=scor[:, ts(kt, 512)], in0=pss,
                                        in1=bias_t, op=OP.add,
                                        waits=tuple(bw), inc="ve_s")
                                sc_cnt += 1
                            vsv = kb.val("ve_s")
                            kb.emit("vector", "reduce_max",
                                    stats[:, 192 + qt:193 + qt],
                                    scor[:, 0:span * 512], axis=AX.X,
                                    waits=(("ve_s", vsv),), inc="ve")
                            nmx = kb.emit("vector", "tensor_scalar",
                                          out=stats[:, 196 + qt:197 + qt],
                                          in0=stats[:, 192 + qt:193 + qt],
                                          scalar1=-1.0, scalar2=None,
                                          op0=OP.mult, op1=OP.bypass, inc="ve")
                            prb_g = ("pe_t", kb.val("pe_t"))
                            ex = kb.emit("scalar", "activation",
                                         out=prb[:, qt % 2, 0:span * 512],
                                         in_=scor[:, 0:span * 512], func=AF.Exp,
                                         bias=stats[:, 196 + qt:197 + qt],
                                         scale=1.0,
                                         accum_out=stats[:, 200 + qt:201 + qt],
                                         waits=(("ve", nmx), prb_g),
                                         inc="se_e")
                            rc = kb.emit("vector", "reciprocal",
                                         stats[:, 204 + qt:205 + qt],
                                         stats[:, 200 + qt:201 + qt],
                                         waits=(("se_e", ex),), inc="ve")
                            kb.emit("vector", "tensor_scalar",
                                    out=dg[:, qt % 2, :], in0=identC[:, :],
                                    scalar1=stats[:, 204 + qt:205 + qt],
                                    scalar2=None, op0=OP.mult, op1=OP.bypass,
                                    waits=(("ve", rc), mkv, prb_g), inc="ve_d")
                            if qt % 2 == 1:
                                dgv = ("ve_d", kb.val("ve_d"))
                                exv = ("se_e", kb.val("se_e"))
                                nkc = span * 4
                                pvo = PSB[1][:, (pv_cnt % 2) * 512:
                                             (pv_cnt % 2) * 512 + 256]
                                pv_evac_g = ("se_o", kb.val("se_o") - 1)

                                def tmm(kc):
                                    sl = kc % 2
                                    pst = PSB[0][:, sl * 512:sl * 512 + 256]
                                    ev_g = ("se_p", kb.val("se_p") - 1)
                                    for j, qq in enumerate((qt - 1, qt)):
                                        kb.emit("tensor", "matmul",
                                                pst[:, ts(j, 128)],
                                                prb[:, qq % 2, ts(kc, 128)],
                                                dg[:, qq % 2, :],
                                                start=True, stop=True,
                                                waits=(dgv, exv, ev_g)
                                                if j == 0 else (),
                                                inc="pe_t" if j == 1 else None)
                                    pet = kb.val("pe_t")
                                    kb.emit("scalar", "copy",
                                            out=prT[sl][:, :], in_=pst,
                                            waits=(("pe_t", pet),
                                                   ("pe_pv", kb.val("pe_pv"))),
                                            inc="se_p")
                                    return ("se_p", kb.val("se_p"))

                                pmarks = {}
                                pmarks[0] = tmm(0)
                                for kc in range(nkc):
                                    if kc + 1 < nkc:
                                        pmarks[kc + 1] = tmm(kc + 1)
                                    w0 = [pmarks[kc], vhv]
                                    if kc == 0:
                                        w0.append(pv_evac_g)
                                    kb.emit("tensor", "matmul", pvo,
                                            Vh[:, kc, :], prT[kc % 2][:, :],
                                            start=(kc == 0),
                                            stop=(kc == nkc - 1),
                                            waits=tuple(w0), inc="pe_pv")
                                pvv = kb.val("pe_pv")
                                ab = ao_cnt % 2
                                ao_g = (f"ao{ab}", kb.val(f"ao{ab}"))
                                ov = kb.emit("scalar", "copy",
                                             out=aosb[ab][:, :], in_=pvo,
                                             waits=(("pe_pv", pvv), ao_g),
                                             inc="se_o")
                                pr = (qt - 1) // 2
                                kb.emit("gpsimd", "dma_start",
                                        out=ao_in[qb // QBR, hl, :,
                                                  (qb % QBR) * 512 + pr * 256:
                                                  (qb % QBR) * 512 + pr * 256 + 256],
                                        in_=aosb[ab][:, :],
                                        waits=(("se_o", ov),),
                                        inc=f"ao{ab}", inck=16)
                                ao_cnt += 1
                                pv_cnt += 1
                    head_mark = {"pe_s": kb.val("pe_s"),
                                 "pe_pv": kb.val("pe_pv"),
                                 "ve_s": kb.val("ve_s")}
                kb.emit("gpsimd", "collective_compute",
                        "AllGather", OP.bypass, replica_groups=RG,
                        ins=[ao_in.ap().opt()], outs=[ao_out.ap().opt()],
                        waits=(("ao0", kb.val("ao0")), ("ao1", kb.val("ao1"))),
                        inc="ccA")
                kb.emit("gpsimd", "dma_start", out=ao_cmp,
                        in_=ao_out.ap()
                        .rearrange("(s d) a b c -> s d a b c", d=4)[:, rkv],
                        waits=(("ccA", kb.val("ccA")),), inc="ccAc", inck=16)
                barrier(("ccA", kb.val("ccA")), ("se_o", kb.val("se_o")),
                        ("pe_pv", kb.val("pe_pv")))
                kb.play(block)

        def emit_wout_phase():
            with contextlib.ExitStack() as ps:
                pen = ps.enter_context
                woT = [pen(nc.sbuf_tensor("woT_0", [128, NHC, 512], BF16))]
                aot = pen(nc.sbuf_tensor("aot", [128, NHC, T], BF16))
                block = pen(nc.Block())

                for hc in range(NHC):
                    kb.emit("gpsimd", "dma_start", out=aot[:, hc, :],
                            in_=ao_cmp[hc // NH_LOC, hc % NH_LOC, :, :],
                            waits=(("ccAc", kb.val("ccAc")),), inc="aol",
                            inck=16)
                aolv = ("aol", kb.val("aol"))
                wo_total = ("woc", kb.val("woc"))
                cnt = 0
                wm_mark = {}
                for fc in range(D // 512):
                    b = 0
                    kb.emit("sync", "dma_start", out=woT[b][:, :, :],
                            in_=wout_c[ts(fc, 512), :], transpose=True,
                            waits=(wo_total, ("pe_w", wm_mark.get(fc - 1, 0))),
                            inc=f"wo{b}", inck=16)
                    wov = (f"wo{b}", kb.val(f"wo{b}"))
                    for tt in range(NT):
                        par = cnt % 2
                        pss = PSB[par][:, 0:512]
                        evg = ("ve_x", kb.val("ve_x") - 1)
                        for hc in range(NHC):
                            w0 = (wov, aolv, evg) if hc == 0 else ()
                            kb.emit("tensor", "matmul", pss,
                                    aot[:, hc, ts(tt, 128)], woT[b][:, hc, :],
                                    start=(hc == 0), stop=(hc == NHC - 1),
                                    waits=w0,
                                    inc="pe_w" if hc == NHC - 1 else None)
                        pwv = kb.val("pe_w")
                        xb = par
                        xr = kb.emit("scalar", "dma_start",
                                     out=xres[xb][:, :],
                                     in_=x1b[ts(tt, 128), ts(fc, 512)],
                                     waits=(("ve_x", kb.val("ve_x") - 1),),
                                     inc=f"xr{xb}", inck=16)
                        xv = kb.emit("vector", "tensor_tensor",
                                     out=xo[xb][:, :], in0=pss,
                                     in1=xres[xb][:, :], op=OP.add,
                                     waits=(("pe_w", pwv), (f"xr{xb}", xr),
                                            (f"xs{xb}", kb.val(f"xs{xb}"))),
                                     inc="ve_x")
                        kb.emit("vector", "reduce_sum",
                                stats[:, tt * 8 + fc:tt * 8 + fc + 1],
                                xo[xb][:, :], axis=AX.X, inc="ve")
                        sq = kb.emit("scalar", "activation",
                                     out=xnrm[0][:, 0:512], in_=xo[xb][:, :],
                                     func=AF.Square,
                                     accum_out=stats[:, 64 + tt * 8 + fc:64 + tt * 8 + fc + 1],
                                     waits=(("ve_x", xv),), inc="se_st")
                        kb.emit("scalar", "dma_start",
                                out=x2b[ts(tt, 128), ts(fc, 512)],
                                in_=xo[xb][:, :],
                                waits=(("se_st", kb.val("se_st")),),
                                inc=f"xs{xb}", inck=16)
                        cnt += 1
                    wm_mark[fc] = kb.val("pe_w")
                barrier(("xs0", kb.val("xs0")), ("xs1", kb.val("xs1")),
                        ("pe_w", kb.val("pe_w")), ("se_st", kb.val("se_st")))
                kb.play(block)

        xs_fin1 = emit_ffn_phase(1, x0, x1b, True, x0, "self",
                                 final_store=(stop_after == "ffn1"))

        if stop_after == "ffn1":
            with nc.Block() as blk:
                for tt in range(NT):
                    kb.emit("gpsimd", "dma_start", out=xoutD[ts(tt, 128), :],
                            in_=x1b[ts(tt, 128), :],
                            waits=xs_fin1, inc="fin", inck=16)
                kb.wait("gpsimd", ("fin", kb.val("fin")))
                kb.play(blk)
            return nc

        emit_qkv_phase()
        emit_attn_phase()
        emit_wout_phase()
        xs_fin2 = emit_ffn_phase(2, x2b, xoutD, False, x2b, "cols",
                                 final_store=True)
        with nc.Block() as blk:
            kb.wait("gpsimd", *xs_fin2)
            kb.play(blk)
    return nc



def _const_inputs():
    ident = np.eye(128, dtype=np.float32).astype(BF)
    masks = np.zeros((4, 128, 512), dtype=np.float32)
    for qt in range(4):
        d = qt * 128
        p = np.arange(128)[:, None]
        j = np.arange(512)[None, :]
        masks[qt] = np.where(j > d + p, NEG, 0.0).astype(np.float32)
    return ident, masks


def alibi_slopes(n_heads, bias_max=8.0):
    next_pow2 = 2 ** int(np.ceil(np.log2(n_heads)))
    m = np.arange(1, next_pow2 + 1, dtype=np.float32) * (bias_max / next_pow2)
    slopes = 1.0 / np.power(2.0, m)
    if next_pow2 != n_heads:
        slopes = np.concatenate([slopes[1::2], slopes[::2]])[:n_heads]
    return slopes.astype(np.float32)


def _alibi_for_core(core):
    nh = D // 128
    nh_loc = nh // 4
    h0 = nh_loc * (core % 4)
    sl = alibi_slopes(nh)[h0:h0 + nh_loc]
    kk = np.arange(4 * T, dtype=np.float32)
    a = sl[:, None] * kk[None, :]
    return np.broadcast_to(a[:, None, :], (nh_loc, 128, 4 * T)).copy()


def kernel(L0_attn_hidden_states, L0_attn_Wqkv, L0_attn_q_ln_w, L0_attn_q_ln_b,
           L0_attn_k_ln_w, L0_attn_k_ln_b, L0_attn_Wout, L0_ffn_qw_gate,
           L0_ffn_qw_up, L0_ffn_qw_down, L0_ffn_s_gate, L0_ffn_z_gate,
           L0_ffn_s_up, L0_ffn_z_up, L0_ffn_s_down, L0_ffn_z_down,
           _stop_after=None, _only_maps=False):
    x = np.ascontiguousarray(np.asarray(L0_attn_hidden_states, dtype=np.float32)
                             .reshape(8 * T, D))
    ident, masks = _const_inputs()

    def col128(v):
        return np.ascontiguousarray(
            np.asarray(v, np.float32).reshape(D // 128, 128).T)

    in_maps = _prepare_in_maps_inner(x, ident, masks, col128,
        L0_attn_Wqkv, L0_attn_Wout, L0_ffn_qw_gate, L0_ffn_qw_up,
        L0_ffn_qw_down, L0_ffn_s_gate, L0_ffn_z_gate, L0_ffn_s_up,
        L0_ffn_z_up, L0_ffn_s_down, L0_ffn_z_down,
        L0_attn_q_ln_w, L0_attn_q_ln_b, L0_attn_k_ln_w, L0_attn_k_ln_b)
    if _only_maps:
        return in_maps

    nc = build_nc(stop_after=_stop_after)
    res = run_bass_kernel_spmd(nc, in_maps, core_ids=list(range(8)))
    out = np.concatenate([res.results[c]["xout"] for c in range(8)], axis=0)
    return out.reshape(2, 4 * T, D).astype(np.float32)


def _ilv(s, z):
    s = np.asarray(s, np.float32)
    z = np.asarray(z, np.float32)
    return np.ascontiguousarray(np.stack([s, z], axis=2).reshape(s.shape[0], -1))


def _prepare_in_maps_inner(x, ident, masks, col128, Wqkv, Wout, qwg, qwu, qwd,
                           sg, zg, su, zu, sd, zd, qw_, qb_, kw_, kb_):
    common = dict(
        wqkv=np.asarray(Wqkv, np.float32),
        wout=np.asarray(Wout, np.float32),
        qwg=np.asarray(qwg, np.int32),
        qwu=np.asarray(qwu, np.int32),
        qwd=np.asarray(qwd, np.int32),
        szg=_ilv(sg, zg),
        szu=_ilv(su, zu),
        szd=_ilv(sd, zd),
        qlnw=col128(qw_), qlnb=col128(qb_),
        klnw=col128(kw_), klnb=col128(kb_),
        ident=ident, masks=masks,
    )
    in_maps = []
    for c in range(8):
        m = dict(common)
        m["x0"] = np.ascontiguousarray(x[c * T:(c + 1) * T])
        m["alibi"] = _alibi_for_core(c)
        in_maps.append(m)
    return in_maps


def _prepare_in_maps(**inputs):
    return kernel(**inputs, _only_maps=True)


# version 10
# speedup vs baseline: 1.0865x; 1.0865x over previous
import contextlib
import numpy as np
import ml_dtypes
import concourse.bass as bass
import concourse.mybir as mybir
from concourse.engine_type import EngineType
from concourse.bass_utils import run_bass_kernel_spmd

F32 = mybir.dt.float32
BF16 = mybir.dt.bfloat16
I32 = mybir.dt.int32
AF = mybir.ActivationFunctionType
OP = mybir.AluOpType
AX = mybir.AxisListType
BF = ml_dtypes.bfloat16
ts = bass.ts

D = 4096
IF = 11008
NI = IF // 128
T = 1024
NT = T // 128
NHC = D // 128
EPS = 1e-5
CHUNK = 2048
GSZ = 4
QK_SCALE = 1.0 / np.sqrt(128.0)
NEG = -1e30

ENGINES = ("sync", "scalar", "vector", "tensor", "gpsimd")


class KB:

    def __init__(self, nc, stack):
        self.nc = nc
        self.stack = stack
        self.ops = {e: [] for e in ENGINES}
        self.semh = {}
        self.semc = {}
        self.last_inc = {}

    def sem(self, name):
        if name not in self.semh:
            self.semh[name] = self.stack.enter_context(self.nc.semaphore(name))
            self.semc[name] = 0
        return self.semh[name]

    def inc(self, name, k=1):
        self.sem(name)
        self.semc[name] += k
        return self.semc[name]

    def val(self, name):
        return self.semc.get(name, 0)

    def emit(self, eng, method, *args, waits=(), inc=None, inck=1, **kw):
        rv = None
        ih = None
        waits = list(waits)
        is_dma = method == "dma_start"
        if eng in ("vector", "scalar") and not is_dma and eng in self.last_inc:
            waits.append(self.last_inc[eng])
        if inc is not None:
            rv = self.inc(inc, inck)
            ih = self.semh[inc]
            if eng in ("vector", "scalar") and not is_dma:
                self.last_inc[eng] = (inc, rv)
        seen = set()
        wl = []
        for (n, v) in waits:
            if v > 0 and (n, v) not in seen:
                seen.add((n, v))
                wl.append((self.sem(n), v))

        def f(E, method=method, args=args, kw=kw, wl=wl, ih=ih, inck=inck):
            for (h, v) in wl:
                E.wait_ge(h, v)
            r = getattr(E, method)(*args, **kw)
            if ih is not None:
                r.then_inc(ih, inck)

        self.ops[eng].append(f)
        return rv

    def wait(self, eng, *waits):
        wl = [(self.sem(n), v) for (n, v) in waits if v > 0]

        def f(E, wl=wl):
            for (h, v) in wl:
                E.wait_ge(h, v)

        self.ops[eng].append(f)

    def play(self, block):
        for e in ENGINES:
            dec = getattr(block, e)
            ops = self.ops[e]

            def body(E, ops=ops):
                for f in ops:
                    f(E)

            dec(body)
        self.ops = {e: [] for e in ENGINES}


def build_nc(stop_after=None):
    NI = IF // 128
    NT = T // 128
    NHC = D // 128
    TH = T // 2
    FCW = min(512, D, T)
    FC_N = D // FCW
    GCH = [(c0, min(CHUNK, D - c0)) for c0 in range(0, D, CHUNK)]
    DCH = [(c0, min(CHUNK, IF - c0)) for c0 in range(0, IF, CHUNK)]
    nc = bass.Bass("TRN2", target_bir_lowering=False)

    x0 = nc.dram_tensor("x0", [T, D], F32, kind="ExternalInput").ap()
    wqkv = nc.dram_tensor("wqkv", [3 * D, D], F32, kind="ExternalInput").ap()
    wout = nc.dram_tensor("wout", [D, D], F32, kind="ExternalInput").ap()
    qwg = nc.dram_tensor("qwg", [IF, D], I32, kind="ExternalInput").ap()
    qwu = nc.dram_tensor("qwu", [IF, D], I32, kind="ExternalInput").ap()
    qwd = nc.dram_tensor("qwd", [D, IF], I32, kind="ExternalInput").ap()
    szg = nc.dram_tensor("szg", [IF, 64], F32, kind="ExternalInput").ap()
    szu = nc.dram_tensor("szu", [IF, 64], F32, kind="ExternalInput").ap()
    szd = nc.dram_tensor("szd", [D, 172], F32, kind="ExternalInput").ap()
    qlnw = nc.dram_tensor("qlnw", [128, NHC], F32, kind="ExternalInput").ap()
    qlnb = nc.dram_tensor("qlnb", [128, NHC], F32, kind="ExternalInput").ap()
    klnw = nc.dram_tensor("klnw", [128, NHC], F32, kind="ExternalInput").ap()
    klnb = nc.dram_tensor("klnb", [128, NHC], F32, kind="ExternalInput").ap()
    ident = nc.dram_tensor("ident", [128, 128], BF16, kind="ExternalInput").ap()
    masksI = nc.dram_tensor("masks", [4, 128, 512], F32, kind="ExternalInput").ap()
    alibiI = nc.dram_tensor("alibi", [NHC // 4, 128, 4 * T], F32,
                            kind="ExternalInput").ap()

    xoutD = nc.dram_tensor("xout", [T, D], F32, kind="ExternalOutput").ap()

    wg_c = nc.dram_tensor("wg_c", [IF, D], BF16, kind="Internal").ap()
    wu_c = nc.dram_tensor("wu_c", [IF, D], BF16, kind="Internal").ap()
    wd_c = nc.dram_tensor("wd_c", [D, IF], BF16, kind="Internal").ap()
    wqkv_c = nc.dram_tensor("wqkv_c", [3 * D, D], BF16, kind="Internal").ap()
    wout_c = nc.dram_tensor("wout_c", [D, D], BF16, kind="Internal").ap()
    x1b = nc.dram_tensor("x1b", [T, D], F32, kind="Internal").ap()
    x2b = nc.dram_tensor("x2b", [T, D], F32, kind="Internal").ap()
    xlnb = nc.dram_tensor("xlnb", [T, D], BF16, kind="Internal").ap()
    hbuf = nc.dram_tensor("hbuf", [NI, 128, T], BF16, kind="Internal").ap()
    segb = nc.dram_tensor("segb", [T, D], BF16, kind="Internal").ap()
    qt_in = nc.dram_tensor("qt_in", [4, NHC // 4, 128, T], BF16, kind="Internal")
    kt_in = nc.dram_tensor("kt_in", [4, NHC // 4, 128, T], BF16, kind="Internal")
    vt_in = nc.dram_tensor("vt_in", [4, NHC // 4, T, 128], BF16, kind="Internal")
    qt_out = nc.dram_tensor("qt_out", [16, NHC // 4, 128, T], BF16, kind="Internal")
    kt_out = nc.dram_tensor("kt_out", [16, NHC // 4, 128, T], BF16, kind="Internal")
    vt_out = nc.dram_tensor("vt_out", [16, NHC // 4, T, 128], BF16, kind="Internal")
    ao_in = nc.dram_tensor("ao_in", [4, NHC // 4, 128, T], BF16, kind="Internal")
    ao_out = nc.dram_tensor("ao_out", [16, NHC // 4, 128, T], BF16, kind="Internal")
    qt_cmp = nc.dram_tensor("qt_cmp", [4, NHC // 4, 128, T], BF16, kind="Internal").ap()
    kt_cmp = nc.dram_tensor("kt_cmp", [4, NHC // 4, 128, T], BF16, kind="Internal").ap()
    vt_cmp = nc.dram_tensor("vt_cmp", [4, NHC // 4, T, 128], BF16, kind="Internal").ap()
    ao_cmp = nc.dram_tensor("ao_cmp", [4, NHC // 4, 128, T], BF16, kind="Internal").ap()

    RG = [[0, 1, 2, 3], [4, 5, 6, 7]]
    pid = nc.partition_id([EngineType.Pool, EngineType.SP])
    rkv = pid % 4
    RIDX = [rkv + 4 * r for r in range(4)]

    with contextlib.ExitStack() as stack:
        en = stack.enter_context
        xtile = en(nc.sbuf_tensor("xtile", [128, D], F32))
        xnrm = [en(nc.sbuf_tensor("xnrm0", [128, D], BF16))]
        xres = [en(nc.sbuf_tensor(f"xres{i}", [128, 512], F32)) for i in range(2)]
        ytmp = [en(nc.sbuf_tensor(f"ytmp{i}", [128, 512], F32)) for i in range(2)]
        xo = [en(nc.sbuf_tensor(f"xo{i}", [128, 512], F32)) for i in range(2)]
        stats = en(nc.sbuf_tensor("stats", [128, 224], F32))
        PSB = [en(nc.psum_tensor(f"psb{i}", [128, 1024], F32)) for i in range(4)]
        kb = KB(nc, stack)

        def barrier(*finals):
            kb.wait("gpsimd", *finals)
            phv = kb.emit("gpsimd", "memset", stats[0:1, 220:221], 0.0, inc="ph")
            for e in ("sync", "scalar", "vector", "tensor"):
                kb.wait(e, ("ph", phv))
            return phv

        def ln_finalize(tt, s_col, q_col):
            base = 128 + 4 * tt
            mu = stats[:, base:base + 1]
            mu2 = stats[:, base + 1:base + 2]
            var = stats[:, base + 2:base + 3]
            sd = stats[:, base + 3:base + 4]
            sc = stats[:, 168 + tt:168 + tt + 1]
            bi = stats[:, 176 + tt:176 + tt + 1]
            kb.emit("vector", "tensor_scalar", out=mu, in0=s_col, scalar1=1.0 / D,
                    scalar2=None, op0=OP.mult, op1=OP.bypass, inc="ve")
            kb.emit("vector", "tensor_tensor", out=mu2, in0=mu, in1=mu, op=OP.mult,
                    inc="ve")
            kb.emit("vector", "tensor_scalar", out=var, in0=q_col, scalar1=1.0 / D,
                    scalar2=None, op0=OP.mult, op1=OP.bypass, inc="ve")
            kb.emit("vector", "tensor_tensor", out=var, in0=var, in1=mu2,
                    op=OP.subtract, inc="ve")
            ve_done = kb.emit("vector", "tensor_scalar", out=var, in0=var,
                              scalar1=EPS, scalar2=None, op0=OP.add,
                              op1=OP.bypass, inc="ve")
            sq_done = kb.emit("scalar", "activation", out=sd, in_=var, func=AF.Sqrt,
                              waits=(("ve", ve_done),), inc="se")
            kb.emit("vector", "reciprocal", sc, sd, waits=(("se", sq_done),),
                    inc="ve")
            kb.emit("vector", "tensor_tensor", out=bi, in0=mu, in1=sc, op=OP.mult,
                    inc="ve")
            bi_done = kb.emit("vector", "tensor_scalar", out=bi, in0=bi,
                              scalar1=-1.0, scalar2=None, op0=OP.mult,
                              op1=OP.bypass, inc="ve")
            return bi_done, sc, bi

        def emit_ln_to_xlnT(src, stats_mode, xlnT_guard, xlnT):
            for tt in range(NT):
                b = 0
                ld = kb.emit("gpsimd", "dma_start", out=xtile[:, :],
                             in_=src[ts(tt, 128), :],
                             waits=(("se_ln", kb.val("se_ln")),),
                             inc="xl", inck=16)
                xw_guard = (f"xw{b}", kb.val(f"xw{b}"))
                if stats_mode == "self":
                    kb.emit("vector", "reduce_sum", stats[:, tt:tt + 1],
                            xtile[:, :], axis=AX.X, waits=(("xl", ld),),
                            inc="ve")
                    qm = kb.emit("scalar", "activation", out=xnrm[b][:, :],
                                 in_=xtile[:, :], func=AF.Square,
                                 accum_out=stats[:, 64 + tt:64 + tt + 1],
                                 waits=(("xl", ld), xw_guard),
                                 inc="se")
                    kb.wait("vector", ("se", qm))
                    rdy, sc_col, bi_col = ln_finalize(tt, stats[:, tt:tt + 1],
                                                      stats[:, 64 + tt:64 + tt + 1])
                else:
                    sm = stats[:, 184:185]
                    qm_ = stats[:, 185:186]
                    kb.emit("vector", "reduce_sum", sm,
                            stats[:, tt * 8:tt * 8 + FC_N], axis=AX.X, inc="ve")
                    kb.emit("vector", "reduce_sum", qm_,
                            stats[:, 64 + tt * 8:64 + tt * 8 + FC_N], axis=AX.X,
                            inc="ve")
                    rdy, sc_col, bi_col = ln_finalize(tt, sm, qm_)
                nd = kb.emit("scalar", "activation", out=xnrm[b][:, :],
                             in_=xtile[:, :], func=AF.Identity,
                             bias=bi_col, scale=sc_col,
                             waits=(("ve", rdy), ("xl", ld), xw_guard),
                             inc="se_ln")
                kb.emit("gpsimd", "dma_start", out=xlnb[ts(tt, 128), :],
                        in_=xnrm[b][:, :], waits=(("se_ln", nd),),
                        inc=f"xw{b}", inck=16)
            xw_fin = (("xw0", kb.val("xw0")), ("xw1", kb.val("xw1")))
            for half in range(2):
                kb.emit("sync", "dma_start", out=xlnT[:, :, ts(half, T // 2)],
                        in_=xlnb[ts(half, T // 2), :], transpose=True,
                        waits=xw_fin + (xlnT_guard,),
                        inc="xt", inck=16)
            return kb.val("xt")

        def emit_ffn_phase(call, x_res_src, x_dst, collect_stats, src_for_ln,
                           ln_mode, final_store=False):
            with contextlib.ExitStack() as ps:
                pe = ps.enter_context
                NBUF = 4
                qnat = [pe(nc.sbuf_tensor(f"qnat{call}_{i}", [128, min(CHUNK, max(D, IF))], BF16))
                        for i in range(NBUF)]
                wnat = [pe(nc.sbuf_tensor(f"wnat{call}_{i}", [128, min(CHUNK, max(D, IF))], BF16))
                        for i in range(NBUF)]
                szb = [pe(nc.sbuf_tensor(f"szb{call}_{i}", [128, 64], F32))
                       for i in range(NBUF)]
                wgT = [pe(nc.sbuf_tensor(f"wgT{call}_{i}", [128, NHC, 128], BF16))
                       for i in range(2)]
                wuT = [pe(nc.sbuf_tensor(f"wuT{call}_{i}", [128, NHC, 128], BF16))
                       for i in range(2)]
                hsil = pe(nc.sbuf_tensor(f"hsil{call}", [128, T], F32))
                hbf = [pe(nc.sbuf_tensor(f"hbf{call}_{i}", [128, T], BF16))
                       for i in range(2)]
                wdT = [pe(nc.sbuf_tensor(f"wdT{call}_{i}", [128, GSZ, FCW], BF16))
                       for i in range(4)]
                hld = [pe(nc.sbuf_tensor(f"hld{call}_{i}", [128, T], BF16))
                       for i in range(5)]
                block = pe(nc.Block())

                xlnT = pe(nc.sbuf_tensor(f"xlnT{call}", [128, NHC, T], BF16))
                xt_ready = emit_ln_to_xlnT(src_for_ln, ln_mode,
                                           ("pe", kb.val("pe")), xlnT)

                hw_base = kb.val("hw")

                p_state = {"cnt": 0}
                wc_gu = {}
                wcd_done = {"v": 0}

                def produce_tile(qsrc, szsrc, cache, crow, ccol0, cw, wcfam):
                    p = p_state["cnt"]
                    p_state["cnt"] += 1
                    b = p % NBUF
                    dq_guard = ("dq", kb.val("dq") - (NBUF - 1))
                    qn = kb.emit("gpsimd", "dma_start", out=qnat[b][:, 0:cw],
                                 in_=qsrc[ts(crow, 128), ccol0:ccol0 + cw],
                                 waits=(dq_guard,), inc=f"qn{b}", inck=16)
                    ng = cw // 128
                    g0 = ccol0 // 128
                    sz = kb.emit("gpsimd", "dma_start", out=szb[b][:, 0:2 * ng],
                                 in_=szsrc[ts(crow, 128), 2 * g0:2 * (g0 + ng)],
                                 waits=(dq_guard,), inc=f"sz{b}", inck=16)
                    wc_guard = (f"wc{b}", kb.val(f"wc{b}"))
                    for g in range(ng):
                        w = ((f"qn{b}", qn), (f"sz{b}", sz), wc_guard) if g == 0 else ()
                        kb.emit("vector", "tensor_scalar",
                                out=wnat[b][:, ts(g, 128)],
                                in0=qnat[b][:, ts(g, 128)],
                                scalar1=szb[b][:, 2 * g + 1:2 * g + 2],
                                scalar2=szb[b][:, 2 * g:2 * g + 1],
                                op0=OP.subtract, op1=OP.mult,
                                waits=w, inc="dq" if g == ng - 1 else None)
                    v = kb.emit("gpsimd", "dma_start",
                                out=cache[ts(crow, 128), ccol0:ccol0 + cw],
                                in_=wnat[b][:, 0:cw],
                                waits=(("dq", kb.val("dq")),),
                                inc=f"wc{b}", inck=16)
                    return (f"wc{b}", v)

                def produce_gu(ic):
                    wg = [produce_tile(qwg, szg, wg_c, ic, c0, cw, "wcg")
                          for c0, cw in GCH]
                    wu = [produce_tile(qwu, szu, wu_c, ic, c0, cw, "wcu")
                          for c0, cw in GCH]
                    wc_gu[ic] = (wg, wu)

                def produce_down(j):
                    fc, cj = divmod(j, len(DCH))
                    c0, cw = DCH[cj]
                    produce_tile(qwd, szd, wd_c, fc, c0, cw, "wcd")
                    wcd_done["v"] = tuple((f"wc{i}", kb.val(f"wc{i}"))
                                          for i in range(NBUF))

                pe_mark = {}
                hw_val = {}
                down_j = {"j": 0}
                ND = NHC * len(DCH)

                def consume_ic(ic):
                    s = ic % 2
                    for m, bufs, cache, wci in (("g", wgT, wg_c, 0),
                                                ("u", wuT, wu_c, 1)):
                        buf = bufs[s]
                        w = [("pe", pe_mark.get(ic - 2, 0))]
                        if call == 1:
                            w.extend(wc_gu[ic][wci])
                        kb.emit("sync", "dma_start", out=buf[:, :, :],
                                in_=cache[ts(ic, 128), :], transpose=True,
                                waits=tuple(w), inc=f"wt{m}{s}", inck=16)
                    wt_w = ((f"wtg{s}", kb.val(f"wtg{s}")),
                            (f"wtu{s}", kb.val(f"wtu{s}")))
                    par = ic % 2
                    psg = PSB[2 * par]
                    psu = PSB[2 * par + 1]
                    sil_guard = ("se_sil", kb.val("se_sil") - 1)
                    mul_guard = ("ve_mul", kb.val("ve_mul") - 1)
                    gbuf = wgT[ic % 2]
                    ubuf = wuT[ic % 2]
                    halves = [(0, T)] if T <= 512 else [(0, TH), (TH, T)]
                    for hc in range(NHC):
                        w0 = wt_w + (("xt", xt_ready), sil_guard,
                              mul_guard) if hc == 0 else ()
                        for (h0, h1) in halves:
                            kb.emit("tensor", "matmul", psg[:, h0:h1],
                                    gbuf[:, hc, :], xlnT[:, hc, h0:h1],
                                    start=(hc == 0), stop=(hc == NHC - 1),
                                    waits=w0 if h0 == 0 else ())
                    for hc in range(NHC):
                        for (h0, h1) in halves:
                            kb.emit("tensor", "matmul", psu[:, h0:h1],
                                    ubuf[:, hc, :], xlnT[:, hc, h0:h1],
                                    start=(hc == 0), stop=(hc == NHC - 1),
                                    inc="pe" if (hc == NHC - 1 and h1 == T) else None)
                    pev = kb.val("pe")
                    hb = ic % 2
                    hw_guard = (f"hw{hb}", kb.val(f"hw{hb}"))
                    sv = kb.emit("scalar", "activation", out=hsil[:, :],
                                 in_=psg[:, 0:T], func=AF.Silu,
                                 waits=(("pe", pev), ("ve_mul", kb.val("ve_mul"))),
                                 inc="se_sil")
                    mv = kb.emit("vector", "tensor_tensor", out=hbf[hb][:, :],
                                 in0=hsil[:, :], in1=psu[:, 0:T], op=OP.mult,
                                 waits=(("se_sil", sv), hw_guard), inc="ve_mul")
                    hwv = kb.emit("scalar", "dma_start", out=hbuf[ic, :, :],
                                  in_=hbf[hb][:, :], waits=(("ve_mul", mv),),
                                  inc=f"hw{hb}", inck=16)
                    hw_val[ic] = (f"hw{hb}", hwv)
                    pe_mark[ic] = kb.val("pe")

                if call == 1:
                    cast_jobs = []
                    R3 = 3 * D
                    CR = max(128, R3 // 96)
                    for r0 in range(0, R3, CR):
                        rn = min(CR, R3 - r0)
                        cast_jobs.append((wqkv_c[r0:r0 + rn, :],
                                          wqkv[r0:r0 + rn, :], "wqc"))
                    CR2 = max(128, D // 16)
                    for r0 in range(0, D, CR2):
                        cast_jobs.append((wout_c[r0:r0 + CR2, :],
                                          wout[r0:r0 + CR2, :], "woc"))
                    cj = {"j": 0}

                    def emit_casts(frac):
                        njobs = min(len(cast_jobs),
                                    int(frac * len(cast_jobs)) + 1)
                        while cj["j"] < njobs:
                            o, i, sem = cast_jobs[cj["j"]]
                            kb.emit("gpsimd", "dma_start", out=o, in_=i,
                                    inc=sem, inck=16)
                            cj["j"] += 1

                    produce_gu(0)
                    produce_gu(1)
                    for ic in range(NI):
                        if ic + 2 < NI:
                            produce_gu(ic + 2)
                        nd = min(ND, (ic + 1) * ND // NI)
                        while down_j["j"] < nd:
                            produce_down(down_j["j"])
                            down_j["j"] += 1
                        consume_ic(ic)
                        emit_casts((ic + 1) / NI)
                else:
                    for ic in range(NI):
                        consume_ic(ic)

                hw_fin = (("hw0", kb.val("hw0")), ("hw1", kb.val("hw1")))
                pe_gu_done = kb.val("pe")

                groups = [(g0, min(GSZ, NI - g0)) for g0 in range(0, NI, GSZ)]
                gmark = {}
                gidx = 0
                hln = {"n": 0}
                for fc in range(FC_N):
                    ev_guard = ("ve_x", kb.val("ve_x"))
                    for gi, (g0, gn) in enumerate(groups):
                        b = gidx % 4
                        w = [("pe_d", gmark.get(gidx - 4, 0))]
                        if call == 1:
                            w.extend(wcd_done["v"])
                        kb.emit("sync", "dma_start", out=wdT[b][:, 0:gn, 0:FCW],
                                in_=wd_c[ts(fc, FCW), g0 * 128:(g0 + gn) * 128],
                                transpose=True, waits=tuple(w), inc=f"wtd{b}",
                                inck=16)
                        tl = (f"wtd{b}", kb.val(f"wtd{b}"))
                        for j in range(gn):
                            ic = g0 + j
                            hb = hln["n"] % 5
                            hln["n"] += 1
                            hl_guard = ("pe_d", kb.val("pe_d") - 4)
                            hneed = (hw_val[ic],) if (call == 1 and fc == 0) else hw_fin
                            hl = kb.emit("gpsimd", "dma_start",
                                         out=hld[hb][:, :], in_=hbuf[ic, :, :],
                                         waits=hneed + (hl_guard,),
                                         inc=f"hr{hb}", inck=16)
                            for tt in range(NT):
                                w0 = []
                                if tt == 0:
                                    w0 = [(f"hr{hb}", hl), tl]
                                    if ic == 0:
                                        w0.append(ev_guard)
                                kb.emit("tensor", "matmul",
                                        PSB[(tt * FCW) // 1024][:, (tt * FCW) % 1024:(tt * FCW) % 1024 + FCW],
                                        hld[hb][:, ts(tt, 128)],
                                        wdT[b][:, j, 0:FCW],
                                        start=(ic == 0), stop=(ic == NI - 1),
                                        waits=tuple(w0),
                                        inc="pe_d" if tt == NT - 1 else None)
                        gmark[gidx] = kb.val("pe_d")
                        gidx += 1
                    pedv = kb.val("pe_d")
                    for tt in range(NT):
                        b = tt % 2
                        xr = kb.emit("scalar", "dma_start", out=xres[b][:, 0:FCW],
                                     in_=x_res_src[ts(tt, 128), ts(fc, FCW)],
                                     waits=(("ve_x", kb.val("ve_x") - 1),),
                                     inc=f"xr{b}", inck=16)
                        yv = kb.emit("scalar", "activation", out=ytmp[b][:, 0:FCW],
                                     in_=PSB[(tt * FCW) // 1024][:, (tt * FCW) % 1024:(tt * FCW) % 1024 + FCW],
                                     func=AF.Copy, scale=0.5,
                                     waits=(("pe_d", pedv),
                                            ("ve_x", kb.val("ve_x") - 1)),
                                     inc="se")
                        xv = kb.emit("vector", "tensor_tensor", out=xo[b][:, 0:FCW],
                                     in0=ytmp[b][:, 0:FCW], in1=xres[b][:, 0:FCW],
                                     op=OP.add,
                                     waits=(("se", yv), (f"xr{b}", xr),
                                            (f"xs{b}", kb.val(f"xs{b}"))),
                                     inc="ve_x")
                        if collect_stats:
                            kb.emit("vector", "reduce_sum",
                                    stats[:, tt * 8 + fc:tt * 8 + fc + 1],
                                    xo[b][:, 0:FCW], axis=AX.X, inc="ve")
                            sq = kb.emit("scalar", "activation",
                                         out=xnrm[0][:, 0:FCW],
                                         in_=xo[b][:, 0:FCW], func=AF.Square,
                                         accum_out=stats[:, 64 + tt * 8 + fc:64 + tt * 8 + fc + 1],
                                         waits=(("ve_x", xv),), inc="se_st")
                            wv = (("se_st", sq),)
                        else:
                            wv = (("ve_x", xv),)
                        kb.emit("scalar", "dma_start",
                                out=x_dst[ts(tt, 128), ts(fc, FCW)],
                                in_=xo[b][:, 0:FCW], waits=wv, inc=f"xs{b}",
                                inck=16)

                xs_fin = (("xs0", kb.val("xs0")), ("xs1", kb.val("xs1")))
                if final_store:
                    kb.wait("gpsimd", *xs_fin)
                else:
                    barrier(*(xs_fin + (("pe_d", kb.val("pe_d")),
                            ("ve_x", kb.val("ve_x")),
                            ("hr0", kb.val("hr0")), ("hr1", kb.val("hr1")),
                            ("se_st" if collect_stats else "se",
                             kb.val("se_st") if collect_stats else kb.val("se")))))
                kb.play(block)
                return xs_fin


        NH_LOC = NHC // 4
        S = 4 * T
        NQB = S // 512
        QBR = max(1, T // 512)

        def emit_qkv_phase():
            with contextlib.ExitStack() as ps:
                pen = ps.enter_context
                wqT2 = [pen(nc.sbuf_tensor(f"wqT_{i}", [128, NHC, 512], BF16))
                        for i in range(2)]
                evq = [pen(nc.sbuf_tensor(f"evq_{i}", [128, 512], BF16))
                       for i in range(4)]
                ntmp = [pen(nc.sbuf_tensor(f"ntmp_{i}", [128, D], BF16))
                        for i in range(2)]
                hT = [pen(nc.sbuf_tensor(f"hT_{i}", [128, T], BF16))
                      for i in range(2)]
                hTa = [pen(nc.sbuf_tensor(f"hTa_{i}", [128, T], BF16))
                       for i in range(2)]
                lnp = pen(nc.sbuf_tensor("lnp", [128, 4 * NHC], F32))
                lnq = pen(nc.sbuf_tensor("lnq", [128, 2 * NHC], F32))
                block = pen(nc.Block())

                xlnT = pen(nc.sbuf_tensor("xlnTq", [128, NHC, T], BF16))
                xt2 = emit_ln_to_xlnT(x1b, "cols", ("pe", kb.val("pe")), xlnT)

                lnl = 0
                for i, srcp in enumerate((qlnw, qlnb, klnw, klnb)):
                    lnl = kb.emit("gpsimd", "dma_start",
                                  out=lnp[:, i * NHC:(i + 1) * NHC],
                                  in_=srcp, inc="lnl", inck=16)
                lq = kb.emit("vector", "tensor_scalar", out=lnq[:, :],
                             in0=lnp[:, 0:2 * NHC], scalar1=QK_SCALE,
                             scalar2=None, op0=OP.mult, op1=OP.bypass,
                             waits=(("lnl", lnl),), inc="ve")

                wq_total = ("wqc", kb.val("wqc"))
                segs = [("k", D, kt_in, "ccK"), ("v", 2 * D, vt_in, "ccV"),
                        ("q", 0, qt_in, "ccQ")]
                FCQ = D // 512
                qm_mark = {}
                gfc = {"i": 0}
                for sname, row0, dest, ccsem in segs:
                    cnt = 0
                    seg0 = (("nw0", kb.val("nw0")), ("nw1", kb.val("nw1")),
                            ("tp0", kb.val("tp0")), ("tp1", kb.val("tp1")),
                            ("xt", xt2))
                    for fc in range(FCQ):
                        gi_ = gfc["i"]
                        b = gi_ % 2
                        w = [wq_total, ("pe_q", qm_mark.get(gi_ - 2, 0))]
                        kb.emit("sync", "dma_start", out=wqT2[b][:, :, :],
                                in_=wqkv_c[row0 + fc * 512:row0 + fc * 512 + 512, :],
                                transpose=True, waits=tuple(w),
                                inc=f"wq{b}", inck=16)
                        wqv = (f"wq{b}", kb.val(f"wq{b}"))
                        for tt in range(NT):
                            par = cnt % 4
                            pss = PSB[par][:, 0:512]
                            evg = ("ve_q", kb.val("ve_q") - 3)
                            for hc in range(NHC):
                                w0 = (wqv, ("xt", xt2), evg) if hc == 0 else ()
                                kb.emit("tensor", "matmul", pss,
                                        xlnT[:, hc, ts(tt, 128)],
                                        wqT2[b][:, hc, :],
                                        start=(hc == 0), stop=(hc == NHC - 1),
                                        waits=w0,
                                        inc="pe_q" if hc == NHC - 1 else None)
                            pqv = kb.val("pe_q")
                            sg_g = (f"sg{par}", kb.val(f"sg{par}"))
                            ev = kb.emit("vector", "tensor_scalar",
                                         out=evq[par][:, :], in0=pss,
                                         scalar1=-8.0, scalar2=8.0,
                                         op0=OP.max, op1=OP.min,
                                         waits=(("pe_q", pqv), sg_g),
                                         inc="ve_q")
                            wv = (("ve_q", ev),)
                            if sname != "v":
                                kb.emit("vector", "reduce_sum",
                                        stats[:, tt * 8 + fc:tt * 8 + fc + 1],
                                        evq[par][:, :], axis=AX.X, inc="ve")
                                sq = kb.emit("scalar", "activation",
                                             out=ntmp[0][:, 0:512],
                                             in_=evq[par][:, :], func=AF.Square,
                                             accum_out=stats[:, 64 + tt * 8 + fc:64 + tt * 8 + fc + 1],
                                             waits=(("ve_q", ev),) +
                                             (seg0 if cnt == 0 else ()),
                                             inc="se_q")
                                wv = (("se_q", sq),)
                            kb.emit("gpsimd", "dma_start",
                                    out=segb[ts(tt, 128), ts(fc, 512)],
                                    in_=evq[par][:, :],
                                    waits=wv + (seg0 if cnt == 0 else ()),
                                    inc=f"sg{par}", inck=16)
                            cnt += 1
                        qm_mark[gi_] = kb.val("pe_q")
                        gfc["i"] += 1
                    sg_fin = tuple((f"sg{i}", kb.val(f"sg{i}"))
                                   for i in range(4))
                    if sname != "v":
                        for tt in range(NT):
                            bb = tt % 2
                            ld = kb.emit("gpsimd", "dma_start", out=xtile[:, :],
                                         in_=segb[ts(tt, 128), :],
                                         waits=sg_fin + (("se_ln", kb.val("se_ln")),),
                                         inc="xl", inck=16)
                            sm = stats[:, 184:185]
                            qm_ = stats[:, 185:186]
                            kb.emit("vector", "reduce_sum", sm,
                                    stats[:, tt * 8:tt * 8 + FCQ], axis=AX.X,
                                    waits=(("se_q", kb.val("se_q")),), inc="ve")
                            kb.emit("vector", "reduce_sum", qm_,
                                    stats[:, 64 + tt * 8:64 + tt * 8 + FCQ],
                                    axis=AX.X, inc="ve")
                            rdy, sc_col, bi_col = ln_finalize(tt, sm, qm_)
                            nw_g = (f"nw{bb}", kb.val(f"nw{bb}"))
                            nd = kb.emit("scalar", "activation",
                                         out=ntmp[bb][:, :], in_=xtile[:, :],
                                         func=AF.Identity, bias=bi_col,
                                         scale=sc_col,
                                         waits=(("ve", rdy), ("xl", ld), nw_g),
                                         inc="se_ln")
                            kb.emit("gpsimd", "dma_start",
                                    out=xlnb[ts(tt, 128), :], in_=ntmp[bb][:, :],
                                    waits=(("se_ln", nd),), inc=f"nw{bb}",
                                    inck=16)
                        nw_fin = (("nw0", kb.val("nw0")), ("nw1", kb.val("nw1")))
                        hsrc = xlnb
                    else:
                        nw_fin = sg_fin
                        hsrc = segb
                    for h in range(NHC):
                        hb = h % 2
                        if sname == "v":
                            kb.emit("gpsimd", "dma_start",
                                    out=dest[h // NH_LOC, h % NH_LOC, :, :],
                                    in_=segb[0:T, ts(h, 128)],
                                    waits=sg_fin if h == 0 else (),
                                    inc=f"sd{hb}", inck=16)
                            continue
                        tp_g = ("se_a", kb.val("se_a") - 1)
                        tp = kb.emit("sync", "dma_start", out=hT[hb][:, :],
                                     in_=hsrc[0:T, ts(h, 128)], transpose=True,
                                     waits=nw_fin + (tp_g,),
                                     inc=f"tp{hb}", inck=16)
                        tpv = (f"tp{hb}", kb.val(f"tp{hb}"))
                        if sname == "q":
                            sc_col = lnq[:, h:h + 1]
                            bi_col = lnq[:, NHC + h:NHC + h + 1]
                        else:
                            sc_col = lnp[:, 2 * NHC + h:2 * NHC + h + 1]
                            bi_col = lnp[:, 3 * NHC + h:3 * NHC + h + 1]
                        sd_g = (f"sd{hb}", kb.val(f"sd{hb}"))
                        av = kb.emit("scalar", "activation",
                                     out=hTa[hb][:, :], in_=hT[hb][:, :],
                                     func=AF.Identity, scale=sc_col,
                                     bias=bi_col,
                                     waits=(tpv, ("ve", lq), sd_g),
                                     inc="se_a")
                        kb.emit("gpsimd", "dma_start",
                                out=dest[h // NH_LOC, h % NH_LOC, :, :],
                                in_=hTa[hb][:, :], waits=(("se_a", av),),
                                inc=f"sd{hb}", inck=16)
                    sd_fin = (("sd0", kb.val("sd0")), ("sd1", kb.val("sd1")))
                    gathered = {"ccK": kt_out, "ccV": vt_out,
                                "ccQ": qt_out}[ccsem]
                    cmp_t = {"ccK": kt_cmp, "ccV": vt_cmp,
                             "ccQ": qt_cmp}[ccsem]
                    kb.emit("gpsimd", "collective_compute",
                            "AllGather", OP.bypass, replica_groups=RG,
                            ins=[dest.ap().opt()],
                            outs=[gathered.ap().opt()],
                            waits=sd_fin, inc=ccsem)
                    kb.emit("gpsimd", "dma_start", out=cmp_t,
                            in_=gathered.ap()
                            .rearrange("(s d) a b c -> s d a b c", d=4)[:, rkv],
                            waits=((ccsem, kb.val(ccsem)),),
                            inc=ccsem + "c", inck=16)
                barrier(("sd0", kb.val("sd0")), ("sd1", kb.val("sd1")),
                        ("tp0", kb.val("tp0")), ("tp1", kb.val("tp1")),
                        ("se_a", kb.val("se_a")),
                        ("pe_q", kb.val("pe_q")), ("ve_q", kb.val("ve_q")),
                        ("se_q", kb.val("se_q")))
                kb.play(block)

        def emit_attn_phase():
            with contextlib.ExitStack() as ps:
                pen = ps.enter_context
                Kh2 = [pen(nc.sbuf_tensor(f"Kh{i}", [128, S], BF16)) for i in range(2)]
                Qh2 = [pen(nc.sbuf_tensor(f"Qh{i}", [128, S], BF16)) for i in range(2)]
                Vh2 = [pen(nc.sbuf_tensor(f"Vh{i}", [128, S // 128, 128], BF16)) for i in range(2)]
                Pf2 = [pen(nc.sbuf_tensor(f"Pf{i}", [128, S], F32)) for i in range(2)]
                pfd = pen(nc.sbuf_tensor("pfd", [128, 512], F32))
                maskC = pen(nc.sbuf_tensor("maskC", [128, 4, 512], F32))
                identC = pen(nc.sbuf_tensor("identC", [128, 128], BF16))
                scor = pen(nc.sbuf_tensor("scor", [128, S], F32))
                prb = pen(nc.sbuf_tensor("prb", [128, 2, S], BF16))
                prT = [pen(nc.sbuf_tensor(f"prT_{i}", [128, 4, 256], BF16))
                       for i in range(2)]
                dg = pen(nc.sbuf_tensor("dg", [128, 2, 128], BF16))
                aosb = [pen(nc.sbuf_tensor(f"aosb_{i}", [128, 256], BF16))
                        for i in range(2)]
                block = pen(nc.Block())

                mk = kb.emit("sync", "dma_start", out=maskC[:, :, :],
                             in_=masksI.rearrange("q p f -> p q f"),
                             inc="mk", inck=16)
                mk = kb.emit("sync", "dma_start", out=identC[:, :], in_=ident,
                             inc="mk", inck=16)
                mkv = ("mk", kb.val("mk"))

                sc_cnt = 0
                pv_cnt = 0
                ao_cnt = 0
                head_mark = {}
                for hl in range(NH_LOC):
                    hb2 = hl % 2
                    Kh = Kh2[hb2]
                    Qh = Qh2[hb2]
                    Vh = Vh2[hb2]
                    Pf = Pf2[hb2]
                    hm = head_mark.get(hl - 2, {"pe_s": 0, "pe_pv": 0, "ve_s": 0})
                    kg = ("pe_s", hm["pe_s"])
                    vg = ("pe_pv", hm["pe_pv"])
                    pg = ("ve_s", hm["ve_s"])
                    for r in range(4):
                        kb.emit("gpsimd", "dma_start", out=Kh[:, ts(r, T)],
                                in_=kt_cmp[r, hl, :, :],
                                waits=(("ccKc", kb.val("ccKc")), kg),
                                inc="kh", inck=16)
                        kb.emit("gpsimd", "dma_start", out=Qh[:, ts(r, T)],
                                in_=qt_cmp[r, hl, :, :],
                                waits=(("ccQc", kb.val("ccQc")), kg),
                                inc="kh", inck=16)
                        kb.emit("gpsimd", "dma_start",
                                out=Vh[:, r * (T // 128):(r + 1) * (T // 128), :],
                                in_=vt_cmp[r, hl, :, :]
                                .rearrange("(c p) d -> p c d", p=128),
                                waits=(("ccVc", kb.val("ccVc")), vg),
                                inc="vh", inck=16)
                    pfl = kb.emit("gpsimd", "dma_start", out=Pf[:, :],
                                  in_=alibiI[hl, :, :],
                                  waits=(pg,), inc="pfl", inck=16)
                    khv = ("kh", kb.val("kh"))
                    vhv = ("vh", kb.val("vh"))
                    pflv = ("pfl", kb.val("pfl"))
                    for qb in range(NQB):
                        span = qb + 1
                        for qt in range(4):
                            qcol = qb * 512 + qt * 128
                            dv = kb.emit("vector", "tensor_tensor", out=pfd[:, :],
                                         in0=Pf[:, ts(qb, 512)],
                                         in1=maskC[:, qt, :], op=OP.add,
                                         waits=(pflv, mkv,
                                                ("ve_s", kb.val("ve_s"))),
                                         inc="ve")
                            exp_guard = ("se_e", kb.val("se_e"))
                            for kt in range(span):
                                par = sc_cnt % 2
                                pss = PSB[2][:, par * 512:par * 512 + 512]
                                evg = ("ve_s", kb.val("ve_s") - 1)
                                kb.emit("tensor", "matmul", pss,
                                        Qh[:, qcol:qcol + 128],
                                        Kh[:, ts(kt, 512)],
                                        start=True, stop=True,
                                        waits=(khv, evg), inc="pe_s")
                                psv = kb.val("pe_s")
                                bias_t = pfd[:, :] if kt == qb \
                                    else Pf[:, ts(kt, 512)]
                                bw = [("pe_s", psv)]
                                if kt == qb:
                                    bw.append(("ve", dv))
                                if kt == 0:
                                    bw.append(exp_guard)
                                kb.emit("vector", "tensor_tensor",
                                        out=scor[:, ts(kt, 512)], in0=pss,
                                        in1=bias_t, op=OP.add,
                                        waits=tuple(bw), inc="ve_s")
                                sc_cnt += 1
                            vsv = kb.val("ve_s")
                            kb.emit("vector", "reduce_max",
                                    stats[:, 192 + qt:193 + qt],
                                    scor[:, 0:span * 512], axis=AX.X,
                                    waits=(("ve_s", vsv),), inc="ve")
                            nmx = kb.emit("vector", "tensor_scalar",
                                          out=stats[:, 196 + qt:197 + qt],
                                          in0=stats[:, 192 + qt:193 + qt],
                                          scalar1=-1.0, scalar2=None,
                                          op0=OP.mult, op1=OP.bypass, inc="ve")
                            prb_g = ("pe_t", kb.val("pe_t"))
                            ex = kb.emit("scalar", "activation",
                                         out=prb[:, qt % 2, 0:span * 512],
                                         in_=scor[:, 0:span * 512], func=AF.Exp,
                                         bias=stats[:, 196 + qt:197 + qt],
                                         scale=1.0,
                                         accum_out=stats[:, 200 + qt:201 + qt],
                                         waits=(("ve", nmx), prb_g),
                                         inc="se_e")
                            rc = kb.emit("vector", "reciprocal",
                                         stats[:, 204 + qt:205 + qt],
                                         stats[:, 200 + qt:201 + qt],
                                         waits=(("se_e", ex),), inc="ve")
                            kb.emit("vector", "tensor_scalar",
                                    out=dg[:, qt % 2, :], in0=identC[:, :],
                                    scalar1=stats[:, 204 + qt:205 + qt],
                                    scalar2=None, op0=OP.mult, op1=OP.bypass,
                                    waits=(("ve", rc), mkv, prb_g), inc="ve_d")
                            if qt % 2 == 1:
                                dgv = ("ve_d", kb.val("ve_d"))
                                exv = ("se_e", kb.val("se_e"))
                                nkc = span * 4
                                ngr = span
                                pvo = PSB[1][:, (pv_cnt % 2) * 512:
                                             (pv_cnt % 2) * 512 + 256]
                                pv_evac_g = ("se_o", kb.val("se_o") - 1)

                                def tmm(gr):
                                    sl = gr % 2
                                    pst = (PSB[0], PSB[3])[sl]
                                    ev_g = ("se_p", kb.val("se_p") - 1)
                                    for jk in range(4):
                                        kc = gr * 4 + jk
                                        for j, qq in enumerate((qt - 1, qt)):
                                            kb.emit("tensor", "matmul",
                                                    pst[:, jk * 256 + j * 128:
                                                        jk * 256 + j * 128 + 128],
                                                    prb[:, qq % 2, ts(kc, 128)],
                                                    dg[:, qq % 2, :],
                                                    start=True, stop=True,
                                                    waits=(dgv, exv, ev_g)
                                                    if (jk == 0 and j == 0) else (),
                                                    inc="pe_t"
                                                    if (jk == 3 and j == 1) else None)
                                    pet = kb.val("pe_t")
                                    kb.emit("scalar", "copy",
                                            out=prT[sl][:, :, :], in_=pst[:, 0:1024],
                                            waits=(("pe_t", pet),
                                                   ("pe_pv", kb.val("pe_pv"))),
                                            inc="se_p")
                                    return ("se_p", kb.val("se_p"))

                                pmarks = {}
                                pmarks[0] = tmm(0)
                                for gr in range(ngr):
                                    if gr + 1 < ngr:
                                        pmarks[gr + 1] = tmm(gr + 1)
                                    for jk in range(4):
                                        kc = gr * 4 + jk
                                        w0 = [pmarks[gr], vhv] if jk == 0 else []
                                        if kc == 0:
                                            w0.append(pv_evac_g)
                                        kb.emit("tensor", "matmul", pvo,
                                                Vh[:, kc, :],
                                                prT[gr % 2][:, jk, :],
                                                start=(kc == 0),
                                                stop=(kc == nkc - 1),
                                                waits=tuple(w0), inc="pe_pv")
                                pvv = kb.val("pe_pv")
                                ab = ao_cnt % 2
                                ao_g = (f"ao{ab}", kb.val(f"ao{ab}"))
                                ov = kb.emit("scalar", "copy",
                                             out=aosb[ab][:, :], in_=pvo,
                                             waits=(("pe_pv", pvv), ao_g),
                                             inc="se_o")
                                pr = (qt - 1) // 2
                                kb.emit("gpsimd", "dma_start",
                                        out=ao_in[qb // QBR, hl, :,
                                                  (qb % QBR) * 512 + pr * 256:
                                                  (qb % QBR) * 512 + pr * 256 + 256],
                                        in_=aosb[ab][:, :],
                                        waits=(("se_o", ov),),
                                        inc=f"ao{ab}", inck=16)
                                ao_cnt += 1
                                pv_cnt += 1
                    head_mark[hl] = {"pe_s": kb.val("pe_s"),
                                     "pe_pv": kb.val("pe_pv"),
                                     "ve_s": kb.val("ve_s")}
                kb.emit("gpsimd", "collective_compute",
                        "AllGather", OP.bypass, replica_groups=RG,
                        ins=[ao_in.ap().opt()], outs=[ao_out.ap().opt()],
                        waits=(("ao0", kb.val("ao0")), ("ao1", kb.val("ao1"))),
                        inc="ccA")
                kb.emit("gpsimd", "dma_start", out=ao_cmp,
                        in_=ao_out.ap()
                        .rearrange("(s d) a b c -> s d a b c", d=4)[:, rkv],
                        waits=(("ccA", kb.val("ccA")),), inc="ccAc", inck=16)
                barrier(("ccA", kb.val("ccA")), ("se_o", kb.val("se_o")),
                        ("pe_pv", kb.val("pe_pv")))
                kb.play(block)

        def emit_wout_phase():
            with contextlib.ExitStack() as ps:
                pen = ps.enter_context
                woT = [pen(nc.sbuf_tensor("woT_0", [128, NHC, 512], BF16))]
                aot = pen(nc.sbuf_tensor("aot", [128, NHC, T], BF16))
                block = pen(nc.Block())

                for hc in range(NHC):
                    kb.emit("gpsimd", "dma_start", out=aot[:, hc, :],
                            in_=ao_cmp[hc // NH_LOC, hc % NH_LOC, :, :],
                            waits=(("ccAc", kb.val("ccAc")),), inc="aol",
                            inck=16)
                aolv = ("aol", kb.val("aol"))
                wo_total = ("woc", kb.val("woc"))
                cnt = 0
                wm_mark = {}
                for fc in range(D // 512):
                    b = 0
                    kb.emit("sync", "dma_start", out=woT[b][:, :, :],
                            in_=wout_c[ts(fc, 512), :], transpose=True,
                            waits=(wo_total, ("pe_w", wm_mark.get(fc - 1, 0))),
                            inc=f"wo{b}", inck=16)
                    wov = (f"wo{b}", kb.val(f"wo{b}"))
                    for tt in range(NT):
                        par = cnt % 2
                        pss = PSB[par][:, 0:512]
                        evg = ("ve_x", kb.val("ve_x") - 1)
                        for hc in range(NHC):
                            w0 = (wov, aolv, evg) if hc == 0 else ()
                            kb.emit("tensor", "matmul", pss,
                                    aot[:, hc, ts(tt, 128)], woT[b][:, hc, :],
                                    start=(hc == 0), stop=(hc == NHC - 1),
                                    waits=w0,
                                    inc="pe_w" if hc == NHC - 1 else None)
                        pwv = kb.val("pe_w")
                        xb = par
                        xr = kb.emit("scalar", "dma_start",
                                     out=xres[xb][:, :],
                                     in_=x1b[ts(tt, 128), ts(fc, 512)],
                                     waits=(("ve_x", kb.val("ve_x") - 1),),
                                     inc=f"xr{xb}", inck=16)
                        xv = kb.emit("vector", "tensor_tensor",
                                     out=xo[xb][:, :], in0=pss,
                                     in1=xres[xb][:, :], op=OP.add,
                                     waits=(("pe_w", pwv), (f"xr{xb}", xr),
                                            (f"xs{xb}", kb.val(f"xs{xb}"))),
                                     inc="ve_x")
                        kb.emit("vector", "reduce_sum",
                                stats[:, tt * 8 + fc:tt * 8 + fc + 1],
                                xo[xb][:, :], axis=AX.X, inc="ve")
                        sq = kb.emit("scalar", "activation",
                                     out=xnrm[0][:, 0:512], in_=xo[xb][:, :],
                                     func=AF.Square,
                                     accum_out=stats[:, 64 + tt * 8 + fc:64 + tt * 8 + fc + 1],
                                     waits=(("ve_x", xv),), inc="se_st")
                        kb.emit("scalar", "dma_start",
                                out=x2b[ts(tt, 128), ts(fc, 512)],
                                in_=xo[xb][:, :],
                                waits=(("se_st", kb.val("se_st")),),
                                inc=f"xs{xb}", inck=16)
                        cnt += 1
                    wm_mark[fc] = kb.val("pe_w")
                barrier(("xs0", kb.val("xs0")), ("xs1", kb.val("xs1")),
                        ("pe_w", kb.val("pe_w")), ("se_st", kb.val("se_st")))
                kb.play(block)

        xs_fin1 = emit_ffn_phase(1, x0, x1b, True, x0, "self",
                                 final_store=(stop_after == "ffn1"))

        if stop_after == "ffn1":
            with nc.Block() as blk:
                for tt in range(NT):
                    kb.emit("gpsimd", "dma_start", out=xoutD[ts(tt, 128), :],
                            in_=x1b[ts(tt, 128), :],
                            waits=xs_fin1, inc="fin", inck=16)
                kb.wait("gpsimd", ("fin", kb.val("fin")))
                kb.play(blk)
            return nc

        emit_qkv_phase()
        emit_attn_phase()
        emit_wout_phase()
        xs_fin2 = emit_ffn_phase(2, x2b, xoutD, False, x2b, "cols",
                                 final_store=True)
        with nc.Block() as blk:
            kb.wait("gpsimd", *xs_fin2)
            kb.play(blk)
    return nc



def _const_inputs():
    ident = np.eye(128, dtype=np.float32).astype(BF)
    masks = np.zeros((4, 128, 512), dtype=np.float32)
    for qt in range(4):
        d = qt * 128
        p = np.arange(128)[:, None]
        j = np.arange(512)[None, :]
        masks[qt] = np.where(j > d + p, NEG, 0.0).astype(np.float32)
    return ident, masks


def alibi_slopes(n_heads, bias_max=8.0):
    next_pow2 = 2 ** int(np.ceil(np.log2(n_heads)))
    m = np.arange(1, next_pow2 + 1, dtype=np.float32) * (bias_max / next_pow2)
    slopes = 1.0 / np.power(2.0, m)
    if next_pow2 != n_heads:
        slopes = np.concatenate([slopes[1::2], slopes[::2]])[:n_heads]
    return slopes.astype(np.float32)


def _alibi_for_core(core):
    nh = D // 128
    nh_loc = nh // 4
    h0 = nh_loc * (core % 4)
    sl = alibi_slopes(nh)[h0:h0 + nh_loc]
    kk = np.arange(4 * T, dtype=np.float32)
    a = sl[:, None] * kk[None, :]
    return np.broadcast_to(a[:, None, :], (nh_loc, 128, 4 * T)).copy()


def kernel(L0_attn_hidden_states, L0_attn_Wqkv, L0_attn_q_ln_w, L0_attn_q_ln_b,
           L0_attn_k_ln_w, L0_attn_k_ln_b, L0_attn_Wout, L0_ffn_qw_gate,
           L0_ffn_qw_up, L0_ffn_qw_down, L0_ffn_s_gate, L0_ffn_z_gate,
           L0_ffn_s_up, L0_ffn_z_up, L0_ffn_s_down, L0_ffn_z_down,
           _stop_after=None, _only_maps=False):
    x = np.ascontiguousarray(np.asarray(L0_attn_hidden_states, dtype=np.float32)
                             .reshape(8 * T, D))
    ident, masks = _const_inputs()

    def col128(v):
        return np.ascontiguousarray(
            np.asarray(v, np.float32).reshape(D // 128, 128).T)

    in_maps = _prepare_in_maps_inner(x, ident, masks, col128,
        L0_attn_Wqkv, L0_attn_Wout, L0_ffn_qw_gate, L0_ffn_qw_up,
        L0_ffn_qw_down, L0_ffn_s_gate, L0_ffn_z_gate, L0_ffn_s_up,
        L0_ffn_z_up, L0_ffn_s_down, L0_ffn_z_down,
        L0_attn_q_ln_w, L0_attn_q_ln_b, L0_attn_k_ln_w, L0_attn_k_ln_b)
    if _only_maps:
        return in_maps

    nc = build_nc(stop_after=_stop_after)
    res = run_bass_kernel_spmd(nc, in_maps, core_ids=list(range(8)))
    out = np.concatenate([res.results[c]["xout"] for c in range(8)], axis=0)
    return out.reshape(2, 4 * T, D).astype(np.float32)


def _ilv(s, z):
    s = np.asarray(s, np.float32)
    z = np.asarray(z, np.float32)
    return np.ascontiguousarray(np.stack([s, z], axis=2).reshape(s.shape[0], -1))


def _prepare_in_maps_inner(x, ident, masks, col128, Wqkv, Wout, qwg, qwu, qwd,
                           sg, zg, su, zu, sd, zd, qw_, qb_, kw_, kb_):
    common = dict(
        wqkv=np.asarray(Wqkv, np.float32),
        wout=np.asarray(Wout, np.float32),
        qwg=np.asarray(qwg, np.int32),
        qwu=np.asarray(qwu, np.int32),
        qwd=np.asarray(qwd, np.int32),
        szg=_ilv(sg, zg),
        szu=_ilv(su, zu),
        szd=_ilv(sd, zd),
        qlnw=col128(qw_), qlnb=col128(qb_),
        klnw=col128(kw_), klnb=col128(kb_),
        ident=ident, masks=masks,
    )
    in_maps = []
    for c in range(8):
        m = dict(common)
        m["x0"] = np.ascontiguousarray(x[c * T:(c + 1) * T])
        m["alibi"] = _alibi_for_core(c)
        in_maps.append(m)
    return in_maps


def _prepare_in_maps(**inputs):
    return kernel(**inputs, _only_maps=True)


# version 12
# speedup vs baseline: 1.1001x; 1.0125x over previous
import contextlib
import numpy as np
import ml_dtypes
import concourse.bass as bass
import concourse.mybir as mybir
from concourse.engine_type import EngineType
from concourse.bass_utils import run_bass_kernel_spmd

F32 = mybir.dt.float32
BF16 = mybir.dt.bfloat16
I32 = mybir.dt.int32
AF = mybir.ActivationFunctionType
OP = mybir.AluOpType
AX = mybir.AxisListType
BF = ml_dtypes.bfloat16
ts = bass.ts

D = 4096
IF = 11008
NI = IF // 128
T = 1024
NT = T // 128
NHC = D // 128
EPS = 1e-5
CHUNK = 2048
GSZ = 4
QK_SCALE = 1.0 / np.sqrt(128.0)
NEG = -1e30

ENGINES = ("sync", "scalar", "vector", "tensor", "gpsimd")


class KB:

    def __init__(self, nc, stack):
        self.nc = nc
        self.stack = stack
        self.ops = {e: [] for e in ENGINES}
        self.semh = {}
        self.semc = {}
        self.last_inc = {}

    def sem(self, name):
        if name not in self.semh:
            self.semh[name] = self.stack.enter_context(self.nc.semaphore(name))
            self.semc[name] = 0
        return self.semh[name]

    def inc(self, name, k=1):
        self.sem(name)
        self.semc[name] += k
        return self.semc[name]

    def val(self, name):
        return self.semc.get(name, 0)

    def emit(self, eng, method, *args, waits=(), inc=None, inck=1, **kw):
        rv = None
        ih = None
        waits = list(waits)
        is_dma = method == "dma_start"
        if eng in ("vector", "scalar") and not is_dma and eng in self.last_inc:
            waits.append(self.last_inc[eng])
        if inc is not None:
            rv = self.inc(inc, inck)
            ih = self.semh[inc]
            if eng in ("vector", "scalar") and not is_dma:
                self.last_inc[eng] = (inc, rv)
        seen = set()
        wl = []
        for (n, v) in waits:
            if v > 0 and (n, v) not in seen:
                seen.add((n, v))
                wl.append((self.sem(n), v))

        def f(E, method=method, args=args, kw=kw, wl=wl, ih=ih, inck=inck):
            for (h, v) in wl:
                E.wait_ge(h, v)
            r = getattr(E, method)(*args, **kw)
            if ih is not None:
                r.then_inc(ih, inck)

        self.ops[eng].append(f)
        return rv

    def wait(self, eng, *waits):
        wl = [(self.sem(n), v) for (n, v) in waits if v > 0]

        def f(E, wl=wl):
            for (h, v) in wl:
                E.wait_ge(h, v)

        self.ops[eng].append(f)

    def play(self, block):
        for e in ENGINES:
            dec = getattr(block, e)
            ops = self.ops[e]

            def body(E, ops=ops):
                for f in ops:
                    f(E)

            dec(body)
        self.ops = {e: [] for e in ENGINES}


def build_nc(stop_after=None):
    NI = IF // 128
    NT = T // 128
    NHC = D // 128
    TH = T // 2
    FCW = min(512, D, T)
    FC_N = D // FCW
    GCH = [(c0, min(CHUNK, D - c0)) for c0 in range(0, D, CHUNK)]
    DCH = [(c0, min(CHUNK, IF - c0)) for c0 in range(0, IF, CHUNK)]
    nc = bass.Bass("TRN2", target_bir_lowering=False)

    x0 = nc.dram_tensor("x0", [T, D], F32, kind="ExternalInput").ap()
    wqkv = nc.dram_tensor("wqkv", [3 * D, D], F32, kind="ExternalInput").ap()
    wout = nc.dram_tensor("wout", [D, D], F32, kind="ExternalInput").ap()
    qwg = nc.dram_tensor("qwg", [IF, D], I32, kind="ExternalInput").ap()
    qwu = nc.dram_tensor("qwu", [IF, D], I32, kind="ExternalInput").ap()
    qwd = nc.dram_tensor("qwd", [D, IF], I32, kind="ExternalInput").ap()
    szg = nc.dram_tensor("szg", [IF, 64], F32, kind="ExternalInput").ap()
    szu = nc.dram_tensor("szu", [IF, 64], F32, kind="ExternalInput").ap()
    szd = nc.dram_tensor("szd", [D, 172], F32, kind="ExternalInput").ap()
    qlnw = nc.dram_tensor("qlnw", [128, NHC], F32, kind="ExternalInput").ap()
    qlnb = nc.dram_tensor("qlnb", [128, NHC], F32, kind="ExternalInput").ap()
    klnw = nc.dram_tensor("klnw", [128, NHC], F32, kind="ExternalInput").ap()
    klnb = nc.dram_tensor("klnb", [128, NHC], F32, kind="ExternalInput").ap()
    ident = nc.dram_tensor("ident", [128, 128], BF16, kind="ExternalInput").ap()
    masksI = nc.dram_tensor("masks", [4, 128, 512], F32, kind="ExternalInput").ap()
    alibiI = nc.dram_tensor("alibi", [NHC // 4, 128, 4 * T], F32,
                            kind="ExternalInput").ap()

    xoutD = nc.dram_tensor("xout", [T, D], F32, kind="ExternalOutput").ap()

    wg_c = nc.dram_tensor("wg_c", [IF, D], BF16, kind="Internal").ap()
    wu_c = nc.dram_tensor("wu_c", [IF, D], BF16, kind="Internal").ap()
    wd_c = nc.dram_tensor("wd_c", [D, IF], BF16, kind="Internal").ap()
    wqkv_c = nc.dram_tensor("wqkv_c", [3 * D, D], BF16, kind="Internal").ap()
    wout_c = nc.dram_tensor("wout_c", [D, D], BF16, kind="Internal").ap()
    x1b = nc.dram_tensor("x1b", [T, D], F32, kind="Internal").ap()
    x2b = nc.dram_tensor("x2b", [T, D], F32, kind="Internal").ap()
    xlnb = nc.dram_tensor("xlnb", [T, D], BF16, kind="Internal").ap()
    hbuf = nc.dram_tensor("hbuf", [NI, 128, T], BF16, kind="Internal").ap()
    segb = nc.dram_tensor("segb", [T, D], BF16, kind="Internal").ap()
    qt_in = nc.dram_tensor("qt_in", [4, NHC // 4, 128, T], BF16, kind="Internal")
    kt_in = nc.dram_tensor("kt_in", [4, NHC // 4, 128, T], BF16, kind="Internal")
    vt_in = nc.dram_tensor("vt_in", [4, NHC // 4, T, 128], BF16, kind="Internal")
    qt_out = nc.dram_tensor("qt_out", [16, NHC // 4, 128, T], BF16, kind="Internal")
    kt_out = nc.dram_tensor("kt_out", [16, NHC // 4, 128, T], BF16, kind="Internal")
    vt_out = nc.dram_tensor("vt_out", [16, NHC // 4, T, 128], BF16, kind="Internal")
    ao_in = nc.dram_tensor("ao_in", [4, NHC // 4, 128, T], BF16, kind="Internal")
    ao_out = nc.dram_tensor("ao_out", [16, NHC // 4, 128, T], BF16, kind="Internal")
    qt_cmp = nc.dram_tensor("qt_cmp", [4, NHC // 4, 128, T], BF16, kind="Internal").ap()
    kt_cmp = nc.dram_tensor("kt_cmp", [4, NHC // 4, 128, T], BF16, kind="Internal").ap()
    vt_cmp = nc.dram_tensor("vt_cmp", [4, NHC // 4, T, 128], BF16, kind="Internal").ap()
    ao_cmp = nc.dram_tensor("ao_cmp", [4, NHC // 4, 128, T], BF16, kind="Internal").ap()

    RG = [[0, 1, 2, 3], [4, 5, 6, 7]]
    pid = nc.partition_id([EngineType.Pool, EngineType.SP])
    rkv = pid % 4
    RIDX = [rkv + 4 * r for r in range(4)]

    with contextlib.ExitStack() as stack:
        en = stack.enter_context
        xtile = en(nc.sbuf_tensor("xtile", [128, D], F32))
        xnrm = [en(nc.sbuf_tensor("xnrm0", [128, D], BF16))]
        xres = [en(nc.sbuf_tensor(f"xres{i}", [128, 512], F32)) for i in range(2)]
        ytmp = [en(nc.sbuf_tensor(f"ytmp{i}", [128, 512], F32)) for i in range(2)]
        xo = [en(nc.sbuf_tensor(f"xo{i}", [128, 512], F32)) for i in range(2)]
        stats = en(nc.sbuf_tensor("stats", [128, 224], F32))
        PSB = [en(nc.psum_tensor(f"psb{i}", [128, 1024], F32)) for i in range(4)]
        kb = KB(nc, stack)

        def barrier(*finals):
            kb.wait("gpsimd", *finals)
            phv = kb.emit("gpsimd", "memset", stats[0:1, 220:221], 0.0, inc="ph")
            for e in ("sync", "scalar", "vector", "tensor"):
                kb.wait(e, ("ph", phv))
            return phv

        def ln_finalize(tt, s_col, q_col):
            base = 128 + 4 * tt
            mu = stats[:, base:base + 1]
            mu2 = stats[:, base + 1:base + 2]
            var = stats[:, base + 2:base + 3]
            sd = stats[:, base + 3:base + 4]
            sc = stats[:, 168 + tt:168 + tt + 1]
            bi = stats[:, 176 + tt:176 + tt + 1]
            kb.emit("vector", "tensor_scalar", out=mu, in0=s_col, scalar1=1.0 / D,
                    scalar2=None, op0=OP.mult, op1=OP.bypass, inc="ve")
            kb.emit("vector", "tensor_tensor", out=mu2, in0=mu, in1=mu, op=OP.mult,
                    inc="ve")
            kb.emit("vector", "tensor_scalar", out=var, in0=q_col, scalar1=1.0 / D,
                    scalar2=None, op0=OP.mult, op1=OP.bypass, inc="ve")
            kb.emit("vector", "tensor_tensor", out=var, in0=var, in1=mu2,
                    op=OP.subtract, inc="ve")
            ve_done = kb.emit("vector", "tensor_scalar", out=var, in0=var,
                              scalar1=EPS, scalar2=None, op0=OP.add,
                              op1=OP.bypass, inc="ve")
            sq_done = kb.emit("scalar", "activation", out=sd, in_=var, func=AF.Sqrt,
                              waits=(("ve", ve_done),), inc="se")
            kb.emit("vector", "reciprocal", sc, sd, waits=(("se", sq_done),),
                    inc="ve")
            kb.emit("vector", "tensor_tensor", out=bi, in0=mu, in1=sc, op=OP.mult,
                    inc="ve")
            bi_done = kb.emit("vector", "tensor_scalar", out=bi, in0=bi,
                              scalar1=-1.0, scalar2=None, op0=OP.mult,
                              op1=OP.bypass, inc="ve")
            return bi_done, sc, bi

        def emit_ln_to_xlnT(src, stats_mode, xlnT_guard, xlnT):
            for tt in range(NT):
                b = 0
                ld = kb.emit("gpsimd", "dma_start", out=xtile[:, :],
                             in_=src[ts(tt, 128), :],
                             waits=(("se_ln", kb.val("se_ln")),),
                             inc="xl", inck=16)
                xw_guard = (f"xw{b}", kb.val(f"xw{b}"))
                if stats_mode == "self":
                    kb.emit("vector", "reduce_sum", stats[:, tt:tt + 1],
                            xtile[:, :], axis=AX.X, waits=(("xl", ld),),
                            inc="ve")
                    qm = kb.emit("scalar", "activation", out=xnrm[b][:, :],
                                 in_=xtile[:, :], func=AF.Square,
                                 accum_out=stats[:, 64 + tt:64 + tt + 1],
                                 waits=(("xl", ld), xw_guard),
                                 inc="se")
                    kb.wait("vector", ("se", qm))
                    rdy, sc_col, bi_col = ln_finalize(tt, stats[:, tt:tt + 1],
                                                      stats[:, 64 + tt:64 + tt + 1])
                else:
                    sm = stats[:, 184:185]
                    qm_ = stats[:, 185:186]
                    kb.emit("vector", "reduce_sum", sm,
                            stats[:, tt * 8:tt * 8 + FC_N], axis=AX.X, inc="ve")
                    kb.emit("vector", "reduce_sum", qm_,
                            stats[:, 64 + tt * 8:64 + tt * 8 + FC_N], axis=AX.X,
                            inc="ve")
                    rdy, sc_col, bi_col = ln_finalize(tt, sm, qm_)
                nd = kb.emit("scalar", "activation", out=xnrm[b][:, :],
                             in_=xtile[:, :], func=AF.Identity,
                             bias=bi_col, scale=sc_col,
                             waits=(("ve", rdy), ("xl", ld), xw_guard),
                             inc="se_ln")
                kb.emit("gpsimd", "dma_start", out=xlnb[ts(tt, 128), :],
                        in_=xnrm[b][:, :], waits=(("se_ln", nd),),
                        inc=f"xw{b}", inck=16)
            xw_fin = (("xw0", kb.val("xw0")), ("xw1", kb.val("xw1")))
            for half in range(2):
                kb.emit("sync", "dma_start", out=xlnT[:, :, ts(half, T // 2)],
                        in_=xlnb[ts(half, T // 2), :], transpose=True,
                        waits=xw_fin + (xlnT_guard,),
                        inc="xt", inck=16)
            return kb.val("xt")

        def emit_ffn_phase(call, x_res_src, x_dst, collect_stats, src_for_ln,
                           ln_mode, final_store=False):
            with contextlib.ExitStack() as ps:
                pe = ps.enter_context
                NBUF = 4
                qnat = [pe(nc.sbuf_tensor(f"qnat{call}_{i}", [128, min(CHUNK, max(D, IF))], BF16))
                        for i in range(NBUF)]
                wnat = [pe(nc.sbuf_tensor(f"wnat{call}_{i}", [128, min(CHUNK, max(D, IF))], BF16))
                        for i in range(NBUF)]
                szb = [pe(nc.sbuf_tensor(f"szb{call}_{i}", [128, 64], F32))
                       for i in range(NBUF)]
                wgT = [pe(nc.sbuf_tensor(f"wgT{call}_{i}", [128, NHC, 128], BF16))
                       for i in range(2)]
                wuT = [pe(nc.sbuf_tensor(f"wuT{call}_{i}", [128, NHC, 128], BF16))
                       for i in range(2)]
                hsil = pe(nc.sbuf_tensor(f"hsil{call}", [128, T], F32))
                hbf = [pe(nc.sbuf_tensor(f"hbf{call}_{i}", [128, T], BF16))
                       for i in range(2)]
                wdT = [pe(nc.sbuf_tensor(f"wdT{call}_{i}", [128, GSZ, FCW], BF16))
                       for i in range(4)]
                hld = [pe(nc.sbuf_tensor(f"hld{call}_{i}", [128, T], BF16))
                       for i in range(5)]
                block = pe(nc.Block())

                xlnT = pe(nc.sbuf_tensor(f"xlnT{call}", [128, NHC, T], BF16))
                xt_ready = emit_ln_to_xlnT(src_for_ln, ln_mode,
                                           ("pe", kb.val("pe")), xlnT)

                hw_base = kb.val("hw")

                p_state = {"cnt": 0}
                wc_gu = {}
                wcd_done = {"v": 0}

                def produce_tile(qsrc, szsrc, cache, crow, ccol0, cw, wcfam):
                    p = p_state["cnt"]
                    p_state["cnt"] += 1
                    b = p % NBUF
                    dq_guard = ("dq", kb.val("dq") - (NBUF - 1))
                    qn = kb.emit("gpsimd", "dma_start", out=qnat[b][:, 0:cw],
                                 in_=qsrc[ts(crow, 128), ccol0:ccol0 + cw],
                                 waits=(dq_guard,), inc=f"qn{b}", inck=16)
                    ng = cw // 128
                    g0 = ccol0 // 128
                    sz = kb.emit("gpsimd", "dma_start", out=szb[b][:, 0:2 * ng],
                                 in_=szsrc[ts(crow, 128), 2 * g0:2 * (g0 + ng)],
                                 waits=(dq_guard,), inc=f"sz{b}", inck=16)
                    wc_guard = (f"wc{b}", kb.val(f"wc{b}"))
                    for g in range(ng):
                        w = ((f"qn{b}", qn), (f"sz{b}", sz), wc_guard) if g == 0 else ()
                        kb.emit("vector", "tensor_scalar",
                                out=wnat[b][:, ts(g, 128)],
                                in0=qnat[b][:, ts(g, 128)],
                                scalar1=szb[b][:, 2 * g + 1:2 * g + 2],
                                scalar2=szb[b][:, 2 * g:2 * g + 1],
                                op0=OP.subtract, op1=OP.mult,
                                waits=w, inc="dq" if g == ng - 1 else None)
                    v = kb.emit("gpsimd", "dma_start",
                                out=cache[ts(crow, 128), ccol0:ccol0 + cw],
                                in_=wnat[b][:, 0:cw],
                                waits=(("dq", kb.val("dq")),),
                                inc=f"wc{b}", inck=16)
                    return (f"wc{b}", v)

                def produce_gu(ic):
                    wg = [produce_tile(qwg, szg, wg_c, ic, c0, cw, "wcg")
                          for c0, cw in GCH]
                    wu = [produce_tile(qwu, szu, wu_c, ic, c0, cw, "wcu")
                          for c0, cw in GCH]
                    wc_gu[ic] = (wg, wu)

                def produce_down(j):
                    fc, cj = divmod(j, len(DCH))
                    c0, cw = DCH[cj]
                    produce_tile(qwd, szd, wd_c, fc, c0, cw, "wcd")
                    wcd_done["v"] = tuple((f"wc{i}", kb.val(f"wc{i}"))
                                          for i in range(NBUF))

                pe_mark = {}
                hw_val = {}
                down_j = {"j": 0}
                ND = NHC * len(DCH)

                KLAG = 4

                def consume_ic(ic):
                    s = ic % 2
                    for m, bufs, cache, wci in (("g", wgT, wg_c, 0),
                                                ("u", wuT, wu_c, 1)):
                        buf = bufs[s]
                        w = [("pe", pe_mark.get(ic - 2, 0))]
                        if call == 1:
                            w.extend(wc_gu[min(ic + KLAG, NI - 1)][wci])
                        kb.emit("sync", "dma_start", out=buf[:, :, :],
                                in_=cache[ts(ic, 128), :], transpose=True,
                                waits=tuple(w), inc=f"wt{m}{s}", inck=16)
                    wt_w = ((f"wtg{s}", kb.val(f"wtg{s}")),
                            (f"wtu{s}", kb.val(f"wtu{s}")))
                    par = ic % 2
                    psg = PSB[2 * par]
                    psu = PSB[2 * par + 1]
                    sil_guard = ("se_sil", kb.val("se_sil") - 1)
                    mul_guard = ("ve_mul", kb.val("ve_mul") - 1)
                    gbuf = wgT[ic % 2]
                    ubuf = wuT[ic % 2]
                    halves = [(0, T)] if T <= 512 else [(0, TH), (TH, T)]
                    for hc in range(NHC):
                        w0 = wt_w + (("xt", xt_ready), sil_guard,
                              mul_guard) if hc == 0 else ()
                        for (h0, h1) in halves:
                            kb.emit("tensor", "matmul", psg[:, h0:h1],
                                    gbuf[:, hc, :], xlnT[:, hc, h0:h1],
                                    start=(hc == 0), stop=(hc == NHC - 1),
                                    waits=w0 if h0 == 0 else ())
                    for hc in range(NHC):
                        for (h0, h1) in halves:
                            kb.emit("tensor", "matmul", psu[:, h0:h1],
                                    ubuf[:, hc, :], xlnT[:, hc, h0:h1],
                                    start=(hc == 0), stop=(hc == NHC - 1),
                                    inc="pe" if (hc == NHC - 1 and h1 == T) else None)
                    pev = kb.val("pe")
                    hb = ic % 2
                    hw_guard = (f"hw{hb}", kb.val(f"hw{hb}"))
                    sv = kb.emit("scalar", "activation", out=hsil[:, :],
                                 in_=psg[:, 0:T], func=AF.Silu,
                                 waits=(("pe", pev), ("ve_mul", kb.val("ve_mul"))),
                                 inc="se_sil")
                    mv = kb.emit("vector", "tensor_tensor", out=hbf[hb][:, :],
                                 in0=hsil[:, :], in1=psu[:, 0:T], op=OP.mult,
                                 waits=(("se_sil", sv), hw_guard), inc="ve_mul")
                    hwv = kb.emit("scalar", "dma_start", out=hbuf[ic, :, :],
                                  in_=hbf[hb][:, :], waits=(("ve_mul", mv),),
                                  inc=f"hw{hb}", inck=16)
                    hw_val[ic] = (f"hw{hb}", hwv)
                    pe_mark[ic] = kb.val("pe")

                if call == 1:
                    cast_jobs = []
                    R3 = 3 * D
                    CR = max(128, R3 // 96)
                    for r0 in range(0, R3, CR):
                        rn = min(CR, R3 - r0)
                        cast_jobs.append((wqkv_c[r0:r0 + rn, :],
                                          wqkv[r0:r0 + rn, :], "wqc"))
                    CR2 = max(128, D // 16)
                    for r0 in range(0, D, CR2):
                        cast_jobs.append((wout_c[r0:r0 + CR2, :],
                                          wout[r0:r0 + CR2, :], "woc"))
                    cj = {"j": 0}

                    def emit_casts(frac):
                        njobs = min(len(cast_jobs),
                                    int(frac * len(cast_jobs)) + 1)
                        while cj["j"] < njobs:
                            o, i, sem = cast_jobs[cj["j"]]
                            kb.emit("gpsimd", "dma_start", out=o, in_=i,
                                    inc=sem, inck=16)
                            cj["j"] += 1

                    for ic0 in range(KLAG + 2):
                        produce_gu(ic0)
                    for ic in range(NI):
                        if ic + KLAG + 2 < NI:
                            produce_gu(ic + KLAG + 2)
                        nd = min(ND, (ic + 1) * ND // NI)
                        while down_j["j"] < nd:
                            produce_down(down_j["j"])
                            down_j["j"] += 1
                        consume_ic(ic)
                        emit_casts((ic + 1) / NI)
                else:
                    for ic in range(NI):
                        consume_ic(ic)

                hw_fin = (("hw0", kb.val("hw0")), ("hw1", kb.val("hw1")))
                pe_gu_done = kb.val("pe")

                groups = [(g0, min(GSZ, NI - g0)) for g0 in range(0, NI, GSZ)]
                gmark = {}
                gidx = 0
                hln = {"n": 0}
                for fc in range(FC_N):
                    ev_guard = ("ve_x", kb.val("ve_x"))
                    for gi, (g0, gn) in enumerate(groups):
                        b = gidx % 4
                        w = [("pe_d", gmark.get(gidx - 4, 0))]
                        if call == 1:
                            w.extend(wcd_done["v"])
                        kb.emit("sync", "dma_start", out=wdT[b][:, 0:gn, 0:FCW],
                                in_=wd_c[ts(fc, FCW), g0 * 128:(g0 + gn) * 128],
                                transpose=True, waits=tuple(w), inc=f"wtd{b}",
                                inck=16)
                        tl = (f"wtd{b}", kb.val(f"wtd{b}"))
                        for j in range(gn):
                            ic = g0 + j
                            hb = hln["n"] % 5
                            hln["n"] += 1
                            hl_guard = ("pe_d", kb.val("pe_d") - 4)
                            hneed = (hw_val[ic],) if (call == 1 and fc == 0) else hw_fin
                            hl = kb.emit("gpsimd", "dma_start",
                                         out=hld[hb][:, :], in_=hbuf[ic, :, :],
                                         waits=hneed + (hl_guard,),
                                         inc=f"hr{hb}", inck=16)
                            for tt in range(NT):
                                w0 = []
                                if tt == 0:
                                    w0 = [(f"hr{hb}", hl), tl]
                                    if ic == 0:
                                        w0.append(ev_guard)
                                kb.emit("tensor", "matmul",
                                        PSB[(tt * FCW) // 1024][:, (tt * FCW) % 1024:(tt * FCW) % 1024 + FCW],
                                        hld[hb][:, ts(tt, 128)],
                                        wdT[b][:, j, 0:FCW],
                                        start=(ic == 0), stop=(ic == NI - 1),
                                        waits=tuple(w0),
                                        inc="pe_d" if tt == NT - 1 else None)
                        gmark[gidx] = kb.val("pe_d")
                        gidx += 1
                    pedv = kb.val("pe_d")
                    for tt in range(NT):
                        b = tt % 2
                        xr = kb.emit("scalar", "dma_start", out=xres[b][:, 0:FCW],
                                     in_=x_res_src[ts(tt, 128), ts(fc, FCW)],
                                     waits=(("ve_x", kb.val("ve_x") - 1),),
                                     inc=f"xr{b}", inck=16)
                        yv = kb.emit("scalar", "activation", out=ytmp[b][:, 0:FCW],
                                     in_=PSB[(tt * FCW) // 1024][:, (tt * FCW) % 1024:(tt * FCW) % 1024 + FCW],
                                     func=AF.Copy, scale=0.5,
                                     waits=(("pe_d", pedv),
                                            ("ve_x", kb.val("ve_x") - 1)),
                                     inc="se")
                        xv = kb.emit("vector", "tensor_tensor", out=xo[b][:, 0:FCW],
                                     in0=ytmp[b][:, 0:FCW], in1=xres[b][:, 0:FCW],
                                     op=OP.add,
                                     waits=(("se", yv), (f"xr{b}", xr),
                                            (f"xs{b}", kb.val(f"xs{b}"))),
                                     inc="ve_x")
                        if collect_stats:
                            kb.emit("vector", "reduce_sum",
                                    stats[:, tt * 8 + fc:tt * 8 + fc + 1],
                                    xo[b][:, 0:FCW], axis=AX.X, inc="ve")
                            sq = kb.emit("scalar", "activation",
                                         out=xnrm[0][:, 0:FCW],
                                         in_=xo[b][:, 0:FCW], func=AF.Square,
                                         accum_out=stats[:, 64 + tt * 8 + fc:64 + tt * 8 + fc + 1],
                                         waits=(("ve_x", xv),), inc="se_st")
                            wv = (("se_st", sq),)
                        else:
                            wv = (("ve_x", xv),)
                        kb.emit("scalar", "dma_start",
                                out=x_dst[ts(tt, 128), ts(fc, FCW)],
                                in_=xo[b][:, 0:FCW], waits=wv, inc=f"xs{b}",
                                inck=16)

                xs_fin = (("xs0", kb.val("xs0")), ("xs1", kb.val("xs1")))
                if final_store:
                    kb.wait("gpsimd", *xs_fin)
                else:
                    barrier(*(xs_fin + (("pe_d", kb.val("pe_d")),
                            ("ve_x", kb.val("ve_x")),
                            ("hr0", kb.val("hr0")), ("hr1", kb.val("hr1")),
                            ("se_st" if collect_stats else "se",
                             kb.val("se_st") if collect_stats else kb.val("se")))))
                kb.play(block)
                return xs_fin


        NH_LOC = NHC // 4
        S = 4 * T
        NQB = S // 512
        QBR = max(1, T // 512)

        def emit_qkv_phase():
            with contextlib.ExitStack() as ps:
                pen = ps.enter_context
                wqT2 = [pen(nc.sbuf_tensor(f"wqT_{i}", [128, NHC, 512], BF16))
                        for i in range(2)]
                evq = [pen(nc.sbuf_tensor(f"evq_{i}", [128, 512], BF16))
                       for i in range(4)]
                ntmp = [pen(nc.sbuf_tensor(f"ntmp_{i}", [128, D], BF16))
                        for i in range(2)]
                hT = [pen(nc.sbuf_tensor(f"hT_{i}", [128, T], BF16))
                      for i in range(2)]
                hTa = [pen(nc.sbuf_tensor(f"hTa_{i}", [128, T], BF16))
                       for i in range(2)]
                lnp = pen(nc.sbuf_tensor("lnp", [128, 4 * NHC], F32))
                lnq = pen(nc.sbuf_tensor("lnq", [128, 2 * NHC], F32))
                block = pen(nc.Block())

                xlnT = pen(nc.sbuf_tensor("xlnTq", [128, NHC, T], BF16))
                xt2 = emit_ln_to_xlnT(x1b, "cols", ("pe", kb.val("pe")), xlnT)

                lnl = 0
                for i, srcp in enumerate((qlnw, qlnb, klnw, klnb)):
                    lnl = kb.emit("gpsimd", "dma_start",
                                  out=lnp[:, i * NHC:(i + 1) * NHC],
                                  in_=srcp, inc="lnl", inck=16)
                lq = kb.emit("vector", "tensor_scalar", out=lnq[:, :],
                             in0=lnp[:, 0:2 * NHC], scalar1=QK_SCALE,
                             scalar2=None, op0=OP.mult, op1=OP.bypass,
                             waits=(("lnl", lnl),), inc="ve")

                wq_total = ("wqc", kb.val("wqc"))
                segs = [("k", D, kt_in, "ccK"), ("v", 2 * D, vt_in, "ccV"),
                        ("q", 0, qt_in, "ccQ")]
                FCQ = D // 512
                qm_mark = {}
                gfc = {"i": 0}
                for sname, row0, dest, ccsem in segs:
                    cnt = 0
                    seg0 = (("nw0", kb.val("nw0")), ("nw1", kb.val("nw1")),
                            ("tp0", kb.val("tp0")), ("tp1", kb.val("tp1")),
                            ("xt", xt2))
                    for fc in range(FCQ):
                        gi_ = gfc["i"]
                        b = gi_ % 2
                        w = [wq_total, ("pe_q", qm_mark.get(gi_ - 2, 0))]
                        kb.emit("sync", "dma_start", out=wqT2[b][:, :, :],
                                in_=wqkv_c[row0 + fc * 512:row0 + fc * 512 + 512, :],
                                transpose=True, waits=tuple(w),
                                inc=f"wq{b}", inck=16)
                        wqv = (f"wq{b}", kb.val(f"wq{b}"))
                        for tt in range(NT):
                            par = cnt % 4
                            pss = PSB[par][:, 0:512]
                            evg = ("ve_q", kb.val("ve_q") - 3)
                            for hc in range(NHC):
                                w0 = (wqv, ("xt", xt2), evg) if hc == 0 else ()
                                kb.emit("tensor", "matmul", pss,
                                        xlnT[:, hc, ts(tt, 128)],
                                        wqT2[b][:, hc, :],
                                        start=(hc == 0), stop=(hc == NHC - 1),
                                        waits=w0,
                                        inc="pe_q" if hc == NHC - 1 else None)
                            pqv = kb.val("pe_q")
                            sg_g = (f"sg{par}", kb.val(f"sg{par}"))
                            ev = kb.emit("vector", "tensor_scalar",
                                         out=evq[par][:, :], in0=pss,
                                         scalar1=-8.0, scalar2=8.0,
                                         op0=OP.max, op1=OP.min,
                                         waits=(("pe_q", pqv), sg_g),
                                         inc="ve_q")
                            wv = (("ve_q", ev),)
                            if sname != "v":
                                kb.emit("vector", "reduce_sum",
                                        stats[:, tt * 8 + fc:tt * 8 + fc + 1],
                                        evq[par][:, :], axis=AX.X, inc="ve")
                                sq = kb.emit("scalar", "activation",
                                             out=ntmp[0][:, 0:512],
                                             in_=evq[par][:, :], func=AF.Square,
                                             accum_out=stats[:, 64 + tt * 8 + fc:64 + tt * 8 + fc + 1],
                                             waits=(("ve_q", ev),) +
                                             (seg0 if cnt == 0 else ()),
                                             inc="se_q")
                                wv = (("se_q", sq),)
                            kb.emit("gpsimd", "dma_start",
                                    out=segb[ts(tt, 128), ts(fc, 512)],
                                    in_=evq[par][:, :],
                                    waits=wv + (seg0 if cnt == 0 else ()),
                                    inc=f"sg{par}", inck=16)
                            cnt += 1
                        qm_mark[gi_] = kb.val("pe_q")
                        gfc["i"] += 1
                    sg_fin = tuple((f"sg{i}", kb.val(f"sg{i}"))
                                   for i in range(4))
                    if sname != "v":
                        for tt in range(NT):
                            bb = tt % 2
                            ld = kb.emit("gpsimd", "dma_start", out=xtile[:, :],
                                         in_=segb[ts(tt, 128), :],
                                         waits=sg_fin + (("se_ln", kb.val("se_ln")),),
                                         inc="xl", inck=16)
                            sm = stats[:, 184:185]
                            qm_ = stats[:, 185:186]
                            kb.emit("vector", "reduce_sum", sm,
                                    stats[:, tt * 8:tt * 8 + FCQ], axis=AX.X,
                                    waits=(("se_q", kb.val("se_q")),), inc="ve")
                            kb.emit("vector", "reduce_sum", qm_,
                                    stats[:, 64 + tt * 8:64 + tt * 8 + FCQ],
                                    axis=AX.X, inc="ve")
                            rdy, sc_col, bi_col = ln_finalize(tt, sm, qm_)
                            nw_g = (f"nw{bb}", kb.val(f"nw{bb}"))
                            nd = kb.emit("scalar", "activation",
                                         out=ntmp[bb][:, :], in_=xtile[:, :],
                                         func=AF.Identity, bias=bi_col,
                                         scale=sc_col,
                                         waits=(("ve", rdy), ("xl", ld), nw_g),
                                         inc="se_ln")
                            kb.emit("gpsimd", "dma_start",
                                    out=xlnb[ts(tt, 128), :], in_=ntmp[bb][:, :],
                                    waits=(("se_ln", nd),), inc=f"nw{bb}",
                                    inck=16)
                        nw_fin = (("nw0", kb.val("nw0")), ("nw1", kb.val("nw1")))
                        hsrc = xlnb
                    else:
                        nw_fin = sg_fin
                        hsrc = segb
                    for h in range(NHC):
                        hb = h % 2
                        if sname == "v":
                            kb.emit("gpsimd", "dma_start",
                                    out=dest[h // NH_LOC, h % NH_LOC, :, :],
                                    in_=segb[0:T, ts(h, 128)],
                                    waits=sg_fin if h == 0 else (),
                                    inc=f"sd{hb}", inck=16)
                            continue
                        tp_g = ("se_a", kb.val("se_a") - 1)
                        tp = kb.emit("sync", "dma_start", out=hT[hb][:, :],
                                     in_=hsrc[0:T, ts(h, 128)], transpose=True,
                                     waits=nw_fin + (tp_g,),
                                     inc=f"tp{hb}", inck=16)
                        tpv = (f"tp{hb}", kb.val(f"tp{hb}"))
                        if sname == "q":
                            sc_col = lnq[:, h:h + 1]
                            bi_col = lnq[:, NHC + h:NHC + h + 1]
                        else:
                            sc_col = lnp[:, 2 * NHC + h:2 * NHC + h + 1]
                            bi_col = lnp[:, 3 * NHC + h:3 * NHC + h + 1]
                        sd_g = (f"sd{hb}", kb.val(f"sd{hb}"))
                        av = kb.emit("scalar", "activation",
                                     out=hTa[hb][:, :], in_=hT[hb][:, :],
                                     func=AF.Identity, scale=sc_col,
                                     bias=bi_col,
                                     waits=(tpv, ("ve", lq), sd_g),
                                     inc="se_a")
                        kb.emit("gpsimd", "dma_start",
                                out=dest[h // NH_LOC, h % NH_LOC, :, :],
                                in_=hTa[hb][:, :], waits=(("se_a", av),),
                                inc=f"sd{hb}", inck=16)
                    sd_fin = (("sd0", kb.val("sd0")), ("sd1", kb.val("sd1")))
                    gathered = {"ccK": kt_out, "ccV": vt_out,
                                "ccQ": qt_out}[ccsem]
                    cmp_t = {"ccK": kt_cmp, "ccV": vt_cmp,
                             "ccQ": qt_cmp}[ccsem]
                    kb.emit("gpsimd", "collective_compute",
                            "AllGather", OP.bypass, replica_groups=RG,
                            ins=[dest.ap().opt()],
                            outs=[gathered.ap().opt()],
                            waits=sd_fin, inc=ccsem)
                    kb.emit("gpsimd", "dma_start", out=cmp_t,
                            in_=gathered.ap()
                            .rearrange("(s d) a b c -> s d a b c", d=4)[:, rkv],
                            waits=((ccsem, kb.val(ccsem)),),
                            inc=ccsem + "c", inck=16)
                barrier(("sd0", kb.val("sd0")), ("sd1", kb.val("sd1")),
                        ("tp0", kb.val("tp0")), ("tp1", kb.val("tp1")),
                        ("se_a", kb.val("se_a")),
                        ("pe_q", kb.val("pe_q")), ("ve_q", kb.val("ve_q")),
                        ("se_q", kb.val("se_q")))
                kb.play(block)

        def emit_attn_phase():
            with contextlib.ExitStack() as ps:
                pen = ps.enter_context
                Kh2 = [pen(nc.sbuf_tensor(f"Kh{i}", [128, S], BF16)) for i in range(2)]
                Qh2 = [pen(nc.sbuf_tensor(f"Qh{i}", [128, S], BF16)) for i in range(2)]
                Vh2 = [pen(nc.sbuf_tensor(f"Vh{i}", [128, S // 128, 128], BF16)) for i in range(2)]
                Pf2 = [pen(nc.sbuf_tensor(f"Pf{i}", [128, S], F32)) for i in range(2)]
                pfd = pen(nc.sbuf_tensor("pfd", [128, 512], F32))
                maskC = pen(nc.sbuf_tensor("maskC", [128, 4, 512], F32))
                identC = pen(nc.sbuf_tensor("identC", [128, 128], BF16))
                scor = pen(nc.sbuf_tensor("scor", [128, S], F32))
                prb = pen(nc.sbuf_tensor("prb", [128, 2, S], BF16))
                prT = [pen(nc.sbuf_tensor(f"prT_{i}", [128, 4, 256], BF16))
                       for i in range(2)]
                dg = pen(nc.sbuf_tensor("dg", [128, 2, 128], BF16))
                aosb = [pen(nc.sbuf_tensor(f"aosb_{i}", [128, 256], BF16))
                        for i in range(2)]
                block = pen(nc.Block())

                mk = kb.emit("sync", "dma_start", out=maskC[:, :, :],
                             in_=masksI.rearrange("q p f -> p q f"),
                             inc="mk", inck=16)
                mk = kb.emit("sync", "dma_start", out=identC[:, :], in_=ident,
                             inc="mk", inck=16)
                mkv = ("mk", kb.val("mk"))

                sc_cnt = 0
                pv_cnt = 0
                ao_cnt = 0
                head_mark = {}
                for hl in range(NH_LOC):
                    hb2 = hl % 2
                    Kh = Kh2[hb2]
                    Qh = Qh2[hb2]
                    Vh = Vh2[hb2]
                    Pf = Pf2[hb2]
                    hm = head_mark.get(hl - 2, {"pe_s": 0, "pe_pv": 0, "ve_s": 0})
                    kg = ("pe_s", hm["pe_s"])
                    vg = ("pe_pv", hm["pe_pv"])
                    pg = ("ve_s", hm["ve_s"])
                    for r in range(4):
                        kb.emit("gpsimd", "dma_start", out=Kh[:, ts(r, T)],
                                in_=kt_cmp[r, hl, :, :],
                                waits=(("ccKc", kb.val("ccKc")), kg),
                                inc="kh", inck=16)
                        kb.emit("gpsimd", "dma_start", out=Qh[:, ts(r, T)],
                                in_=qt_cmp[r, hl, :, :],
                                waits=(("ccQc", kb.val("ccQc")), kg),
                                inc="kh", inck=16)
                        kb.emit("gpsimd", "dma_start",
                                out=Vh[:, r * (T // 128):(r + 1) * (T // 128), :],
                                in_=vt_cmp[r, hl, :, :]
                                .rearrange("(c p) d -> p c d", p=128),
                                waits=(("ccVc", kb.val("ccVc")), vg),
                                inc="vh", inck=16)
                    pfl = kb.emit("gpsimd", "dma_start", out=Pf[:, :],
                                  in_=alibiI[hl, :, :],
                                  waits=(pg,), inc="pfl", inck=16)
                    khv = ("kh", kb.val("kh"))
                    vhv = ("vh", kb.val("vh"))
                    pflv = ("pfl", kb.val("pfl"))
                    for qb in range(NQB):
                        span = qb + 1
                        for qt in range(4):
                            qcol = qb * 512 + qt * 128
                            dv = kb.emit("vector", "tensor_tensor", out=pfd[:, :],
                                         in0=Pf[:, ts(qb, 512)],
                                         in1=maskC[:, qt, :], op=OP.add,
                                         waits=(pflv, mkv,
                                                ("ve_s", kb.val("ve_s"))),
                                         inc="ve")
                            exp_guard = ("se_e", kb.val("se_e"))
                            for kt in range(span):
                                par = sc_cnt % 2
                                pss = PSB[2][:, par * 512:par * 512 + 512]
                                evg = ("ve_s", kb.val("ve_s") - 1)
                                kb.emit("tensor", "matmul", pss,
                                        Qh[:, qcol:qcol + 128],
                                        Kh[:, ts(kt, 512)],
                                        start=True, stop=True,
                                        waits=(khv, evg), inc="pe_s")
                                psv = kb.val("pe_s")
                                bias_t = pfd[:, :] if kt == qb \
                                    else Pf[:, ts(kt, 512)]
                                bw = [("pe_s", psv)]
                                if kt == qb:
                                    bw.append(("ve", dv))
                                if kt == 0:
                                    bw.append(exp_guard)
                                kb.emit("vector", "tensor_tensor",
                                        out=scor[:, ts(kt, 512)], in0=pss,
                                        in1=bias_t, op=OP.add,
                                        waits=tuple(bw), inc="ve_s")
                                sc_cnt += 1
                            vsv = kb.val("ve_s")
                            kb.emit("vector", "reduce_max",
                                    stats[:, 192 + qt:193 + qt],
                                    scor[:, 0:span * 512], axis=AX.X,
                                    waits=(("ve_s", vsv),), inc="ve")
                            nmx = kb.emit("vector", "tensor_scalar",
                                          out=stats[:, 196 + qt:197 + qt],
                                          in0=stats[:, 192 + qt:193 + qt],
                                          scalar1=-1.0, scalar2=None,
                                          op0=OP.mult, op1=OP.bypass, inc="ve")
                            prb_g = ("pe_t", kb.val("pe_t"))
                            ex = kb.emit("scalar", "activation",
                                         out=prb[:, qt % 2, 0:span * 512],
                                         in_=scor[:, 0:span * 512], func=AF.Exp,
                                         bias=stats[:, 196 + qt:197 + qt],
                                         scale=1.0,
                                         accum_out=stats[:, 200 + qt:201 + qt],
                                         waits=(("ve", nmx), prb_g),
                                         inc="se_e")
                            rc = kb.emit("vector", "reciprocal",
                                         stats[:, 204 + qt:205 + qt],
                                         stats[:, 200 + qt:201 + qt],
                                         waits=(("se_e", ex),), inc="ve")
                            kb.emit("vector", "tensor_scalar",
                                    out=dg[:, qt % 2, :], in0=identC[:, :],
                                    scalar1=stats[:, 204 + qt:205 + qt],
                                    scalar2=None, op0=OP.mult, op1=OP.bypass,
                                    waits=(("ve", rc), mkv, prb_g), inc="ve_d")
                            if qt % 2 == 1:
                                dgv = ("ve_d", kb.val("ve_d"))
                                exv = ("se_e", kb.val("se_e"))
                                nkc = span * 4
                                ngr = span
                                pvo = PSB[1][:, (pv_cnt % 2) * 512:
                                             (pv_cnt % 2) * 512 + 256]
                                pv_evac_g = ("se_o", kb.val("se_o") - 1)

                                def tmm(gr):
                                    sl = gr % 2
                                    pst = (PSB[0], PSB[3])[sl]
                                    ev_g = ("se_p", kb.val("se_p") - 1)
                                    for jk in range(4):
                                        kc = gr * 4 + jk
                                        for j, qq in enumerate((qt - 1, qt)):
                                            kb.emit("tensor", "matmul",
                                                    pst[:, jk * 256 + j * 128:
                                                        jk * 256 + j * 128 + 128],
                                                    prb[:, qq % 2, ts(kc, 128)],
                                                    dg[:, qq % 2, :],
                                                    start=True, stop=True,
                                                    waits=(dgv, exv, ev_g)
                                                    if (jk == 0 and j == 0) else (),
                                                    inc="pe_t"
                                                    if (jk == 3 and j == 1) else None)
                                    pet = kb.val("pe_t")
                                    kb.emit("scalar", "copy",
                                            out=prT[sl][:, :, :], in_=pst[:, 0:1024],
                                            waits=(("pe_t", pet),
                                                   ("pe_pv", kb.val("pe_pv"))),
                                            inc="se_p")
                                    return ("se_p", kb.val("se_p"))

                                pmarks = {}
                                pmarks[0] = tmm(0)
                                for gr in range(ngr):
                                    if gr + 1 < ngr:
                                        pmarks[gr + 1] = tmm(gr + 1)
                                    for jk in range(4):
                                        kc = gr * 4 + jk
                                        w0 = [pmarks[gr], vhv] if jk == 0 else []
                                        if kc == 0:
                                            w0.append(pv_evac_g)
                                        kb.emit("tensor", "matmul", pvo,
                                                Vh[:, kc, :],
                                                prT[gr % 2][:, jk, :],
                                                start=(kc == 0),
                                                stop=(kc == nkc - 1),
                                                waits=tuple(w0), inc="pe_pv")
                                pvv = kb.val("pe_pv")
                                ab = ao_cnt % 2
                                ao_g = (f"ao{ab}", kb.val(f"ao{ab}"))
                                ov = kb.emit("scalar", "copy",
                                             out=aosb[ab][:, :], in_=pvo,
                                             waits=(("pe_pv", pvv), ao_g),
                                             inc="se_o")
                                pr = (qt - 1) // 2
                                kb.emit("gpsimd", "dma_start",
                                        out=ao_in[qb // QBR, hl, :,
                                                  (qb % QBR) * 512 + pr * 256:
                                                  (qb % QBR) * 512 + pr * 256 + 256],
                                        in_=aosb[ab][:, :],
                                        waits=(("se_o", ov),),
                                        inc=f"ao{ab}", inck=16)
                                ao_cnt += 1
                                pv_cnt += 1
                    head_mark[hl] = {"pe_s": kb.val("pe_s"),
                                     "pe_pv": kb.val("pe_pv"),
                                     "ve_s": kb.val("ve_s")}
                kb.emit("gpsimd", "collective_compute",
                        "AllGather", OP.bypass, replica_groups=RG,
                        ins=[ao_in.ap().opt()], outs=[ao_out.ap().opt()],
                        waits=(("ao0", kb.val("ao0")), ("ao1", kb.val("ao1"))),
                        inc="ccA")
                kb.emit("gpsimd", "dma_start", out=ao_cmp,
                        in_=ao_out.ap()
                        .rearrange("(s d) a b c -> s d a b c", d=4)[:, rkv],
                        waits=(("ccA", kb.val("ccA")),), inc="ccAc", inck=16)
                barrier(("ccA", kb.val("ccA")), ("se_o", kb.val("se_o")),
                        ("pe_pv", kb.val("pe_pv")))
                kb.play(block)

        def emit_wout_phase():
            with contextlib.ExitStack() as ps:
                pen = ps.enter_context
                woT = [pen(nc.sbuf_tensor("woT_0", [128, NHC, 512], BF16))]
                aot = pen(nc.sbuf_tensor("aot", [128, NHC, T], BF16))
                block = pen(nc.Block())

                for hc in range(NHC):
                    kb.emit("gpsimd", "dma_start", out=aot[:, hc, :],
                            in_=ao_cmp[hc // NH_LOC, hc % NH_LOC, :, :],
                            waits=(("ccAc", kb.val("ccAc")),), inc="aol",
                            inck=16)
                aolv = ("aol", kb.val("aol"))
                wo_total = ("woc", kb.val("woc"))
                cnt = 0
                wm_mark = {}
                for fc in range(D // 512):
                    b = 0
                    kb.emit("sync", "dma_start", out=woT[b][:, :, :],
                            in_=wout_c[ts(fc, 512), :], transpose=True,
                            waits=(wo_total, ("pe_w", wm_mark.get(fc - 1, 0))),
                            inc=f"wo{b}", inck=16)
                    wov = (f"wo{b}", kb.val(f"wo{b}"))
                    for tt in range(NT):
                        par = cnt % 2
                        pss = PSB[par][:, 0:512]
                        evg = ("ve_x", kb.val("ve_x") - 1)
                        for hc in range(NHC):
                            w0 = (wov, aolv, evg) if hc == 0 else ()
                            kb.emit("tensor", "matmul", pss,
                                    aot[:, hc, ts(tt, 128)], woT[b][:, hc, :],
                                    start=(hc == 0), stop=(hc == NHC - 1),
                                    waits=w0,
                                    inc="pe_w" if hc == NHC - 1 else None)
                        pwv = kb.val("pe_w")
                        xb = par
                        xr = kb.emit("scalar", "dma_start",
                                     out=xres[xb][:, :],
                                     in_=x1b[ts(tt, 128), ts(fc, 512)],
                                     waits=(("ve_x", kb.val("ve_x") - 1),),
                                     inc=f"xr{xb}", inck=16)
                        xv = kb.emit("vector", "tensor_tensor",
                                     out=xo[xb][:, :], in0=pss,
                                     in1=xres[xb][:, :], op=OP.add,
                                     waits=(("pe_w", pwv), (f"xr{xb}", xr),
                                            (f"xs{xb}", kb.val(f"xs{xb}"))),
                                     inc="ve_x")
                        kb.emit("vector", "reduce_sum",
                                stats[:, tt * 8 + fc:tt * 8 + fc + 1],
                                xo[xb][:, :], axis=AX.X, inc="ve")
                        sq = kb.emit("scalar", "activation",
                                     out=xnrm[0][:, 0:512], in_=xo[xb][:, :],
                                     func=AF.Square,
                                     accum_out=stats[:, 64 + tt * 8 + fc:64 + tt * 8 + fc + 1],
                                     waits=(("ve_x", xv),), inc="se_st")
                        kb.emit("scalar", "dma_start",
                                out=x2b[ts(tt, 128), ts(fc, 512)],
                                in_=xo[xb][:, :],
                                waits=(("se_st", kb.val("se_st")),),
                                inc=f"xs{xb}", inck=16)
                        cnt += 1
                    wm_mark[fc] = kb.val("pe_w")
                barrier(("xs0", kb.val("xs0")), ("xs1", kb.val("xs1")),
                        ("pe_w", kb.val("pe_w")), ("se_st", kb.val("se_st")))
                kb.play(block)

        xs_fin1 = emit_ffn_phase(1, x0, x1b, True, x0, "self",
                                 final_store=(stop_after == "ffn1"))

        if stop_after == "ffn1":
            with nc.Block() as blk:
                for tt in range(NT):
                    kb.emit("gpsimd", "dma_start", out=xoutD[ts(tt, 128), :],
                            in_=x1b[ts(tt, 128), :],
                            waits=xs_fin1, inc="fin", inck=16)
                kb.wait("gpsimd", ("fin", kb.val("fin")))
                kb.play(blk)
            return nc

        emit_qkv_phase()
        emit_attn_phase()
        emit_wout_phase()
        xs_fin2 = emit_ffn_phase(2, x2b, xoutD, False, x2b, "cols",
                                 final_store=True)
        with nc.Block() as blk:
            kb.wait("gpsimd", *xs_fin2)
            kb.play(blk)
    return nc



def _const_inputs():
    ident = np.eye(128, dtype=np.float32).astype(BF)
    masks = np.zeros((4, 128, 512), dtype=np.float32)
    for qt in range(4):
        d = qt * 128
        p = np.arange(128)[:, None]
        j = np.arange(512)[None, :]
        masks[qt] = np.where(j > d + p, NEG, 0.0).astype(np.float32)
    return ident, masks


def alibi_slopes(n_heads, bias_max=8.0):
    next_pow2 = 2 ** int(np.ceil(np.log2(n_heads)))
    m = np.arange(1, next_pow2 + 1, dtype=np.float32) * (bias_max / next_pow2)
    slopes = 1.0 / np.power(2.0, m)
    if next_pow2 != n_heads:
        slopes = np.concatenate([slopes[1::2], slopes[::2]])[:n_heads]
    return slopes.astype(np.float32)


def _alibi_for_core(core):
    nh = D // 128
    nh_loc = nh // 4
    h0 = nh_loc * (core % 4)
    sl = alibi_slopes(nh)[h0:h0 + nh_loc]
    kk = np.arange(4 * T, dtype=np.float32)
    a = sl[:, None] * kk[None, :]
    return np.broadcast_to(a[:, None, :], (nh_loc, 128, 4 * T)).copy()


def kernel(L0_attn_hidden_states, L0_attn_Wqkv, L0_attn_q_ln_w, L0_attn_q_ln_b,
           L0_attn_k_ln_w, L0_attn_k_ln_b, L0_attn_Wout, L0_ffn_qw_gate,
           L0_ffn_qw_up, L0_ffn_qw_down, L0_ffn_s_gate, L0_ffn_z_gate,
           L0_ffn_s_up, L0_ffn_z_up, L0_ffn_s_down, L0_ffn_z_down,
           _stop_after=None, _only_maps=False):
    x = np.ascontiguousarray(np.asarray(L0_attn_hidden_states, dtype=np.float32)
                             .reshape(8 * T, D))
    ident, masks = _const_inputs()

    def col128(v):
        return np.ascontiguousarray(
            np.asarray(v, np.float32).reshape(D // 128, 128).T)

    in_maps = _prepare_in_maps_inner(x, ident, masks, col128,
        L0_attn_Wqkv, L0_attn_Wout, L0_ffn_qw_gate, L0_ffn_qw_up,
        L0_ffn_qw_down, L0_ffn_s_gate, L0_ffn_z_gate, L0_ffn_s_up,
        L0_ffn_z_up, L0_ffn_s_down, L0_ffn_z_down,
        L0_attn_q_ln_w, L0_attn_q_ln_b, L0_attn_k_ln_w, L0_attn_k_ln_b)
    if _only_maps:
        return in_maps

    nc = build_nc(stop_after=_stop_after)
    res = run_bass_kernel_spmd(nc, in_maps, core_ids=list(range(8)))
    out = np.concatenate([res.results[c]["xout"] for c in range(8)], axis=0)
    return out.reshape(2, 4 * T, D).astype(np.float32)


def _ilv(s, z):
    s = np.asarray(s, np.float32)
    z = np.asarray(z, np.float32)
    return np.ascontiguousarray(np.stack([s, z], axis=2).reshape(s.shape[0], -1))


def _prepare_in_maps_inner(x, ident, masks, col128, Wqkv, Wout, qwg, qwu, qwd,
                           sg, zg, su, zu, sd, zd, qw_, qb_, kw_, kb_):
    common = dict(
        wqkv=np.asarray(Wqkv, np.float32),
        wout=np.asarray(Wout, np.float32),
        qwg=np.asarray(qwg, np.int32),
        qwu=np.asarray(qwu, np.int32),
        qwd=np.asarray(qwd, np.int32),
        szg=_ilv(sg, zg),
        szu=_ilv(su, zu),
        szd=_ilv(sd, zd),
        qlnw=col128(qw_), qlnb=col128(qb_),
        klnw=col128(kw_), klnb=col128(kb_),
        ident=ident, masks=masks,
    )
    in_maps = []
    for c in range(8):
        m = dict(common)
        m["x0"] = np.ascontiguousarray(x[c * T:(c + 1) * T])
        m["alibi"] = _alibi_for_core(c)
        in_maps.append(m)
    return in_maps


def _prepare_in_maps(**inputs):
    return kernel(**inputs, _only_maps=True)


# version 13
# speedup vs baseline: 1.1100x; 1.0091x over previous
import contextlib
import numpy as np
import ml_dtypes
import concourse.bass as bass
import concourse.mybir as mybir
from concourse.engine_type import EngineType
from concourse.bass_utils import run_bass_kernel_spmd

F32 = mybir.dt.float32
BF16 = mybir.dt.bfloat16
I32 = mybir.dt.int32
AF = mybir.ActivationFunctionType
OP = mybir.AluOpType
AX = mybir.AxisListType
BF = ml_dtypes.bfloat16
ts = bass.ts

D = 4096
IF = 11008
NI = IF // 128
T = 1024
NT = T // 128
NHC = D // 128
EPS = 1e-5
CHUNK = 2048
GSZ = 4
QK_SCALE = 1.0 / np.sqrt(128.0)
NEG = -1e30

ENGINES = ("sync", "scalar", "vector", "tensor", "gpsimd")


class KB:

    def __init__(self, nc, stack):
        self.nc = nc
        self.stack = stack
        self.ops = {e: [] for e in ENGINES}
        self.semh = {}
        self.semc = {}
        self.last_inc = {}

    def sem(self, name):
        if name not in self.semh:
            self.semh[name] = self.stack.enter_context(self.nc.semaphore(name))
            self.semc[name] = 0
        return self.semh[name]

    def inc(self, name, k=1):
        self.sem(name)
        self.semc[name] += k
        return self.semc[name]

    def val(self, name):
        return self.semc.get(name, 0)

    def emit(self, eng, method, *args, waits=(), inc=None, inck=1, **kw):
        rv = None
        ih = None
        waits = list(waits)
        is_dma = method == "dma_start"
        if eng in ("vector", "scalar") and not is_dma and eng in self.last_inc:
            waits.append(self.last_inc[eng])
        if inc is not None:
            rv = self.inc(inc, inck)
            ih = self.semh[inc]
            if eng in ("vector", "scalar") and not is_dma:
                self.last_inc[eng] = (inc, rv)
        seen = set()
        wl = []
        for (n, v) in waits:
            if v > 0 and (n, v) not in seen:
                seen.add((n, v))
                wl.append((self.sem(n), v))

        def f(E, method=method, args=args, kw=kw, wl=wl, ih=ih, inck=inck):
            for (h, v) in wl:
                E.wait_ge(h, v)
            r = getattr(E, method)(*args, **kw)
            if ih is not None:
                r.then_inc(ih, inck)

        self.ops[eng].append(f)
        return rv

    def wait(self, eng, *waits):
        wl = [(self.sem(n), v) for (n, v) in waits if v > 0]

        def f(E, wl=wl):
            for (h, v) in wl:
                E.wait_ge(h, v)

        self.ops[eng].append(f)

    def play(self, block):
        for e in ENGINES:
            dec = getattr(block, e)
            ops = self.ops[e]

            def body(E, ops=ops):
                for f in ops:
                    f(E)

            dec(body)
        self.ops = {e: [] for e in ENGINES}


def build_nc(stop_after=None):
    NI = IF // 128
    NT = T // 128
    NHC = D // 128
    TH = T // 2
    FCW = min(512, D, T)
    FC_N = D // FCW
    GCH = [(c0, min(CHUNK, D - c0)) for c0 in range(0, D, CHUNK)]
    DCH = [(c0, min(CHUNK, IF - c0)) for c0 in range(0, IF, CHUNK)]
    nc = bass.Bass("TRN2", target_bir_lowering=False)

    x0 = nc.dram_tensor("x0", [T, D], F32, kind="ExternalInput").ap()
    wqkv = nc.dram_tensor("wqkv", [3 * D, D], F32, kind="ExternalInput").ap()
    wout = nc.dram_tensor("wout", [D, D], F32, kind="ExternalInput").ap()
    qwg = nc.dram_tensor("qwg", [IF, D], I32, kind="ExternalInput").ap()
    qwu = nc.dram_tensor("qwu", [IF, D], I32, kind="ExternalInput").ap()
    qwd = nc.dram_tensor("qwd", [D, IF], I32, kind="ExternalInput").ap()
    szg = nc.dram_tensor("szg", [IF, 64], F32, kind="ExternalInput").ap()
    szu = nc.dram_tensor("szu", [IF, 64], F32, kind="ExternalInput").ap()
    szd = nc.dram_tensor("szd", [D, 172], F32, kind="ExternalInput").ap()
    qlnw = nc.dram_tensor("qlnw", [128, NHC], F32, kind="ExternalInput").ap()
    qlnb = nc.dram_tensor("qlnb", [128, NHC], F32, kind="ExternalInput").ap()
    klnw = nc.dram_tensor("klnw", [128, NHC], F32, kind="ExternalInput").ap()
    klnb = nc.dram_tensor("klnb", [128, NHC], F32, kind="ExternalInput").ap()
    ident = nc.dram_tensor("ident", [128, 128], BF16, kind="ExternalInput").ap()
    masksI = nc.dram_tensor("masks", [4, 128, 512], F32, kind="ExternalInput").ap()
    alibiI = nc.dram_tensor("alibi", [NHC // 4, 128, 4 * T], F32,
                            kind="ExternalInput").ap()

    xoutD = nc.dram_tensor("xout", [T, D], F32, kind="ExternalOutput").ap()

    wg_c = nc.dram_tensor("wg_c", [IF, D], BF16, kind="Internal").ap()
    wu_c = nc.dram_tensor("wu_c", [IF, D], BF16, kind="Internal").ap()
    wd_c = nc.dram_tensor("wd_c", [D, IF], BF16, kind="Internal").ap()
    wqkv_c = nc.dram_tensor("wqkv_c", [3 * D, D], BF16, kind="Internal").ap()
    wout_c = nc.dram_tensor("wout_c", [D, D], BF16, kind="Internal").ap()
    x1b = nc.dram_tensor("x1b", [T, D], F32, kind="Internal").ap()
    x2b = nc.dram_tensor("x2b", [T, D], F32, kind="Internal").ap()
    xlnb = nc.dram_tensor("xlnb", [T, D], BF16, kind="Internal").ap()
    hbuf = nc.dram_tensor("hbuf", [NI, 128, T], BF16, kind="Internal").ap()
    segb = nc.dram_tensor("segb", [T, D], BF16, kind="Internal").ap()
    qt_in = nc.dram_tensor("qt_in", [4, NHC // 4, 128, T], BF16, kind="Internal")
    kt_in = nc.dram_tensor("kt_in", [4, NHC // 4, 128, T], BF16, kind="Internal")
    vt_in = nc.dram_tensor("vt_in", [4, NHC // 4, T, 128], BF16, kind="Internal")
    qt_out = nc.dram_tensor("qt_out", [16, NHC // 4, 128, T], BF16, kind="Internal")
    kt_out = nc.dram_tensor("kt_out", [16, NHC // 4, 128, T], BF16, kind="Internal")
    vt_out = nc.dram_tensor("vt_out", [16, NHC // 4, T, 128], BF16, kind="Internal")
    ao_in = nc.dram_tensor("ao_in", [4, NHC // 4, 128, T], BF16, kind="Internal")
    ao_out = nc.dram_tensor("ao_out", [16, NHC // 4, 128, T], BF16, kind="Internal")
    qt_cmp = nc.dram_tensor("qt_cmp", [4, NHC // 4, 128, T], BF16, kind="Internal").ap()
    kt_cmp = nc.dram_tensor("kt_cmp", [4, NHC // 4, 128, T], BF16, kind="Internal").ap()
    vt_cmp = nc.dram_tensor("vt_cmp", [4, NHC // 4, T, 128], BF16, kind="Internal").ap()
    ao_cmp = nc.dram_tensor("ao_cmp", [4, NHC // 4, 128, T], BF16, kind="Internal").ap()

    RG = [[0, 1, 2, 3], [4, 5, 6, 7]]
    pid = nc.partition_id([EngineType.Pool, EngineType.SP])
    rkv = pid % 4
    RIDX = [rkv + 4 * r for r in range(4)]

    with contextlib.ExitStack() as stack:
        en = stack.enter_context
        xtile = en(nc.sbuf_tensor("xtile", [128, D], F32))
        xnrm = [en(nc.sbuf_tensor("xnrm0", [128, D], BF16))]
        xres = [en(nc.sbuf_tensor(f"xres{i}", [128, 512], F32)) for i in range(2)]
        ytmp = [en(nc.sbuf_tensor(f"ytmp{i}", [128, 512], F32)) for i in range(2)]
        xo = [en(nc.sbuf_tensor(f"xo{i}", [128, 512], F32)) for i in range(2)]
        stats = en(nc.sbuf_tensor("stats", [128, 224], F32))
        PSB = [en(nc.psum_tensor(f"psb{i}", [128, 1024], F32)) for i in range(4)]
        kb = KB(nc, stack)

        def barrier(*finals):
            kb.wait("gpsimd", *finals)
            phv = kb.emit("gpsimd", "memset", stats[0:1, 220:221], 0.0, inc="ph")
            for e in ("sync", "scalar", "vector", "tensor"):
                kb.wait(e, ("ph", phv))
            return phv

        def ln_finalize(tt, s_col, q_col):
            base = 128 + 4 * tt
            mu = stats[:, base:base + 1]
            mu2 = stats[:, base + 1:base + 2]
            var = stats[:, base + 2:base + 3]
            sd = stats[:, base + 3:base + 4]
            sc = stats[:, 168 + tt:168 + tt + 1]
            bi = stats[:, 176 + tt:176 + tt + 1]
            kb.emit("vector", "tensor_scalar", out=mu, in0=s_col, scalar1=1.0 / D,
                    scalar2=None, op0=OP.mult, op1=OP.bypass, inc="ve")
            kb.emit("vector", "tensor_tensor", out=mu2, in0=mu, in1=mu, op=OP.mult,
                    inc="ve")
            kb.emit("vector", "tensor_scalar", out=var, in0=q_col, scalar1=1.0 / D,
                    scalar2=None, op0=OP.mult, op1=OP.bypass, inc="ve")
            kb.emit("vector", "tensor_tensor", out=var, in0=var, in1=mu2,
                    op=OP.subtract, inc="ve")
            ve_done = kb.emit("vector", "tensor_scalar", out=var, in0=var,
                              scalar1=EPS, scalar2=None, op0=OP.add,
                              op1=OP.bypass, inc="ve")
            sq_done = kb.emit("scalar", "activation", out=sd, in_=var, func=AF.Sqrt,
                              waits=(("ve", ve_done),), inc="se")
            kb.emit("vector", "reciprocal", sc, sd, waits=(("se", sq_done),),
                    inc="ve")
            kb.emit("vector", "tensor_tensor", out=bi, in0=mu, in1=sc, op=OP.mult,
                    inc="ve")
            bi_done = kb.emit("vector", "tensor_scalar", out=bi, in0=bi,
                              scalar1=-1.0, scalar2=None, op0=OP.mult,
                              op1=OP.bypass, inc="ve")
            return bi_done, sc, bi

        def emit_ln_to_xlnT(src, stats_mode, xlnT_guard, xlnT):
            for tt in range(NT):
                b = 0
                ld = kb.emit("gpsimd", "dma_start", out=xtile[:, :],
                             in_=src[ts(tt, 128), :],
                             waits=(("se_ln", kb.val("se_ln")),),
                             inc="xl", inck=16)
                xw_guard = (f"xw{b}", kb.val(f"xw{b}"))
                if stats_mode == "self":
                    kb.emit("vector", "reduce_sum", stats[:, tt:tt + 1],
                            xtile[:, :], axis=AX.X, waits=(("xl", ld),),
                            inc="ve")
                    qm = kb.emit("scalar", "activation", out=xnrm[b][:, :],
                                 in_=xtile[:, :], func=AF.Square,
                                 accum_out=stats[:, 64 + tt:64 + tt + 1],
                                 waits=(("xl", ld), xw_guard),
                                 inc="se")
                    kb.wait("vector", ("se", qm))
                    rdy, sc_col, bi_col = ln_finalize(tt, stats[:, tt:tt + 1],
                                                      stats[:, 64 + tt:64 + tt + 1])
                else:
                    sm = stats[:, 184:185]
                    qm_ = stats[:, 185:186]
                    kb.emit("vector", "reduce_sum", sm,
                            stats[:, tt * 8:tt * 8 + FC_N], axis=AX.X, inc="ve")
                    kb.emit("vector", "reduce_sum", qm_,
                            stats[:, 64 + tt * 8:64 + tt * 8 + FC_N], axis=AX.X,
                            inc="ve")
                    rdy, sc_col, bi_col = ln_finalize(tt, sm, qm_)
                nd = kb.emit("scalar", "activation", out=xnrm[b][:, :],
                             in_=xtile[:, :], func=AF.Identity,
                             bias=bi_col, scale=sc_col,
                             waits=(("ve", rdy), ("xl", ld), xw_guard),
                             inc="se_ln")
                kb.emit("gpsimd", "dma_start", out=xlnb[ts(tt, 128), :],
                        in_=xnrm[b][:, :], waits=(("se_ln", nd),),
                        inc=f"xw{b}", inck=16)
            xw_fin = (("xw0", kb.val("xw0")), ("xw1", kb.val("xw1")))
            for half in range(2):
                kb.emit("sync", "dma_start", out=xlnT[:, :, ts(half, T // 2)],
                        in_=xlnb[ts(half, T // 2), :], transpose=True,
                        waits=xw_fin + (xlnT_guard,),
                        inc="xt", inck=16)
            return kb.val("xt")

        def emit_ffn_phase(call, x_res_src, x_dst, collect_stats, src_for_ln,
                           ln_mode, final_store=False):
            with contextlib.ExitStack() as ps:
                pe = ps.enter_context
                NBUF = 4
                qnat = [pe(nc.sbuf_tensor(f"qnat{call}_{i}", [128, min(CHUNK, max(D, IF))], BF16))
                        for i in range(NBUF)]
                wnat = [pe(nc.sbuf_tensor(f"wnat{call}_{i}", [128, min(CHUNK, max(D, IF))], BF16))
                        for i in range(NBUF)]
                szb = [pe(nc.sbuf_tensor(f"szb{call}_{i}", [128, 64], F32))
                       for i in range(NBUF)]
                wgT = [pe(nc.sbuf_tensor(f"wgT{call}_{i}", [128, NHC, 128], BF16))
                       for i in range(2)]
                wuT = [pe(nc.sbuf_tensor(f"wuT{call}_{i}", [128, NHC, 128], BF16))
                       for i in range(2)]
                hsil = pe(nc.sbuf_tensor(f"hsil{call}", [128, T], F32))
                hbf = [pe(nc.sbuf_tensor(f"hbf{call}_{i}", [128, T], BF16))
                       for i in range(2)]
                wdT = [pe(nc.sbuf_tensor(f"wdT{call}_{i}", [128, GSZ, FCW], BF16))
                       for i in range(4)]
                hld = [pe(nc.sbuf_tensor(f"hld{call}_{i}", [128, T], BF16))
                       for i in range(5)]
                block = pe(nc.Block())

                xlnT = pe(nc.sbuf_tensor(f"xlnT{call}", [128, NHC, T], BF16))
                xt_ready = emit_ln_to_xlnT(src_for_ln, ln_mode,
                                           ("pe", kb.val("pe")), xlnT)

                hw_base = kb.val("hw")

                p_state = {"cnt": 0}
                wc_gu = {}
                wcd_done = {"v": 0}

                def produce_tile(qsrc, szsrc, cache, crow, ccol0, cw, wcfam):
                    p = p_state["cnt"]
                    p_state["cnt"] += 1
                    b = p % NBUF
                    dq_guard = ("dq", kb.val("dq") - (NBUF - 1))
                    qn = kb.emit("gpsimd", "dma_start", out=qnat[b][:, 0:cw],
                                 in_=qsrc[ts(crow, 128), ccol0:ccol0 + cw],
                                 waits=(dq_guard,), inc=f"qn{b}", inck=16)
                    ng = cw // 128
                    g0 = ccol0 // 128
                    sz = kb.emit("gpsimd", "dma_start", out=szb[b][:, 0:2 * ng],
                                 in_=szsrc[ts(crow, 128), 2 * g0:2 * (g0 + ng)],
                                 waits=(dq_guard,), inc=f"sz{b}", inck=16)
                    wc_guard = (f"wc{b}", kb.val(f"wc{b}"))
                    for g in range(ng):
                        w = ((f"qn{b}", qn), (f"sz{b}", sz), wc_guard) if g == 0 else ()
                        kb.emit("vector", "tensor_scalar",
                                out=wnat[b][:, ts(g, 128)],
                                in0=qnat[b][:, ts(g, 128)],
                                scalar1=szb[b][:, 2 * g + 1:2 * g + 2],
                                scalar2=szb[b][:, 2 * g:2 * g + 1],
                                op0=OP.subtract, op1=OP.mult,
                                waits=w, inc="dq" if g == ng - 1 else None)
                    v = kb.emit("gpsimd", "dma_start",
                                out=cache[ts(crow, 128), ccol0:ccol0 + cw],
                                in_=wnat[b][:, 0:cw],
                                waits=(("dq", kb.val("dq")),),
                                inc=f"wc{b}", inck=16)
                    return (f"wc{b}", v)

                def produce_gu(ic):
                    wg = [produce_tile(qwg, szg, wg_c, ic, c0, cw, "wcg")
                          for c0, cw in GCH]
                    wu = [produce_tile(qwu, szu, wu_c, ic, c0, cw, "wcu")
                          for c0, cw in GCH]
                    wc_gu[ic] = (wg, wu)

                def produce_down(j):
                    fc, cj = divmod(j, len(DCH))
                    c0, cw = DCH[cj]
                    produce_tile(qwd, szd, wd_c, fc, c0, cw, "wcd")
                    wcd_done["v"] = tuple((f"wc{i}", kb.val(f"wc{i}"))
                                          for i in range(NBUF))

                pe_mark = {}
                hw_val = {}
                down_j = {"j": 0}
                ND = NHC * len(DCH)

                KLAG = 4

                def consume_ic(ic):
                    s = ic % 2
                    for m, bufs, cache, wci in (("g", wgT, wg_c, 0),
                                                ("u", wuT, wu_c, 1)):
                        buf = bufs[s]
                        w = [("pe", pe_mark.get(ic - 2, 0))]
                        if call == 1:
                            w.extend(wc_gu[min(ic + KLAG, NI - 1)][wci])
                        kb.emit("sync", "dma_start", out=buf[:, :, :],
                                in_=cache[ts(ic, 128), :], transpose=True,
                                waits=tuple(w), inc=f"wt{m}{s}", inck=16)
                    wt_w = ((f"wtg{s}", kb.val(f"wtg{s}")),
                            (f"wtu{s}", kb.val(f"wtu{s}")))
                    par = ic % 2
                    psg = PSB[2 * par]
                    psu = PSB[2 * par + 1]
                    sil_guard = ("se_sil", kb.val("se_sil") - 1)
                    mul_guard = ("ve_mul", kb.val("ve_mul") - 1)
                    gbuf = wgT[ic % 2]
                    ubuf = wuT[ic % 2]
                    halves = [(0, T)] if T <= 512 else [(0, TH), (TH, T)]
                    for hc in range(NHC):
                        w0 = wt_w + (("xt", xt_ready), sil_guard,
                              mul_guard) if hc == 0 else ()
                        for (h0, h1) in halves:
                            kb.emit("tensor", "matmul", psg[:, h0:h1],
                                    gbuf[:, hc, :], xlnT[:, hc, h0:h1],
                                    start=(hc == 0), stop=(hc == NHC - 1),
                                    waits=w0 if h0 == 0 else ())
                    for hc in range(NHC):
                        for (h0, h1) in halves:
                            kb.emit("tensor", "matmul", psu[:, h0:h1],
                                    ubuf[:, hc, :], xlnT[:, hc, h0:h1],
                                    start=(hc == 0), stop=(hc == NHC - 1),
                                    inc="pe" if (hc == NHC - 1 and h1 == T) else None)
                    pev = kb.val("pe")
                    hb = ic % 2
                    hw_guard = (f"hw{hb}", kb.val(f"hw{hb}"))
                    sv = kb.emit("scalar", "activation", out=hsil[:, :],
                                 in_=psg[:, 0:T], func=AF.Silu,
                                 waits=(("pe", pev), ("ve_mul", kb.val("ve_mul"))),
                                 inc="se_sil")
                    mv = kb.emit("vector", "tensor_tensor", out=hbf[hb][:, :],
                                 in0=hsil[:, :], in1=psu[:, 0:T], op=OP.mult,
                                 waits=(("se_sil", sv), hw_guard), inc="ve_mul")
                    hwv = kb.emit("scalar", "dma_start", out=hbuf[ic, :, :],
                                  in_=hbf[hb][:, :], waits=(("ve_mul", mv),),
                                  inc=f"hw{hb}", inck=16)
                    hw_val[ic] = (f"hw{hb}", hwv)
                    pe_mark[ic] = kb.val("pe")

                if call == 1:
                    cast_jobs = []
                    R3 = 3 * D
                    CR = max(128, R3 // 96)
                    for r0 in range(0, R3, CR):
                        rn = min(CR, R3 - r0)
                        cast_jobs.append((wqkv_c[r0:r0 + rn, :],
                                          wqkv[r0:r0 + rn, :], "wqc"))
                    CR2 = max(128, D // 16)
                    for r0 in range(0, D, CR2):
                        cast_jobs.append((wout_c[r0:r0 + CR2, :],
                                          wout[r0:r0 + CR2, :], "woc"))
                    cj = {"j": 0}

                    def emit_casts(frac):
                        njobs = min(len(cast_jobs),
                                    int(frac * len(cast_jobs)) + 1)
                        while cj["j"] < njobs:
                            o, i, sem = cast_jobs[cj["j"]]
                            kb.emit("gpsimd", "dma_start", out=o, in_=i,
                                    inc=sem, inck=16)
                            cj["j"] += 1

                    for ic0 in range(KLAG + 2):
                        produce_gu(ic0)
                    for ic in range(NI):
                        if ic + KLAG + 2 < NI:
                            produce_gu(ic + KLAG + 2)
                        nd = min(ND, (ic + 1) * ND // NI)
                        while down_j["j"] < nd:
                            produce_down(down_j["j"])
                            down_j["j"] += 1
                        consume_ic(ic)
                        emit_casts((ic + 1) / NI)
                else:
                    for ic in range(NI):
                        consume_ic(ic)

                hw_fin = (("hw0", kb.val("hw0")), ("hw1", kb.val("hw1")))
                pe_gu_done = kb.val("pe")

                groups = [(g0, min(GSZ, NI - g0)) for g0 in range(0, NI, GSZ)]
                gmark = {}
                gidx = 0
                hln = {"n": 0}
                for fc in range(FC_N):
                    ev_guard = ("ve_x", kb.val("ve_x"))
                    for gi, (g0, gn) in enumerate(groups):
                        b = gidx % 4
                        w = [("pe_d", gmark.get(gidx - 4, 0))]
                        if call == 1:
                            w.extend(wcd_done["v"])
                        kb.emit("sync", "dma_start", out=wdT[b][:, 0:gn, 0:FCW],
                                in_=wd_c[ts(fc, FCW), g0 * 128:(g0 + gn) * 128],
                                transpose=True, waits=tuple(w), inc=f"wtd{b}",
                                inck=16)
                        tl = (f"wtd{b}", kb.val(f"wtd{b}"))
                        for j in range(gn):
                            ic = g0 + j
                            hb = hln["n"] % 5
                            hln["n"] += 1
                            hl_guard = ("pe_d", kb.val("pe_d") - 4)
                            hneed = (hw_val[ic],) if (call == 1 and fc == 0) else hw_fin
                            hl = kb.emit("gpsimd", "dma_start",
                                         out=hld[hb][:, :], in_=hbuf[ic, :, :],
                                         waits=hneed + (hl_guard,),
                                         inc=f"hr{hb}", inck=16)
                            for tt in range(NT):
                                w0 = []
                                if tt == 0:
                                    w0 = [(f"hr{hb}", hl), tl]
                                    if ic == 0:
                                        w0.append(ev_guard)
                                kb.emit("tensor", "matmul",
                                        PSB[(tt * FCW) // 1024][:, (tt * FCW) % 1024:(tt * FCW) % 1024 + FCW],
                                        hld[hb][:, ts(tt, 128)],
                                        wdT[b][:, j, 0:FCW],
                                        start=(ic == 0), stop=(ic == NI - 1),
                                        waits=tuple(w0),
                                        inc="pe_d" if tt == NT - 1 else None)
                        gmark[gidx] = kb.val("pe_d")
                        gidx += 1
                    pedv = kb.val("pe_d")
                    for tt in range(NT):
                        b = tt % 2
                        xr = kb.emit("scalar", "dma_start", out=xres[b][:, 0:FCW],
                                     in_=x_res_src[ts(tt, 128), ts(fc, FCW)],
                                     waits=(("ve_x", kb.val("ve_x") - 1),),
                                     inc=f"xr{b}", inck=16)
                        yv = kb.emit("scalar", "activation", out=ytmp[b][:, 0:FCW],
                                     in_=PSB[(tt * FCW) // 1024][:, (tt * FCW) % 1024:(tt * FCW) % 1024 + FCW],
                                     func=AF.Copy, scale=0.5,
                                     waits=(("pe_d", pedv),
                                            ("ve_x", kb.val("ve_x") - 1)),
                                     inc="se")
                        xv = kb.emit("vector", "tensor_tensor", out=xo[b][:, 0:FCW],
                                     in0=ytmp[b][:, 0:FCW], in1=xres[b][:, 0:FCW],
                                     op=OP.add,
                                     waits=(("se", yv), (f"xr{b}", xr),
                                            (f"xs{b}", kb.val(f"xs{b}"))),
                                     inc="ve_x")
                        if collect_stats:
                            kb.emit("vector", "reduce_sum",
                                    stats[:, tt * 8 + fc:tt * 8 + fc + 1],
                                    xo[b][:, 0:FCW], axis=AX.X, inc="ve")
                            sq = kb.emit("scalar", "activation",
                                         out=xnrm[0][:, 0:FCW],
                                         in_=xo[b][:, 0:FCW], func=AF.Square,
                                         accum_out=stats[:, 64 + tt * 8 + fc:64 + tt * 8 + fc + 1],
                                         waits=(("ve_x", xv),), inc="se_st")
                            wv = (("se_st", sq),)
                        else:
                            wv = (("ve_x", xv),)
                        kb.emit("scalar", "dma_start",
                                out=x_dst[ts(tt, 128), ts(fc, FCW)],
                                in_=xo[b][:, 0:FCW], waits=wv, inc=f"xs{b}",
                                inck=16)

                xs_fin = (("xs0", kb.val("xs0")), ("xs1", kb.val("xs1")))
                if final_store:
                    kb.wait("gpsimd", *xs_fin)
                else:
                    barrier(*(xs_fin + (("pe_d", kb.val("pe_d")),
                            ("ve_x", kb.val("ve_x")),
                            ("hr0", kb.val("hr0")), ("hr1", kb.val("hr1")),
                            ("se_st" if collect_stats else "se",
                             kb.val("se_st") if collect_stats else kb.val("se")))))
                kb.play(block)
                return xs_fin


        NH_LOC = NHC // 4
        S = 4 * T
        NQB = S // 512
        QBR = max(1, T // 512)

        def emit_qkv_phase():
            with contextlib.ExitStack() as ps:
                pen = ps.enter_context
                wqT2 = [pen(nc.sbuf_tensor(f"wqT_{i}", [128, NHC, 512], BF16))
                        for i in range(2)]
                evq = [pen(nc.sbuf_tensor(f"evq_{i}", [128, 512], BF16))
                       for i in range(4)]
                ntmp = [pen(nc.sbuf_tensor(f"ntmp_{i}", [128, D], BF16))
                        for i in range(2)]
                hT = [pen(nc.sbuf_tensor(f"hT_{i}", [128, T], BF16))
                      for i in range(2)]
                hTa = [pen(nc.sbuf_tensor(f"hTa_{i}", [128, T], BF16))
                       for i in range(2)]
                lnp = pen(nc.sbuf_tensor("lnp", [128, 4 * NHC], F32))
                lnq = pen(nc.sbuf_tensor("lnq", [128, 2 * NHC], F32))
                block = pen(nc.Block())

                xlnT = pen(nc.sbuf_tensor("xlnTq", [128, NHC, T], BF16))
                xt2 = emit_ln_to_xlnT(x1b, "cols", ("pe", kb.val("pe")), xlnT)

                lnl = 0
                for i, srcp in enumerate((qlnw, qlnb, klnw, klnb)):
                    lnl = kb.emit("gpsimd", "dma_start",
                                  out=lnp[:, i * NHC:(i + 1) * NHC],
                                  in_=srcp, inc="lnl", inck=16)
                lq = kb.emit("vector", "tensor_scalar", out=lnq[:, :],
                             in0=lnp[:, 0:2 * NHC], scalar1=QK_SCALE,
                             scalar2=None, op0=OP.mult, op1=OP.bypass,
                             waits=(("lnl", lnl),), inc="ve")

                wq_total = ("wqc", kb.val("wqc"))
                segs = [("k", D, kt_in, "ccK"), ("v", 2 * D, vt_in, "ccV"),
                        ("q", 0, qt_in, "ccQ")]
                FCQ = D // 512
                qm_mark = {}
                gfc = {"i": 0}
                for sname, row0, dest, ccsem in segs:
                    cnt = 0
                    seg0 = (("nw0", kb.val("nw0")), ("nw1", kb.val("nw1")),
                            ("tp0", kb.val("tp0")), ("tp1", kb.val("tp1")),
                            ("xt", xt2))
                    for fc in range(FCQ):
                        gi_ = gfc["i"]
                        b = gi_ % 2
                        w = [wq_total, ("pe_q", qm_mark.get(gi_ - 2, 0))]
                        kb.emit("sync", "dma_start", out=wqT2[b][:, :, :],
                                in_=wqkv_c[row0 + fc * 512:row0 + fc * 512 + 512, :],
                                transpose=True, waits=tuple(w),
                                inc=f"wq{b}", inck=16)
                        wqv = (f"wq{b}", kb.val(f"wq{b}"))
                        for tt in range(NT):
                            par = cnt % 4
                            pss = PSB[par][:, 0:512]
                            evg = ("ve_q", kb.val("ve_q") - 3)
                            for hc in range(NHC):
                                w0 = (wqv, ("xt", xt2), evg) if hc == 0 else ()
                                kb.emit("tensor", "matmul", pss,
                                        xlnT[:, hc, ts(tt, 128)],
                                        wqT2[b][:, hc, :],
                                        start=(hc == 0), stop=(hc == NHC - 1),
                                        waits=w0,
                                        inc="pe_q" if hc == NHC - 1 else None)
                            pqv = kb.val("pe_q")
                            sg_g = (f"sg{par}", kb.val(f"sg{par}"))
                            ev = kb.emit("vector", "tensor_scalar",
                                         out=evq[par][:, :], in0=pss,
                                         scalar1=-8.0, scalar2=8.0,
                                         op0=OP.max, op1=OP.min,
                                         waits=(("pe_q", pqv), sg_g),
                                         inc="ve_q")
                            wv = (("ve_q", ev),)
                            if sname != "v":
                                kb.emit("vector", "reduce_sum",
                                        stats[:, tt * 8 + fc:tt * 8 + fc + 1],
                                        evq[par][:, :], axis=AX.X, inc="ve")
                                sq = kb.emit("scalar", "activation",
                                             out=ntmp[0][:, 0:512],
                                             in_=evq[par][:, :], func=AF.Square,
                                             accum_out=stats[:, 64 + tt * 8 + fc:64 + tt * 8 + fc + 1],
                                             waits=(("ve_q", ev),) +
                                             (seg0 if cnt == 0 else ()),
                                             inc="se_q")
                                wv = (("se_q", sq),)
                            kb.emit("gpsimd", "dma_start",
                                    out=segb[ts(tt, 128), ts(fc, 512)],
                                    in_=evq[par][:, :],
                                    waits=wv + (seg0 if cnt == 0 else ()),
                                    inc=f"sg{par}", inck=16)
                            cnt += 1
                        qm_mark[gi_] = kb.val("pe_q")
                        gfc["i"] += 1
                    sg_fin = tuple((f"sg{i}", kb.val(f"sg{i}"))
                                   for i in range(4))
                    if sname != "v":
                        for tt in range(NT):
                            bb = tt % 2
                            ld = kb.emit("gpsimd", "dma_start", out=xtile[:, :],
                                         in_=segb[ts(tt, 128), :],
                                         waits=sg_fin + (("se_ln", kb.val("se_ln")),),
                                         inc="xl", inck=16)
                            sm = stats[:, 184:185]
                            qm_ = stats[:, 185:186]
                            kb.emit("vector", "reduce_sum", sm,
                                    stats[:, tt * 8:tt * 8 + FCQ], axis=AX.X,
                                    waits=(("se_q", kb.val("se_q")),), inc="ve")
                            kb.emit("vector", "reduce_sum", qm_,
                                    stats[:, 64 + tt * 8:64 + tt * 8 + FCQ],
                                    axis=AX.X, inc="ve")
                            rdy, sc_col, bi_col = ln_finalize(tt, sm, qm_)
                            nw_g = (f"nw{bb}", kb.val(f"nw{bb}"))
                            nd = kb.emit("scalar", "activation",
                                         out=ntmp[bb][:, :], in_=xtile[:, :],
                                         func=AF.Identity, bias=bi_col,
                                         scale=sc_col,
                                         waits=(("ve", rdy), ("xl", ld), nw_g),
                                         inc="se_ln")
                            kb.emit("gpsimd", "dma_start",
                                    out=xlnb[ts(tt, 128), :], in_=ntmp[bb][:, :],
                                    waits=(("se_ln", nd),), inc=f"nw{bb}",
                                    inck=16)
                        nw_fin = (("nw0", kb.val("nw0")), ("nw1", kb.val("nw1")))
                        hsrc = xlnb
                    else:
                        nw_fin = sg_fin
                        hsrc = segb
                    for h in range(NHC):
                        hb = h % 2
                        if sname == "v":
                            kb.emit("gpsimd", "dma_start",
                                    out=dest[h // NH_LOC, h % NH_LOC, :, :],
                                    in_=segb[0:T, ts(h, 128)],
                                    waits=sg_fin if h == 0 else (),
                                    inc=f"sd{hb}", inck=16)
                            continue
                        tp_g = ("se_a", kb.val("se_a") - 1)
                        tp = kb.emit("sync", "dma_start", out=hT[hb][:, :],
                                     in_=hsrc[0:T, ts(h, 128)], transpose=True,
                                     waits=nw_fin + (tp_g,),
                                     inc=f"tp{hb}", inck=16)
                        tpv = (f"tp{hb}", kb.val(f"tp{hb}"))
                        if sname == "q":
                            sc_col = lnq[:, h:h + 1]
                            bi_col = lnq[:, NHC + h:NHC + h + 1]
                        else:
                            sc_col = lnp[:, 2 * NHC + h:2 * NHC + h + 1]
                            bi_col = lnp[:, 3 * NHC + h:3 * NHC + h + 1]
                        sd_g = (f"sd{hb}", kb.val(f"sd{hb}"))
                        av = kb.emit("scalar", "activation",
                                     out=hTa[hb][:, :], in_=hT[hb][:, :],
                                     func=AF.Identity, scale=sc_col,
                                     bias=bi_col,
                                     waits=(tpv, ("ve", lq), sd_g),
                                     inc="se_a")
                        kb.emit("gpsimd", "dma_start",
                                out=dest[h // NH_LOC, h % NH_LOC, :, :],
                                in_=hTa[hb][:, :], waits=(("se_a", av),),
                                inc=f"sd{hb}", inck=16)
                    sd_fin = (("sd0", kb.val("sd0")), ("sd1", kb.val("sd1")))
                    gathered = {"ccK": kt_out, "ccV": vt_out,
                                "ccQ": qt_out}[ccsem]
                    cmp_t = {"ccK": kt_cmp, "ccV": vt_cmp,
                             "ccQ": qt_cmp}[ccsem]
                    kb.emit("gpsimd", "collective_compute",
                            "AllGather", OP.bypass, replica_groups=RG,
                            ins=[dest.ap().opt()],
                            outs=[gathered.ap().opt()],
                            waits=sd_fin, inc=ccsem)
                    kb.emit("gpsimd", "dma_start", out=cmp_t,
                            in_=gathered.ap()
                            .rearrange("(s d) a b c -> s d a b c", d=4)[:, rkv],
                            waits=((ccsem, kb.val(ccsem)),),
                            inc=ccsem + "c", inck=16)
                barrier(("sd0", kb.val("sd0")), ("sd1", kb.val("sd1")),
                        ("tp0", kb.val("tp0")), ("tp1", kb.val("tp1")),
                        ("se_a", kb.val("se_a")),
                        ("pe_q", kb.val("pe_q")), ("ve_q", kb.val("ve_q")),
                        ("se_q", kb.val("se_q")))
                kb.play(block)

        def emit_attn_phase():
            with contextlib.ExitStack() as ps:
                pen = ps.enter_context
                Kh2 = [pen(nc.sbuf_tensor(f"Kh{i}", [128, S], BF16)) for i in range(2)]
                Qh2 = [pen(nc.sbuf_tensor(f"Qh{i}", [128, S], BF16)) for i in range(2)]
                Vh2 = [pen(nc.sbuf_tensor(f"Vh{i}", [128, S // 128, 128], BF16)) for i in range(2)]
                Pf2 = [pen(nc.sbuf_tensor(f"Pf{i}", [128, S], F32)) for i in range(2)]
                pfd = pen(nc.sbuf_tensor("pfd", [128, 512], F32))
                maskC = pen(nc.sbuf_tensor("maskC", [128, 4, 512], F32))
                identC = pen(nc.sbuf_tensor("identC", [128, 128], BF16))
                scor = pen(nc.sbuf_tensor("scor", [128, 2, S], F32))
                prb = pen(nc.sbuf_tensor("prb", [128, 4, S], BF16))
                prT = [pen(nc.sbuf_tensor(f"prT_{i}", [128, 4, 256], BF16))
                       for i in range(2)]
                dg = pen(nc.sbuf_tensor("dg", [128, 4, 128], BF16))
                aosb = [pen(nc.sbuf_tensor(f"aosb_{i}", [128, 256], BF16))
                        for i in range(2)]
                block = pen(nc.Block())

                mk = kb.emit("sync", "dma_start", out=maskC[:, :, :],
                             in_=masksI.rearrange("q p f -> p q f"),
                             inc="mk", inck=16)
                mk = kb.emit("sync", "dma_start", out=identC[:, :], in_=ident,
                             inc="mk", inck=16)
                mkv = ("mk", kb.val("mk"))

                sc_cnt = 0
                pv_cnt = 0
                ao_cnt = 0
                prb_mark = {}
                head_mark = {}
                for hl in range(NH_LOC):
                    hb2 = hl % 2
                    Kh = Kh2[hb2]
                    Qh = Qh2[hb2]
                    Vh = Vh2[hb2]
                    Pf = Pf2[hb2]
                    hm = head_mark.get(hl - 2, {"pe_s": 0, "pe_pv": 0, "ve_s": 0})
                    kg = ("pe_s", hm["pe_s"])
                    vg = ("pe_pv", hm["pe_pv"])
                    pg = ("ve_s", hm["ve_s"])
                    for r in range(4):
                        kb.emit("gpsimd", "dma_start", out=Kh[:, ts(r, T)],
                                in_=kt_cmp[r, hl, :, :],
                                waits=(("ccKc", kb.val("ccKc")), kg),
                                inc="kh", inck=16)
                        kb.emit("gpsimd", "dma_start", out=Qh[:, ts(r, T)],
                                in_=qt_cmp[r, hl, :, :],
                                waits=(("ccQc", kb.val("ccQc")), kg),
                                inc="kh", inck=16)
                        kb.emit("gpsimd", "dma_start",
                                out=Vh[:, r * (T // 128):(r + 1) * (T // 128), :],
                                in_=vt_cmp[r, hl, :, :]
                                .rearrange("(c p) d -> p c d", p=128),
                                waits=(("ccVc", kb.val("ccVc")), vg),
                                inc="vh", inck=16)
                    pfl = kb.emit("gpsimd", "dma_start", out=Pf[:, :],
                                  in_=alibiI[hl, :, :],
                                  waits=(pg,), inc="pfl", inck=16)
                    khv = ("kh", kb.val("kh"))
                    vhv = ("vh", kb.val("vh"))
                    pflv = ("pfl", kb.val("pfl"))
                    for qb in range(NQB):
                        span = qb + 1
                        smax = {}
                        for qt in range(4):
                            qcol = qb * 512 + qt * 128
                            dv = kb.emit("vector", "tensor_tensor", out=pfd[:, :],
                                         in0=Pf[:, ts(qb, 512)],
                                         in1=maskC[:, qt, :], op=OP.add,
                                         waits=(pflv, mkv,
                                                ("ve_s", kb.val("ve_s"))),
                                         inc="ve")
                            exp_guard = ("se_e", kb.val("se_e") - 1)
                            for kt in range(span):
                                par = sc_cnt % 2
                                pss = PSB[2][:, par * 512:par * 512 + 512]
                                evg = ("ve_s", kb.val("ve_s") - 1)
                                kb.emit("tensor", "matmul", pss,
                                        Qh[:, qcol:qcol + 128],
                                        Kh[:, ts(kt, 512)],
                                        start=True, stop=True,
                                        waits=(khv, evg), inc="pe_s")
                                psv = kb.val("pe_s")
                                bias_t = pfd[:, :] if kt == qb \
                                    else Pf[:, ts(kt, 512)]
                                bw = [("pe_s", psv)]
                                if kt == qb:
                                    bw.append(("ve", dv))
                                if kt == 0:
                                    bw.append(exp_guard)
                                kb.emit("vector", "tensor_tensor",
                                        out=scor[:, qt % 2, ts(kt, 512)],
                                        in0=pss, in1=bias_t, op=OP.add,
                                        waits=tuple(bw), inc="ve_s")
                                sc_cnt += 1
                            vsv = kb.val("ve_s")
                            kb.emit("vector", "reduce_max",
                                    stats[:, 192 + qt:193 + qt],
                                    scor[:, qt % 2, 0:span * 512], axis=AX.X,
                                    waits=(("ve_s", vsv),), inc="ve")
                            nmx = kb.emit("vector", "tensor_scalar",
                                          out=stats[:, 196 + qt:197 + qt],
                                          in0=stats[:, 192 + qt:193 + qt],
                                          scalar1=-1.0, scalar2=None,
                                          op0=OP.mult, op1=OP.bypass, inc="ve")
                            prb_g = ("pe_t", prb_mark.get(qt, 0))
                            ex = kb.emit("scalar", "activation",
                                         out=prb[:, qt, 0:span * 512],
                                         in_=scor[:, qt % 2, 0:span * 512],
                                         func=AF.Exp,
                                         bias=stats[:, 196 + qt:197 + qt],
                                         scale=1.0,
                                         accum_out=stats[:, 200 + qt:201 + qt],
                                         waits=(("ve", nmx), prb_g),
                                         inc="se_e")
                            rc = kb.emit("vector", "reciprocal",
                                         stats[:, 204 + qt:205 + qt],
                                         stats[:, 200 + qt:201 + qt],
                                         waits=(("se_e", ex),), inc="ve")
                            kb.emit("vector", "tensor_scalar",
                                    out=dg[:, qt, :], in0=identC[:, :],
                                    scalar1=stats[:, 204 + qt:205 + qt],
                                    scalar2=None, op0=OP.mult, op1=OP.bypass,
                                    waits=(("ve", rc), mkv, prb_g), inc="ve_d")
                            smax[qt] = (("se_e", kb.val("se_e")),
                                        ("ve_d", kb.val("ve_d")))
                        for prp in range(2):
                            qta, qtb = 2 * prp, 2 * prp + 1
                            exv, dgv = smax[qtb]
                            nkc = span * 4
                            ngr = span
                            pvo = PSB[1][:, (pv_cnt % 2) * 512:
                                         (pv_cnt % 2) * 512 + 256]
                            pv_evac_g = ("se_o", kb.val("se_o") - 1)

                            def tmm(gr, qta=qta, qtb=qtb, exv=exv, dgv=dgv):
                                sl = gr % 2
                                pst = (PSB[0], PSB[3])[sl]
                                ev_g = ("se_p", kb.val("se_p") - 1)
                                for jk in range(4):
                                    kc = gr * 4 + jk
                                    for j, qq in enumerate((qta, qtb)):
                                        kb.emit("tensor", "matmul",
                                                pst[:, jk * 256 + j * 128:
                                                    jk * 256 + j * 128 + 128],
                                                prb[:, qq, ts(kc, 128)],
                                                dg[:, qq, :],
                                                start=True, stop=True,
                                                waits=(dgv, exv, ev_g)
                                                if (jk == 0 and j == 0) else (),
                                                inc="pe_t"
                                                if (jk == 3 and j == 1) else None)
                                pet = kb.val("pe_t")
                                kb.emit("scalar", "copy",
                                        out=prT[sl][:, :, :], in_=pst[:, 0:1024],
                                        waits=(("pe_t", pet),
                                               ("pe_pv", kb.val("pe_pv"))),
                                        inc="se_p")
                                return ("se_p", kb.val("se_p"))

                            pmarks = {}
                            pmarks[0] = tmm(0)
                            for gr in range(ngr):
                                if gr + 1 < ngr:
                                    pmarks[gr + 1] = tmm(gr + 1)
                                for jk in range(4):
                                    kc = gr * 4 + jk
                                    w0 = [pmarks[gr], vhv] if jk == 0 else []
                                    if kc == 0:
                                        w0.append(pv_evac_g)
                                    kb.emit("tensor", "matmul", pvo,
                                            Vh[:, kc, :],
                                            prT[gr % 2][:, jk, :],
                                            start=(kc == 0),
                                            stop=(kc == nkc - 1),
                                            waits=tuple(w0), inc="pe_pv")
                            pvv = kb.val("pe_pv")
                            prb_mark[qta] = kb.val("pe_t")
                            prb_mark[qtb] = kb.val("pe_t")
                            ab = ao_cnt % 2
                            ao_g = (f"ao{ab}", kb.val(f"ao{ab}"))
                            ov = kb.emit("scalar", "copy",
                                         out=aosb[ab][:, :], in_=pvo,
                                         waits=(("pe_pv", pvv), ao_g),
                                         inc="se_o")
                            kb.emit("gpsimd", "dma_start",
                                    out=ao_in[qb // QBR, hl, :,
                                              (qb % QBR) * 512 + prp * 256:
                                              (qb % QBR) * 512 + prp * 256 + 256],
                                    in_=aosb[ab][:, :],
                                    waits=(("se_o", ov),),
                                    inc=f"ao{ab}", inck=16)
                            ao_cnt += 1
                            pv_cnt += 1
                    head_mark[hl] = {"pe_s": kb.val("pe_s"),
                                     "pe_pv": kb.val("pe_pv"),
                                     "ve_s": kb.val("ve_s")}
                kb.emit("gpsimd", "collective_compute",
                        "AllGather", OP.bypass, replica_groups=RG,
                        ins=[ao_in.ap().opt()], outs=[ao_out.ap().opt()],
                        waits=(("ao0", kb.val("ao0")), ("ao1", kb.val("ao1"))),
                        inc="ccA")
                kb.emit("gpsimd", "dma_start", out=ao_cmp,
                        in_=ao_out.ap()
                        .rearrange("(s d) a b c -> s d a b c", d=4)[:, rkv],
                        waits=(("ccA", kb.val("ccA")),), inc="ccAc", inck=16)
                barrier(("ccA", kb.val("ccA")), ("se_o", kb.val("se_o")),
                        ("pe_pv", kb.val("pe_pv")))
                kb.play(block)

        def emit_wout_phase():
            with contextlib.ExitStack() as ps:
                pen = ps.enter_context
                woT = [pen(nc.sbuf_tensor("woT_0", [128, NHC, 512], BF16))]
                aot = pen(nc.sbuf_tensor("aot", [128, NHC, T], BF16))
                block = pen(nc.Block())

                for hc in range(NHC):
                    kb.emit("gpsimd", "dma_start", out=aot[:, hc, :],
                            in_=ao_cmp[hc // NH_LOC, hc % NH_LOC, :, :],
                            waits=(("ccAc", kb.val("ccAc")),), inc="aol",
                            inck=16)
                aolv = ("aol", kb.val("aol"))
                wo_total = ("woc", kb.val("woc"))
                cnt = 0
                wm_mark = {}
                for fc in range(D // 512):
                    b = 0
                    kb.emit("sync", "dma_start", out=woT[b][:, :, :],
                            in_=wout_c[ts(fc, 512), :], transpose=True,
                            waits=(wo_total, ("pe_w", wm_mark.get(fc - 1, 0))),
                            inc=f"wo{b}", inck=16)
                    wov = (f"wo{b}", kb.val(f"wo{b}"))
                    for tt in range(NT):
                        par = cnt % 2
                        pss = PSB[par][:, 0:512]
                        evg = ("ve_x", kb.val("ve_x") - 1)
                        for hc in range(NHC):
                            w0 = (wov, aolv, evg) if hc == 0 else ()
                            kb.emit("tensor", "matmul", pss,
                                    aot[:, hc, ts(tt, 128)], woT[b][:, hc, :],
                                    start=(hc == 0), stop=(hc == NHC - 1),
                                    waits=w0,
                                    inc="pe_w" if hc == NHC - 1 else None)
                        pwv = kb.val("pe_w")
                        xb = par
                        xr = kb.emit("scalar", "dma_start",
                                     out=xres[xb][:, :],
                                     in_=x1b[ts(tt, 128), ts(fc, 512)],
                                     waits=(("ve_x", kb.val("ve_x") - 1),),
                                     inc=f"xr{xb}", inck=16)
                        xv = kb.emit("vector", "tensor_tensor",
                                     out=xo[xb][:, :], in0=pss,
                                     in1=xres[xb][:, :], op=OP.add,
                                     waits=(("pe_w", pwv), (f"xr{xb}", xr),
                                            (f"xs{xb}", kb.val(f"xs{xb}"))),
                                     inc="ve_x")
                        kb.emit("vector", "reduce_sum",
                                stats[:, tt * 8 + fc:tt * 8 + fc + 1],
                                xo[xb][:, :], axis=AX.X, inc="ve")
                        sq = kb.emit("scalar", "activation",
                                     out=xnrm[0][:, 0:512], in_=xo[xb][:, :],
                                     func=AF.Square,
                                     accum_out=stats[:, 64 + tt * 8 + fc:64 + tt * 8 + fc + 1],
                                     waits=(("ve_x", xv),), inc="se_st")
                        kb.emit("scalar", "dma_start",
                                out=x2b[ts(tt, 128), ts(fc, 512)],
                                in_=xo[xb][:, :],
                                waits=(("se_st", kb.val("se_st")),),
                                inc=f"xs{xb}", inck=16)
                        cnt += 1
                    wm_mark[fc] = kb.val("pe_w")
                barrier(("xs0", kb.val("xs0")), ("xs1", kb.val("xs1")),
                        ("pe_w", kb.val("pe_w")), ("se_st", kb.val("se_st")))
                kb.play(block)

        xs_fin1 = emit_ffn_phase(1, x0, x1b, True, x0, "self",
                                 final_store=(stop_after == "ffn1"))

        if stop_after == "ffn1":
            with nc.Block() as blk:
                for tt in range(NT):
                    kb.emit("gpsimd", "dma_start", out=xoutD[ts(tt, 128), :],
                            in_=x1b[ts(tt, 128), :],
                            waits=xs_fin1, inc="fin", inck=16)
                kb.wait("gpsimd", ("fin", kb.val("fin")))
                kb.play(blk)
            return nc

        emit_qkv_phase()
        emit_attn_phase()
        emit_wout_phase()
        xs_fin2 = emit_ffn_phase(2, x2b, xoutD, False, x2b, "cols",
                                 final_store=True)
        with nc.Block() as blk:
            kb.wait("gpsimd", *xs_fin2)
            kb.play(blk)
    return nc



def _const_inputs():
    ident = np.eye(128, dtype=np.float32).astype(BF)
    masks = np.zeros((4, 128, 512), dtype=np.float32)
    for qt in range(4):
        d = qt * 128
        p = np.arange(128)[:, None]
        j = np.arange(512)[None, :]
        masks[qt] = np.where(j > d + p, NEG, 0.0).astype(np.float32)
    return ident, masks


def alibi_slopes(n_heads, bias_max=8.0):
    next_pow2 = 2 ** int(np.ceil(np.log2(n_heads)))
    m = np.arange(1, next_pow2 + 1, dtype=np.float32) * (bias_max / next_pow2)
    slopes = 1.0 / np.power(2.0, m)
    if next_pow2 != n_heads:
        slopes = np.concatenate([slopes[1::2], slopes[::2]])[:n_heads]
    return slopes.astype(np.float32)


def _alibi_for_core(core):
    nh = D // 128
    nh_loc = nh // 4
    h0 = nh_loc * (core % 4)
    sl = alibi_slopes(nh)[h0:h0 + nh_loc]
    kk = np.arange(4 * T, dtype=np.float32)
    a = sl[:, None] * kk[None, :]
    return np.broadcast_to(a[:, None, :], (nh_loc, 128, 4 * T)).copy()


def kernel(L0_attn_hidden_states, L0_attn_Wqkv, L0_attn_q_ln_w, L0_attn_q_ln_b,
           L0_attn_k_ln_w, L0_attn_k_ln_b, L0_attn_Wout, L0_ffn_qw_gate,
           L0_ffn_qw_up, L0_ffn_qw_down, L0_ffn_s_gate, L0_ffn_z_gate,
           L0_ffn_s_up, L0_ffn_z_up, L0_ffn_s_down, L0_ffn_z_down,
           _stop_after=None, _only_maps=False):
    x = np.ascontiguousarray(np.asarray(L0_attn_hidden_states, dtype=np.float32)
                             .reshape(8 * T, D))
    ident, masks = _const_inputs()

    def col128(v):
        return np.ascontiguousarray(
            np.asarray(v, np.float32).reshape(D // 128, 128).T)

    in_maps = _prepare_in_maps_inner(x, ident, masks, col128,
        L0_attn_Wqkv, L0_attn_Wout, L0_ffn_qw_gate, L0_ffn_qw_up,
        L0_ffn_qw_down, L0_ffn_s_gate, L0_ffn_z_gate, L0_ffn_s_up,
        L0_ffn_z_up, L0_ffn_s_down, L0_ffn_z_down,
        L0_attn_q_ln_w, L0_attn_q_ln_b, L0_attn_k_ln_w, L0_attn_k_ln_b)
    if _only_maps:
        return in_maps

    nc = build_nc(stop_after=_stop_after)
    res = run_bass_kernel_spmd(nc, in_maps, core_ids=list(range(8)))
    out = np.concatenate([res.results[c]["xout"] for c in range(8)], axis=0)
    return out.reshape(2, 4 * T, D).astype(np.float32)


def _ilv(s, z):
    s = np.asarray(s, np.float32)
    z = np.asarray(z, np.float32)
    return np.ascontiguousarray(np.stack([s, z], axis=2).reshape(s.shape[0], -1))


def _prepare_in_maps_inner(x, ident, masks, col128, Wqkv, Wout, qwg, qwu, qwd,
                           sg, zg, su, zu, sd, zd, qw_, qb_, kw_, kb_):
    common = dict(
        wqkv=np.asarray(Wqkv, np.float32),
        wout=np.asarray(Wout, np.float32),
        qwg=np.asarray(qwg, np.int32),
        qwu=np.asarray(qwu, np.int32),
        qwd=np.asarray(qwd, np.int32),
        szg=_ilv(sg, zg),
        szu=_ilv(su, zu),
        szd=_ilv(sd, zd),
        qlnw=col128(qw_), qlnb=col128(qb_),
        klnw=col128(kw_), klnb=col128(kb_),
        ident=ident, masks=masks,
    )
    in_maps = []
    for c in range(8):
        m = dict(common)
        m["x0"] = np.ascontiguousarray(x[c * T:(c + 1) * T])
        m["alibi"] = _alibi_for_core(c)
        in_maps.append(m)
    return in_maps


def _prepare_in_maps(**inputs):
    return kernel(**inputs, _only_maps=True)
